# version 18
# baseline (speedup 1.0000x reference)
import os
import numpy as np
from contextlib import ExitStack
import concourse.bass as bass
import concourse.mybir as mybir
import concourse.bass_utils as _bu
from concourse.bass_utils import run_bass_kernel_spmd

if os.environ.get("KLDW", "1") == "1" and not getattr(_bu, "_ldw_patched", False):
    _orig_run_command = _bu.run_command

    def _run_command_ldw(argv, **kwargs):
        argv = [a.replace("--enable-ldw-opt=false", "--enable-ldw-opt=true")
                if isinstance(a, str) else a for a in argv]
        return _orig_run_command(argv, **kwargs)

    _bu.run_command = _run_command_ldw
    _bu._ldw_patched = True

F32 = mybir.dt.float32
BF16 = mybir.dt.bfloat16
I32 = mybir.dt.int32
AF = mybir.ActivationFunctionType
AX = mybir.AxisListType
ALU = mybir.AluOpType

B, S, D, HID = 16, 2048, 2048, 5632
H, DH = 32, 64
NCORE = 8
BPC = B // NCORE
NT = S // 128
NC = D // 128
NHC = HID // 128
MACRO = 1024
NMACRO = BPC * S // MACRO
NSUB = MACRO // 128
FG = 8
NGRP = (NHC + FG - 1) // FG


class Sem:
    def __init__(self, ctx, nc, name):
        self.h = ctx.enter_context(nc.semaphore(name))
        self.cnt = 0


class Prog:
    ENGINES = ("sync", "tensor", "vector", "scalar", "gpsimd")

    def __init__(self, ctx, nc):
        self.streams = {e: [] for e in self.ENGINES}
        self.sems = {e: Sem(ctx, nc, f"s_{e}") for e in self.ENGINES}
        self.dma_sp = Sem(ctx, nc, "dma_sp")
        self.dma_gp = Sem(ctx, nc, "dma_gp")
        self.extra_dma = []
        self.bank_readers = {b: {} for b in range(8)}
        self.bank_write = {b: None for b in range(8)}
        self.serial = {"vector": True, "scalar": True, "gpsimd": True,
                       "tensor": False, "sync": False}
        self.pending = {e: [] for e in self.ENGINES}

    def emit(self, eng, fn, waits=(), dma=False, inc=True, dsem=None):
        w = [x for x in waits if x is not None]
        if self.pending[eng]:
            w += self.pending[eng]
            self.pending[eng] = []
        if self.serial[eng] and not dma and self.sems[eng].cnt > 0:
            w.append((self.sems[eng], self.sems[eng].cnt))
        if dma and dsem is not None:
            sem = dsem
        else:
            sem = (self.dma_sp if eng == "sync" else self.dma_gp) if dma else self.sems[eng]
        step = 16 if dma else 1
        mil = None
        if inc:
            sem.cnt += step
            mil = (sem, sem.cnt)
        self.streams[eng].append((fn, w, (sem, step) if inc else None))
        return mil

    def psum_write(self, eng, fn, banks, waits=(), inc=True):
        w = list(waits)
        for b in banks:
            for s, v in self.bank_readers[b].items():
                w.append((s, v))
        mil = self.emit(eng, fn, w, inc=inc)
        for b in banks:
            if mil is not None:
                self.bank_write[b] = mil
            self.bank_readers[b] = {}
        return mil

    def psum_read(self, eng, fn, banks, waits=(), dma=False):
        w = list(waits)
        for b in banks:
            if self.bank_write[b] is not None:
                w.append(self.bank_write[b])
        mil = self.emit(eng, fn, w, dma=dma)
        for b in banks:
            self.bank_readers[b][mil[0]] = mil[1]
        return mil

    def all_milestones(self):
        out = []
        for e in self.ENGINES:
            if self.sems[e].cnt:
                out.append((self.sems[e], self.sems[e].cnt))
        if self.dma_sp.cnt:
            out.append((self.dma_sp, self.dma_sp.cnt))
        if self.dma_gp.cnt:
            out.append((self.dma_gp, self.dma_gp.cnt))
        for sem in self.extra_dma:
            if sem.cnt:
                out.append((sem, sem.cnt))
        return out

    def barrier(self):
        mils = self.all_milestones()
        for e in self.ENGINES:
            self.pending[e] = list(mils)

    def replay(self, eng_name, eng):
        for fn, waits, inc in self.streams[eng_name]:
            seen = {}
            for s, v in waits:
                if s not in seen or seen[s] < v:
                    seen[s] = v
            for s, v in seen.items():
                eng.wait_ge(s.h, v)
            inst = fn(eng)
            if inc is not None:
                sem, step = inc
                inst.then_inc(sem.h, step)


def build_nc():
    nc = bass.Bass("TRN2", target_bir_lowering=False)
    x_d = nc.dram_tensor("x", [BPC, S, D], F32, kind="ExternalInput").ap()
    wqt = nc.dram_tensor("wqt", [D, D], F32, kind="ExternalInput").ap()
    wkt = nc.dram_tensor("wkt", [D, 128], F32, kind="ExternalInput").ap()
    wvt = nc.dram_tensor("wvt", [D, 128], F32, kind="ExternalInput").ap()
    wot = nc.dram_tensor("wot", [D, D], F32, kind="ExternalInput").ap()
    wgt = nc.dram_tensor("wgt", [D, HID], I32, kind="ExternalInput").ap()
    wut = nc.dram_tensor("wut", [D, HID], I32, kind="ExternalInput").ap()
    wdt = nc.dram_tensor("wdt", [HID, D], I32, kind="ExternalInput").ap()
    ns_d = nc.dram_tensor("ns", [NC, 128], F32, kind="ExternalInput").ap()
    ss_d = nc.dram_tensor("ss", [NC, 128], F32, kind="ExternalInput").ap()
    sg_d = nc.dram_tensor("sg", [NHC, 128], F32, kind="ExternalInput").ap()
    su_d = nc.dram_tensor("su", [NHC, 128], F32, kind="ExternalInput").ap()
    sdb_d = nc.dram_tensor("sdb", [128, D], F32, kind="ExternalInput").ap()
    y_d = nc.dram_tensor("y", [BPC, S, D], F32, kind="ExternalOutput").ap()
    DBG = os.environ.get("KBG") == "1"
    KSER = int(os.environ.get("KSER", "0"))
    if DBG:
        xn_d = nc.dram_tensor("xn", [BPC, S, D], F32, kind="ExternalOutput").ap()
    else:
        xn_d = nc.dram_tensor("xn", [BPC, S, D], F32).ap()

    ctx = ExitStack()
    with ctx:
        sbt = lambda n, sh, dt: ctx.enter_context(nc.sbuf_tensor(n, sh, dt))
        pst = lambda n, sh, dt: ctx.enter_context(nc.psum_tensor(n, sh, dt))
        wq_b = sbt("wq_b", [128, NC * D], BF16)
        wo_b = sbt("wo_b", [128, NC * D], BF16)
        kv_b = sbt("kv_b", [128, 4096], BF16)
        bbf = sbt("bbf", [128, 14592], BF16)
        bf32 = sbt("bf32", [128, 8192], F32)
        sm = sbt("sm", [128, 160], F32)
        ident = sbt("ident", [128, 128], F32)
        ident_b = sbt("ident_b", [128, 128], BF16)
        ind2 = sbt("ind2", [128, 128], BF16)
        ones_r = sbt("ones_r", [65, 128], BF16)
        mvb = sbt("mvb", [128, 4], BF16)
        lntmp = sbt("lntmp", [128, 512], F32)
        vl1 = lntmp[:, 0:128]
        vl2 = lntmp[:, 128:256]
        A = pst("A", [128, 2048], F32)
        Bp = pst("Bp", [128, 2048], F32)

        wk_v = kv_b[:, 0:2048]
        wv_v = kv_b[:, 2048:4096]
        KT = bbf[:, 0:2048]
        Vv = bbf[:, 2048:4128]
        qT = bbf[:, 4352:6400]
        hbT = bbf[:, 6400:8448]
        attnT = bbf[:, 8448:10496]
        mrow = bbf[:, 8448:10496]
        q_tm = bbf[:, 8448:10496]
        Pb = [bbf[:, 10496:12544], bbf[:, 12544:14592]]
        ygT = bbf[:, 0:8192]
        wu_bufs = [bbf[:, 8192:10240], bbf[:, 10240:12288]]
        gate_sb = bbf[:, 12288:13312]
        up_sb = bbf[:, 13312:14336]
        wg_bufs = [kv_b[:, 0:2048], kv_b[:, 2048:4096]]
        d_acc = wq_b[:, :].bitcast(F32)
        X2T = wo_b[:, 0:16384]
        wd_slot = lambda i: wo_b[:, 16384 + 2048 * i:16384 + 2048 * (i + 1)]

        x_f = bf32[:, 0:2048]
        LNx = bf32[:, 2048:4096]
        hb_bf = bf32[:, 4096:5120].bitcast(BF16)
        attn_bf = bf32[:, 5120:6144].bitcast(BF16)
        resid_f = bf32[:, 6144:8192]
        X2b = bf32[:, 2048:3072].bitcast(BF16)
        sdr_bc = bf32[:, 4096:6144]
        oev = bf32[:, 6144:8192]

        stats = sm[:, 0:24]
        mv = sm[:, 24:26]
        lnv = sm[:, 26:27]
        rstd = sm[:, 27:28]
        t1v = sm[:, 28:29]
        t2v = sm[:, 29:30]
        lnr = sm[:, 30:31]
        rmsv = sm[:, 31:32]
        rmse = sm[:, 32:33]
        rv = sm[:, 33:34]
        rrv = sm[:, 34:35]
        sqcol = sm[:, 36:40]
        kn_t = sm[:, 38:39]
        kmax = sm[:, 39:40]
        mk_vec = sm[:, 40:41]
        recip = sm[:, 41:42]
        eps5 = sm[:, 42:43]
        eps18 = sm[:, 43:44]
        qm_t = sm[:, 44:45]
        qmax2 = sm[:, 45:46]
        qln = sm[:, 46:47]
        bias_g = [sm[:, 152:153], sm[:, 153:154]]
        a_tile = sm[:, 48:64]
        sg_tile = sm[:, 64:108]
        su_tile = sm[:, 108:152]

        prog = Prog(ctx, nc)
        E = prog.emit
        PW = prog.psum_write
        PR = prog.psum_read
        s_watt = Sem(ctx, nc, "d_watt")
        s_misc = Sem(ctx, nc, "d_misc")
        s_wg = [Sem(ctx, nc, "d_wg0"), Sem(ctx, nc, "d_wg1")]
        s_wu = [Sem(ctx, nc, "d_wu0"), Sem(ctx, nc, "d_wu1")]
        s_wd = [Sem(ctx, nc, f"d_wd{i}") for i in range(FG)]
        prog.extra_dma += [s_watt, s_misc] + s_wg + s_wu + s_wd

        TR = [(c % 2) * 512 + (c // 2) * 128 for c in range(8)]
        bTR = lambda i: 4 + ((TR[i]) // 512)
        STM = [0, 512]
        QP = [1024, 1536]
        KP = 1152
        VP = 1664
        AVS = [1280, 1792]

        st = {"x_free": None, "P_free": [None, None], "ST_free": [None, None],
              "y_prev": None, "wg_free": [None, None], "wu_free": [None, None],
              "wd_free": [None] * FG, "oev_free": None, "milo": None,
              "x2b_free": None, "resid_free": None, "lnx_free": None}

        E("gpsimd", lambda g: g.memset(ident[:, :], 0.0))
        milI = E("gpsimd", lambda g: g.affine_select(
            out=ident[:, :], in_=ident[:, :], compare_op=ALU.not_equal,
            fill=1.0, base=0, pattern=[[-1, 128]], channel_multiplier=1))
        E("gpsimd", lambda g: g.memset(ident_b[:, :], 0.0))
        milIb = E("gpsimd", lambda g: g.affine_select(
            out=ident_b[:, :], in_=ident_b[:, :], compare_op=ALU.not_equal,
            fill=1.0, base=0, pattern=[[-1, 128]], channel_multiplier=1))
        m_onesr = E("vector", lambda v: v.memset(ones_r[:, :], 1.0))
        E("vector", lambda v: v.memset(ind2[:, :], 0.0))
        E("vector", lambda v: v.memset(ind2[0:64, 0:1], 1.0))
        m_ind2 = E("vector", lambda v: v.memset(ind2[64:128, 64:65], 1.0))
        E("vector", lambda v: v.memset(eps5[:, :], 1e-5))
        m_eps = E("vector", lambda v: v.memset(eps18[:, :], 1e-18))

        E("gpsimd", lambda g: g.dma_start(
            out=wq_b[:, :].rearrange("p (c n) -> p c n", c=NC),
            in_=wqt.rearrange("(c p) n -> p c n", p=128)), dma=True, dsem=s_watt)
        E("gpsimd", lambda g: g.dma_start(
            out=wo_b[:, :].rearrange("p (c n) -> p c n", c=NC),
            in_=wot.rearrange("(c p) n -> p c n", p=128)), dma=True, dsem=s_watt)
        E("gpsimd", lambda g: g.dma_start(
            out=wk_v.rearrange("p (c n) -> p c n", c=NC),
            in_=wkt.rearrange("(c p) n -> p c n", p=128)), dma=True, dsem=s_watt)
        E("gpsimd", lambda g: g.dma_start(
            out=wv_v.rearrange("p (c n) -> p c n", c=NC),
            in_=wvt.rearrange("(c p) n -> p c n", p=128)), dma=True, dsem=s_watt)
        mil_wq = mil_wo = mil_wk = mil_wv = (s_watt, 64)
        E("sync", lambda s: s.dma_start(out=vl1[0:NC, :], in_=ns_d), dma=True,
          dsem=s_misc)
        E("sync", lambda s: s.dma_start(out=vl2[0:NC, :], in_=ss_d), dma=True,
          dsem=s_misc)
        mil_v1 = mil_v2 = (s_misc, 32)
        mil_sdr = None

        mil_a16 = E("vector", lambda v: v.tensor_tensor(
            out=vl1[0:NC, :], in0=vl1[0:NC, :], in1=vl2[0:NC, :], op=ALU.mult),
            waits=[mil_v1, mil_v2])
        mil_aT = PW("tensor", lambda t: t.transpose(
            Bp[:, 0:16], vl1[0:NC, :], ident[0:NC, 0:NC]),
            banks=[4], waits=[mil_a16, milI])
        PR("vector", lambda v: v.tensor_copy(a_tile, Bp[:, 0:16]),
           banks=[4], waits=[mil_aT])
        E("sync", lambda s: s.dma_start(out=vl1[0:NHC, :], in_=sg_d),
          waits=[mil_a16], dma=True, dsem=s_misc)
        E("sync", lambda s: s.dma_start(out=vl2[0:NHC, :], in_=su_d),
          waits=[mil_a16], dma=True, dsem=s_misc)
        mil_g1 = mil_g2 = (s_misc, 64)
        mil_gT = PW("tensor", lambda t: t.transpose(
            Bp[:, 16:60], vl1[0:NHC, :], ident[0:NHC, 0:NHC]),
            banks=[4], waits=[mil_g1])
        mil_gev = PR("vector", lambda v: v.tensor_copy(sg_tile, Bp[:, 16:60]),
                     banks=[4], waits=[mil_gT])
        mil_uT = PW("tensor", lambda t: t.transpose(
            Bp[:, 512:556], vl2[0:NHC, :], ident[0:NHC, 0:NHC]),
            banks=[5], waits=[mil_g2])
        mil_uev = PR("vector", lambda v: v.tensor_copy(su_tile, Bp[:, 512:556]),
                     banks=[5], waits=[mil_uT])

        def ln_chain(mil_x, want_lnx, want_hbase, hb_waits=()):
            mil_st = None
            for i in range(4):
                mil_st = E("vector", lambda v, i=i: v.bn_stats(
                    stats[:, 6 * i:6 * i + 6], x_f[:, 512 * i:512 * i + 512]),
                    waits=[mil_x])
            mil_mv = E("vector", lambda v: v.bn_aggr(mv, stats), waits=[mil_st])
            mil_lnv = E("scalar", lambda s: s.activation(
                lnv, mv[:, 1:2], AF.Ln, bias=eps5), waits=[mil_mv, m_eps])
            mil_rstd = E("scalar", lambda s: s.activation(
                rstd, lnv, AF.Exp, scale=-0.5), waits=[mil_lnv])
            out = {"rstd": mil_rstd, "lnx": None, "hb": None}
            if want_lnx:
                out["lnx"] = E("vector", lambda v: v.tensor_scalar(
                    out=LNx, in0=x_f, scalar1=mv[:, 0:1], scalar2=rstd,
                    op0=ALU.subtract, op1=ALU.mult), waits=[mil_rstd])
            if want_hbase:
                m1 = E("vector", lambda v: v.tensor_tensor(
                    out=t1v, in0=mv[:, 1:2], in1=rstd, op=ALU.mult),
                    waits=[mil_rstd])
                m2 = E("vector", lambda v: v.tensor_tensor(
                    out=t2v, in0=t1v, in1=rstd, op=ALU.mult), waits=[m1])
                m3 = E("scalar", lambda s: s.activation(
                    lnr, t2v, AF.Ln, bias=eps18), waits=[m2])
                m4 = E("scalar", lambda s: s.activation(
                    rmsv, lnr, AF.Exp, scale=0.5), waits=[m3])
                m5 = E("vector", lambda v: v.tensor_scalar(
                    out=rmse, in0=rmsv, scalar1=1e-8, scalar2=None,
                    op0=ALU.add), waits=[m4])
                m6 = E("vector", lambda v: v.reciprocal(rv, rmse), waits=[m5])
                m7 = E("vector", lambda v: v.tensor_tensor(
                    out=rrv, in0=rstd, in1=rv, op=ALU.mult), waits=[m6])
                out["hb"] = E("vector", lambda v: v.tensor_scalar(
                    out=hb_bf, in0=x_f, scalar1=mv[:, 0:1], scalar2=rrv,
                    op0=ALU.subtract, op1=ALU.mult),
                    waits=[m7] + list(hb_waits))
            return out

        def transpose_16(src, dst_bf, scale_fn, extra_wait, dst_off_fn=None):
            evacs = []
            for c in range(NC):
                slot = TR[c % 8]
                bank = bTR(c % 8)
                milT = PW("tensor", lambda t, c=c, slot=slot, src=src: t.transpose(
                    Bp[:, slot:slot + 128].bitcast(BF16)[:, 0:128],
                    src[:, 128 * c:128 * c + 128],
                    ident_b[:, :]), banks=[bank], waits=[extra_wait, milIb])
                off = dst_off_fn(c) if dst_off_fn else 128 * c
                sc = scale_fn(c) if scale_fn else None
                if c % 2 == 0:
                    mile = PR("scalar",
                              lambda s, slot=slot, off=off, sc=sc, dst=dst_bf:
                              s.activation(dst[:, off:off + 128],
                                           Bp[:, slot:slot + 128]
                                           .bitcast(BF16)[:, 0:128],
                                           AF.Identity,
                                           scale=(1.0 if sc is None else sc)),
                              banks=[bank], waits=[milT])
                elif sc is None:
                    mile = PR("vector",
                              lambda v, slot=slot, off=off, dst=dst_bf:
                              v.tensor_copy(dst[:, off:off + 128],
                                            Bp[:, slot:slot + 128]
                                            .bitcast(BF16)[:, 0:128]),
                              banks=[bank], waits=[milT])
                else:
                    mile = PR("vector",
                              lambda v, slot=slot, off=off, sc=sc, dst=dst_bf:
                              v.tensor_scalar(out=dst[:, off:off + 128],
                                              in0=Bp[:, slot:slot + 128]
                                              .bitcast(BF16)[:, 0:128],
                                              scalar1=sc, scalar2=None,
                                              op0=ALU.mult),
                              banks=[bank], waits=[milT])
                evacs.append(mile)
            return evacs

        def phase1(s_idx):
            mil_vm = E("vector", lambda v: v.memset(Vv, 1.0))
            E("vector", lambda v: v.memset(kmax, 0.0))
            mil_x = E("sync", lambda s: s.dma_start(
                out=x_f, in_=x_d[s_idx, 0:128, :]),
                waits=[st["x_free"]], dma=True)
            ln = ln_chain(mil_x, want_lnx=False, want_hbase=True)
            for j in range(NT):
                st["x_free"] = ln["hb"]
                ev = transpose_16(hb_bf, hbT, lambda c: a_tile[:, c:c + 1], ln["hb"])
                if j + 1 < NT:
                    mil_x = E("sync", lambda s, j=j: s.dma_start(
                        out=x_f, in_=x_d[s_idx, 128 * (j + 1):128 * (j + 2), :]),
                        waits=[st["x_free"]], dma=True)
                    ln = ln_chain(mil_x, want_lnx=False, want_hbase=True,
                                  hb_waits=[ev[-1]])
                milk = None
                for c in range(NC):
                    w = [ev[c]] + ([mil_wk] if c == 0 else [])
                    milk = PW("tensor", lambda t, c=c: t.matmul(
                        Bp[:, KP:KP + 128], wk_v[:, 128 * c:128 * c + 128],
                        hbT[:, 128 * c:128 * c + 128],
                        start=(c == 0), stop=(c == NC - 1)), banks=[6], waits=w,
                        inc=(c == NC - 1))
                milke = PR("scalar", lambda s, j=j: s.activation(
                    KT[:, 128 * j:128 * j + 128], Bp[:, KP:KP + 128],
                    AF.Identity, scale=1.0), banks=[6], waits=[milk])
                milsq = E("vector", lambda v, j=j: v.tensor_tensor(
                    out=lntmp[:, 0:128].bitcast(BF16)[:, 0:128],
                    in0=KT[:, 128 * j:128 * j + 128],
                    in1=KT[:, 128 * j:128 * j + 128], op=ALU.mult),
                    waits=[milke])
                milknm = PW("tensor", lambda t: t.matmul(
                    Bp[:, VP:VP + 128], ind2[:, :],
                    lntmp[:, 0:128].bitcast(BF16)[:, 0:128],
                    start=True, stop=True), banks=[7], waits=[milsq, m_ind2])
                milknr = PR("vector", lambda v: v.reduce_max(
                    kn_t, Bp[:, VP:VP + 128], axis=AX.X),
                    banks=[7], waits=[milknm])
                E("vector", lambda v: v.tensor_tensor(
                    out=kmax, in0=kn_t, in1=kmax, op=ALU.max), waits=[milknr])
                milv = None
                for c in range(NC):
                    w = [ev[c]] + ([mil_wv] if c == 0 else [])
                    milv = PW("tensor", lambda t, c=c: t.matmul(
                        Bp[:, VP:VP + 128], hbT[:, 128 * c:128 * c + 128],
                        wv_v[:, 128 * c:128 * c + 128],
                        start=(c == 0), stop=(c == NC - 1)), banks=[7], waits=w,
                        inc=(c == NC - 1))
                PR("scalar", lambda s, j=j: s.activation(
                    Vv[:, 130 * j:130 * j + 64], Bp[:, VP:VP + 64],
                    AF.Identity, scale=1.0), banks=[7], waits=[milv, mil_vm])
                PR("scalar", lambda s, j=j: s.activation(
                    Vv[:, 130 * j + 65:130 * j + 129], Bp[:, VP + 64:VP + 128],
                    AF.Identity, scale=1.0), banks=[7], waits=[milv, mil_vm])
            E("vector", lambda v: v.tensor_scalar(
                out=mk_vec, in0=kmax, scalar1=1.0 / 64.0,
                scalar2=None, op0=ALU.mult))

        def emit_av(hs):
            h, m, STb, bank0, mil_exp = hs
            milav = None
            for c in range(NC):
                w = []
                if c == 0:
                    w = [mil_exp[0]]
                if c == 8:
                    w = [mil_exp[1]]
                milav = PW("tensor", lambda t, c=c, h=h, m=m, P=STb:
                           t.matmul(P[:, 0:65],
                                    Pb[m][:, 128 * c:128 * c + 128],
                                    Vv[:, 130 * c + 65 * m:130 * c + 65 * m + 65],
                                    start=(c == 0), stop=(c == NC - 1)),
                           banks=[bank0], waits=w, inc=(c == NC - 1))
            st["P_free"][m] = milav
            milr = PR("vector", lambda v, P=STb: v.reciprocal(
                recip, P[:, 64:65]), banks=[bank0], waits=[milav])
            st["milo"] = PR("vector", lambda v, h=h, P=STb: v.tensor_scalar(
                out=attn_bf[:, 64 * h:64 * h + 64], in0=P[:, 0:64],
                scalar1=recip, scalar2=None, op0=ALU.mult),
                banks=[bank0], waits=[milr])

        def phase3(s_idx):
            mil_x = E("sync", lambda s: s.dma_start(
                out=x_f, in_=x_d[s_idx, 0:128, :]),
                waits=[st["x_free"]], dma=True)
            ln = ln_chain(mil_x, want_lnx=True, want_hbase=True)
            for j in range(NT):
                milt1 = E("vector", lambda v: v.tensor_tensor(
                    out=resid_f, in0=x_f, in1=LNx, op=ALU.add),
                    waits=[ln["lnx"], st["resid_free"]])
                st["x_free"] = milt1
                ev = transpose_16(hb_bf, hbT, lambda c: a_tile[:, c:c + 1], ln["hb"])
                milqq = None
                for c in range(NC):
                    for b4 in range(4):
                        w = [ev[c]] + ([mil_wq] if (c == 0 and b4 == 0) else [])
                        milqq = PW("tensor", lambda t, c=c, b4=b4: t.matmul(
                            A[:, 512 * b4:512 * b4 + 512],
                            hbT[:, 128 * c:128 * c + 128],
                            wq_b[:, 2048 * c + 512 * b4:2048 * c + 512 * b4 + 512],
                            start=(c == 0), stop=(c == NC - 1)),
                            banks=[b4], waits=w,
                            inc=(c == NC - 1 and b4 in (1, 3)))
                        if c == NC - 1 and b4 == 1:
                            milq01 = milqq
                milqe = [None, None]
                milqe[0] = PR("vector", lambda v: v.tensor_copy(
                    q_tm[:, 0:1024], A[:, 0:1024]),
                    banks=[0, 1], waits=[milq01])
                milqe[1] = PR("vector", lambda v: v.tensor_copy(
                    q_tm[:, 1024:2048], A[:, 1024:2048]),
                    banks=[2, 3], waits=[milqq])
                mil_q = [None] * NC
                for oc in range(NC):
                    slot = QP[oc % 2]
                    bank = 6 + (oc % 2)
                    milqt = PW("tensor", lambda t, oc=oc, slot=slot: t.transpose(
                        Bp[:, slot:slot + 128].bitcast(BF16)[:, 0:128],
                        q_tm[:, 128 * oc:128 * oc + 128], ident_b[:, :]),
                        banks=[bank], waits=[milqe[oc // 8], milIb])
                    if oc % 2 == 0:
                        mil_q[oc] = PR("scalar",
                                       lambda s, oc=oc, slot=slot: s.activation(
                                           qT[:, 128 * oc:128 * oc + 128],
                                           Bp[:, slot:slot + 128]
                                           .bitcast(BF16)[:, 0:128],
                                           AF.Identity, scale=1.0),
                                       banks=[bank], waits=[milqt])
                    else:
                        mil_q[oc] = PR("vector",
                                       lambda v, oc=oc, slot=slot: v.tensor_copy(
                                           qT[:, 128 * oc:128 * oc + 128],
                                           Bp[:, slot:slot + 128]
                                           .bitcast(BF16)[:, 0:128]),
                                       banks=[bank], waits=[milqt])
                milsq = E("vector", lambda v: v.tensor_tensor(
                    out=hbT, in0=qT, in1=qT, op=ALU.mult),
                    waits=[mil_q[NC - 1], mil_q[NC - 2]])
                milqm = None
                for b4 in range(4):
                    slot = STM[b4 % 2]
                    bank = 4 + (b4 % 2)
                    milm = PW("tensor", lambda t, b4=b4, slot=slot: t.matmul(
                        Bp[:, slot:slot + 512], ind2[:, :],
                        hbT[:, 512 * b4:512 * b4 + 512],
                        start=True, stop=True), banks=[bank],
                        waits=[milsq, m_ind2])
                    milrq = PR("vector", lambda v, slot=slot: v.reduce_max(
                        qm_t, Bp[:, slot:slot + 512], axis=AX.X),
                        banks=[bank], waits=[milm])
                    if b4 == 0:
                        milqm = E("vector", lambda v: v.tensor_copy(
                            qmax2, qm_t), waits=[milrq])
                    else:
                        milqm = E("vector", lambda v: v.tensor_tensor(
                            out=qmax2, in0=qm_t, in1=qmax2, op=ALU.max),
                            waits=[milrq])
                mlnq = E("scalar", lambda s: s.activation(
                    qln, qmax2, AF.Ln, bias=eps18, scale=mk_vec),
                    waits=[milqm])
                mexq = E("scalar", lambda s: s.activation(
                    mvb[:, 0:1], qln, AF.Exp, scale=0.5), waits=[mlnq])
                mil_bias = [None, None]
                for g2 in range(2):
                    slot = STM[g2]
                    bank = 4 + g2
                    mb = PW("tensor", lambda t, g2=g2, slot=slot: t.matmul(
                        Bp[:, slot:slot + 1],
                        ones_r[64 * g2:64 * g2 + 1, 0:128],
                        mvb[64 * g2:64 * g2 + 1, 0:1],
                        start=True, stop=True), banks=[bank],
                        waits=[mexq, m_onesr])
                    mil_bias[g2] = PR("scalar", lambda s, g2=g2, slot=slot:
                                      s.activation(bias_g[g2],
                                                   Bp[:, slot:slot + 1],
                                                   AF.Identity, scale=-0.125),
                                      banks=[bank], waits=[mb])
                if j + 1 < NT:
                    mil_x = E("sync", lambda s, j=j: s.dma_start(
                        out=x_f, in_=x_d[s_idx, 128 * (j + 1):128 * (j + 2), :]),
                        waits=[st["x_free"]], dma=True)
                    ln = ln_chain(mil_x, want_lnx=True, want_hbase=True,
                                  hb_waits=[ev[-1]])
                milo = None
                mil_exp_prev = None
                head_state = []
                for i in range(H):
                    m = i % 2
                    pb = i // 2
                    h = 16 * m + pb
                    qsl = qT[64 * m:64 * m + 64, 128 * pb:128 * pb + 128]
                    STb = A if m == 0 else Bp
                    bank0 = 0 if m == 0 else 4
                    mil_half = [None, None]
                    for c in list(range(4, NC)) + [0, 1, 2, 3]:
                        bank = bank0 + c // 4
                        w = [mil_q[pb]] if c % 4 == 0 else []
                        mq = PW("tensor", lambda t, c=c, qsl=qsl, m=m, P=STb:
                                t.matmul(P[:, 128 * c:128 * c + 128],
                                         KT[64 * m:64 * m + 64,
                                            128 * c:128 * c + 128], qsl,
                                         start=(c % 4 == 0), stop=(c % 4 == 3)),
                                banks=[bank], waits=w, inc=(c in (15, 3)))
                        if c == 15:
                            mil_half[1] = mq
                        elif c == 3:
                            mil_half[0] = mq
                    mil_exp = [None, None]
                    for e in (1, 0):
                        mil_exp[e] = PR("scalar", lambda s, e=e, m=m, P=STb:
                                        s.activation(
                                            Pb[m][:, 1024 * e:1024 * e + 1024],
                                            P[:, 1024 * e:1024 * e + 1024],
                                            AF.Exp, scale=0.125,
                                            bias=bias_g[m]),
                                        banks=[bank0 + 2 * e, bank0 + 2 * e + 1],
                                        waits=[mil_half[e], mil_bias[m],
                                               st["P_free"][m]])
                    head_state.append((h, m, STb, bank0, mil_exp))
                    if i >= 1:
                        emit_av(head_state[i - 1])
                emit_av(head_state[H - 1])
                milo = st["milo"]
                evA = transpose_16(attn_bf, attnT, None, milo)
                milwo = None
                for c in range(NC):
                    for b4 in range(4):
                        w = [evA[c]] + ([mil_wo] if (c == 0 and b4 == 0) else [])
                        milwo = PW("tensor", lambda t, c=c, b4=b4: t.matmul(
                            A[:, 512 * b4:512 * b4 + 512],
                            attnT[:, 128 * c:128 * c + 128],
                            wo_b[:, 2048 * c + 512 * b4:2048 * c + 512 * b4 + 512],
                            start=(c == 0), stop=(c == NC - 1)),
                            banks=[b4], waits=w,
                            inc=(c == NC - 1 and b4 == 3))
                milxn = PR("vector", lambda v: v.tensor_tensor(
                    out=resid_f, in0=A[:, :], in1=resid_f, op=ALU.add),
                    banks=[0, 1, 2, 3], waits=[milwo, milt1])
                milst = E("sync", lambda s, j=j: s.dma_start(
                    out=xn_d[s_idx, 128 * j:128 * j + 128, :], in_=resid_f),
                    waits=[milxn], dma=True)
                st["resid_free"] = milst

        def phaseF(mi):
            nonlocal mil_sdr
            s_idx = mi // 2
            tb = 1024 * (mi % 2)
            if mi == 0:
                E("sync", lambda s: s.dma_start(out=sdr_bc, in_=sdb_d),
                  dma=True, dsem=s_misc)
                mil_sdr = (s_misc, 80)
            for jj in range(NSUB):
                mil_x = E("sync", lambda s, jj=jj: s.dma_start(
                    out=x_f,
                    in_=xn_d[s_idx, tb + 128 * jj:tb + 128 * jj + 128, :]),
                    waits=[st["x_free"]], dma=True)
                ln = ln_chain(mil_x, want_lnx=False, want_hbase=False)
                milx2 = E("vector", lambda v: v.tensor_scalar(
                    out=X2b, in0=x_f, scalar1=mv[:, 0:1], scalar2=rstd,
                    op0=ALU.subtract, op1=ALU.mult), waits=[ln["rstd"]])
                st["x_free"] = milx2
                evs = transpose_16(X2b, X2T, None, milx2,
                                   dst_off_fn=lambda c, jj=jj: 1024 * c + 128 * jj)
                st["x2b_free"] = evs[-1]

            for g in range(NGRP):
                g0 = FG * g
                gn = min(FG, NHC - g0)
                last = (g == NGRP - 1)
                ymils = [None] * gn
                wdm = [None] * gn
                pf_mil = [None] * NSUB
                xnb = lambda jj: x_f if jj % 2 == 0 else LNx
                if last:
                    pf_mil[0] = E("sync", lambda s: s.dma_start(
                        out=x_f, in_=xn_d[s_idx, tb:tb + 128, :]),
                        waits=[st["x_free"]], dma=True)
                    pf_mil[1] = E("sync", lambda s: s.dma_start(
                        out=LNx, in_=xn_d[s_idx, tb + 128:tb + 256, :]),
                        waits=[st["x2b_free"], st["lnx_free"]], dma=True)
                for k in range(gn):
                    hc = g0 + k
                    par = hc % 2
                    mil_wg = E("gpsimd", lambda gg, hc=hc, par=par: gg.dma_start(
                        out=wg_bufs[par][:, :].rearrange("p (c n) -> p c n", c=NC),
                        in_=wgt.rearrange("(c p) n -> p c n", p=128)
                        [:, :, 128 * hc:128 * hc + 128]),
                        waits=[st["wg_free"][par]], dma=True, dsem=s_wg[par])
                    mil_wu = E("gpsimd", lambda gg, hc=hc, par=par: gg.dma_start(
                        out=wu_bufs[par][:, :].rearrange("p (c n) -> p c n", c=NC),
                        in_=wut.rearrange("(c p) n -> p c n", p=128)
                        [:, :, 128 * hc:128 * hc + 128]),
                        waits=[st["wu_free"][par]], dma=True, dsem=s_wu[par])
                    wdm[k] = E("gpsimd", lambda gg, hc=hc, k=k: gg.dma_start(
                        out=wd_slot(k), in_=wdt[128 * hc:128 * hc + 128, :]),
                        waits=[st["wd_free"][k]], dma=True, dsem=s_wd[k])
                    sub_g = E("vector", lambda v, par=par: v.tensor_scalar(
                        out=wg_bufs[par][:, :], in0=wg_bufs[par][:, :],
                        scalar1=-128.0, scalar2=None, op0=ALU.add),
                        waits=[mil_wg])
                    sub_u = E("vector", lambda v, par=par: v.tensor_scalar(
                        out=wu_bufs[par][:, :], in0=wu_bufs[par][:, :],
                        scalar1=-128.0, scalar2=None, op0=ALU.add),
                        waits=[mil_wu])
                    sub_d = E("vector", lambda v, k=k: v.tensor_scalar(
                        out=wd_slot(k), in0=wd_slot(k),
                        scalar1=-128.0, scalar2=None, op0=ALU.add),
                        waits=[wdm[k]])
                    wdm[k] = sub_d
                    mm = None
                    for c in range(NC):
                        for b2 in range(2):
                            w = [sub_g, st["y_prev"]] if (c == 0 and b2 == 0) else []
                            mm = PW("tensor", lambda t, c=c, b2=b2, par=par: t.matmul(
                                A[:, 512 * b2:512 * b2 + 512],
                                wg_bufs[par][:, 128 * c:128 * c + 128],
                                X2T[:, 1024 * c + 512 * b2:
                                    1024 * c + 512 * b2 + 512],
                                start=(c == 0), stop=(c == NC - 1)),
                                banks=[b2], waits=w,
                                inc=(c == NC - 1 and b2 == 1))
                    st["wg_free"][par] = mm
                    milg = PR("scalar", lambda s, hc=hc: s.activation(
                        gate_sb, A[:, 0:1024], AF.Silu,
                        scale=sg_tile[:, hc:hc + 1]), banks=[0, 1],
                        waits=[mm, mil_gev, st["y_prev"]])
                    mm = None
                    for c in range(NC):
                        for b2 in range(2):
                            w = [sub_u] if (c == 0 and b2 == 0) else []
                            mm = PW("tensor", lambda t, c=c, b2=b2, par=par: t.matmul(
                                A[:, 1024 + 512 * b2:1024 + 512 * b2 + 512],
                                wu_bufs[par][:, 128 * c:128 * c + 128],
                                X2T[:, 1024 * c + 512 * b2:
                                    1024 * c + 512 * b2 + 512],
                                start=(c == 0), stop=(c == NC - 1)),
                                banks=[2 + b2], waits=w,
                                inc=(c == NC - 1 and b2 == 1))
                    st["wu_free"][par] = mm
                    milu = PR("scalar", lambda s, hc=hc: s.activation(
                        up_sb, A[:, 1024:2048], AF.Identity,
                        scale=su_tile[:, hc:hc + 1]), banks=[2, 3],
                        waits=[mm, mil_uev, st["y_prev"]])
                    ymils[k] = E("vector", lambda v, k=k: v.tensor_tensor(
                        out=ygT[:, 1024 * k:1024 * k + 1024], in0=gate_sb,
                        in1=up_sb, op=ALU.mult), waits=[milg, milu])
                    st["y_prev"] = ymils[k]
                    if KSER >= 2:
                        prog.barrier()
                for jj in range(NSUB):
                    Pp = A if jj % 2 == 0 else Bp
                    pbanks = [0, 1, 2, 3] if jj % 2 == 0 else [4, 5, 6, 7]
                    mm = None
                    for k in range(gn):
                        for b4 in range(4):
                            w = [wdm[k], ymils[k]] if b4 == 0 else []
                            mm = PW("tensor", lambda t, k=k, b4=b4, jj=jj, P=Pp:
                                    t.matmul(P[:, 512 * b4:512 * b4 + 512],
                                             ygT[:, 1024 * k + 128 * jj:
                                                 1024 * k + 128 * jj + 128],
                                             wd_slot(k)[:, 512 * b4:512 * b4 + 512],
                                             start=(k == 0),
                                             stop=(k == gn - 1)),
                                    banks=[pbanks[b4]], waits=w,
                                    inc=(k == gn - 1 and b4 == 3))
                    if g == 0 and not last:
                        PR("vector", lambda v, jj=jj, P=Pp: v.tensor_copy(
                            d_acc[:, 2048 * jj:2048 * jj + 2048], P[:, :]),
                            banks=pbanks, waits=[mm])
                    elif not last:
                        PR("vector", lambda v, jj=jj, P=Pp: v.tensor_tensor(
                            out=d_acc[:, 2048 * jj:2048 * jj + 2048], in0=P[:, :],
                            in1=d_acc[:, 2048 * jj:2048 * jj + 2048], op=ALU.add),
                            banks=pbanks, waits=[mm])
                    else:
                        m1 = PR("vector", lambda v, jj=jj, P=Pp: v.tensor_tensor(
                            out=oev, in0=P[:, :],
                            in1=d_acc[:, 2048 * jj:2048 * jj + 2048],
                            op=ALU.add), banks=pbanks,
                            waits=[mm, st["oev_free"]])
                        m2 = E("vector", lambda v, jj=jj: v.tensor_tensor(
                            out=d_acc[:, 2048 * jj:2048 * jj + 2048],
                            in0=oev, in1=sdr_bc, op=ALU.mult),
                            waits=[m1, mil_sdr])
                        m3 = E("vector", lambda v, jj=jj: v.tensor_tensor(
                            out=oev, in0=d_acc[:, 2048 * jj:2048 * jj + 2048],
                            in1=xnb(jj), op=ALU.add),
                            waits=[m2, pf_mil[jj]])
                        if jj % 2 == 0:
                            st["x_free"] = m3
                        else:
                            st["lnx_free"] = m3
                        if jj + 2 < NSUB:
                            pf_mil[jj + 2] = E("sync", lambda s, jj=jj: s.dma_start(
                                out=xnb(jj), in_=xn_d[s_idx, tb + 128 * (jj + 2):
                                                      tb + 128 * (jj + 3), :]),
                                waits=[m3], dma=True)
                        st["oev_free"] = E("sync", lambda s, jj=jj: s.dma_start(
                            out=y_d[s_idx, tb + 128 * jj:tb + 128 * jj + 128, :],
                            in_=oev), waits=[m3], dma=True)
                for k in range(gn):
                    st["wd_free"][k] = mm
                if KSER >= 1:
                    prog.barrier()

        phase1(0)
        prog.barrier()
        phase3(0)
        prog.barrier()
        phase1(1)
        prog.barrier()
        phase3(1)
        prog.barrier()
        for mi in range(NMACRO):
            phaseF(mi)
        prog.barrier()

        fin = prog.all_milestones()
        E("sync", lambda s: s.nop(), waits=fin, inc=False)

        with nc.Block() as block:
            @block.sync
            def _(e):
                prog.replay("sync", e)

            @block.tensor
            def _(e):
                prog.replay("tensor", e)

            @block.vector
            def _(e):
                prog.replay("vector", e)

            @block.scalar
            def _(e):
                prog.replay("scalar", e)

            @block.gpsimd
            def _(e):
                prog.replay("gpsimd", e)

    return nc


def kernel(**inp):
    x = np.ascontiguousarray(inp["L0_attn_x"], dtype=np.float32)
    perm = []
    for oc in range(16):
        perm += list(range(64 * oc, 64 * oc + 64))
        perm += list(range(64 * (oc + 16), 64 * (oc + 16) + 64))
    wqt = np.ascontiguousarray(np.asarray(inp["L0_attn_Wq"])[perm, :].T)
    wkt = np.ascontiguousarray(np.asarray(inp["L0_attn_Wk"]).T)
    wvt = np.ascontiguousarray(np.asarray(inp["L0_attn_Wv"]).T)
    wot = np.ascontiguousarray(np.asarray(inp["L0_attn_Wo"]).T)
    wgt = np.ascontiguousarray(np.asarray(inp["L0_ffn_wq_gate"]).T)
    wut = np.ascontiguousarray(np.asarray(inp["L0_ffn_wq_up"]).T)
    wdt = np.ascontiguousarray(np.asarray(inp["L0_ffn_wq_down"]).T)
    ns = np.ascontiguousarray(np.asarray(inp["L0_attn_norm_scale"]).reshape(NC, 128))
    ss = np.ascontiguousarray(np.asarray(inp["L0_attn_split_scale"]).reshape(NC, 128))
    sg = np.ascontiguousarray(np.asarray(inp["L0_ffn_s_gate"]).reshape(NHC, 128))
    su = np.ascontiguousarray(np.asarray(inp["L0_ffn_s_up"]).reshape(NHC, 128))
    sdb = np.ascontiguousarray(np.broadcast_to(
        np.asarray(inp["L0_ffn_s_down"]).reshape(1, D), (128, D)))

    nc = build_nc()
    in_maps = []
    for c in range(NCORE):
        in_maps.append({
            "x": np.ascontiguousarray(x[BPC * c:BPC * (c + 1)]),
            "wqt": wqt, "wkt": wkt, "wvt": wvt, "wot": wot,
            "wgt": wgt, "wut": wut, "wdt": wdt,
            "ns": ns, "ss": ss, "sg": sg, "su": su, "sdb": sdb,
        })
    res = run_bass_kernel_spmd(nc, in_maps, core_ids=list(range(NCORE)))
    return np.concatenate([np.asarray(r["y"]) for r in res.results], axis=0)


# version 19
# speedup vs baseline: 1.0634x; 1.0634x over previous
import os
import numpy as np
from contextlib import ExitStack
import concourse.bass as bass
import concourse.mybir as mybir
import concourse.bass_utils as _bu
from concourse.bass_utils import run_bass_kernel_spmd

if os.environ.get("KLDW", "1") == "1" and not getattr(_bu, "_ldw_patched", False):
    _orig_run_command = _bu.run_command

    def _run_command_ldw(argv, **kwargs):
        argv = [a.replace("--enable-ldw-opt=false", "--enable-ldw-opt=true")
                if isinstance(a, str) else a for a in argv]
        return _orig_run_command(argv, **kwargs)

    _bu.run_command = _run_command_ldw
    _bu._ldw_patched = True

F32 = mybir.dt.float32
BF16 = mybir.dt.bfloat16
I32 = mybir.dt.int32
AF = mybir.ActivationFunctionType
AX = mybir.AxisListType
ALU = mybir.AluOpType

B, S, D, HID = 16, 2048, 2048, 5632
H, DH = 32, 64
NCORE = 8
BPC = B // NCORE
NT = S // 128
NC = D // 128
NHC = HID // 128
MACRO = 1024
NMACRO = BPC * S // MACRO
NSUB = MACRO // 128
FG = 8
NGRP = (NHC + FG - 1) // FG


class Sem:
    def __init__(self, ctx, nc, name):
        self.h = ctx.enter_context(nc.semaphore(name))
        self.cnt = 0


class Prog:
    ENGINES = ("sync", "tensor", "vector", "scalar", "gpsimd")

    def __init__(self, ctx, nc):
        self.streams = {e: [] for e in self.ENGINES}
        self.sems = {e: Sem(ctx, nc, f"s_{e}") for e in self.ENGINES}
        self.dma_sp = Sem(ctx, nc, "dma_sp")
        self.dma_gp = Sem(ctx, nc, "dma_gp")
        self.extra_dma = []
        self.bank_readers = {b: {} for b in range(8)}
        self.bank_write = {b: None for b in range(8)}
        self.serial = {"vector": False, "scalar": False, "gpsimd": False,
                       "tensor": False, "sync": False}
        self.pending = {e: [] for e in self.ENGINES}

    def emit(self, eng, fn, waits=(), dma=False, inc=True, dsem=None):
        w = [x for x in waits if x is not None]
        if self.pending[eng]:
            w += self.pending[eng]
            self.pending[eng] = []
        if self.serial[eng] and not dma and self.sems[eng].cnt > 0:
            w.append((self.sems[eng], self.sems[eng].cnt))
        if dma and dsem is not None:
            sem = dsem
        else:
            sem = (self.dma_sp if eng == "sync" else self.dma_gp) if dma else self.sems[eng]
        step = 16 if dma else 1
        mil = None
        if inc:
            sem.cnt += step
            mil = (sem, sem.cnt)
        self.streams[eng].append((fn, w, (sem, step) if inc else None))
        return mil

    def psum_write(self, eng, fn, banks, waits=(), inc=True):
        w = list(waits)
        for b in banks:
            for s, v in self.bank_readers[b].items():
                w.append((s, v))
        mil = self.emit(eng, fn, w, inc=inc)
        for b in banks:
            if mil is not None:
                self.bank_write[b] = mil
            self.bank_readers[b] = {}
        return mil

    def psum_read(self, eng, fn, banks, waits=(), dma=False):
        w = list(waits)
        for b in banks:
            if self.bank_write[b] is not None:
                w.append(self.bank_write[b])
        mil = self.emit(eng, fn, w, dma=dma)
        for b in banks:
            self.bank_readers[b][mil[0]] = mil[1]
        return mil

    def all_milestones(self):
        out = []
        for e in self.ENGINES:
            if self.sems[e].cnt:
                out.append((self.sems[e], self.sems[e].cnt))
        if self.dma_sp.cnt:
            out.append((self.dma_sp, self.dma_sp.cnt))
        if self.dma_gp.cnt:
            out.append((self.dma_gp, self.dma_gp.cnt))
        for sem in self.extra_dma:
            if sem.cnt:
                out.append((sem, sem.cnt))
        return out

    def barrier(self):
        mils = self.all_milestones()
        for e in self.ENGINES:
            self.pending[e] = list(mils)

    def replay(self, eng_name, eng):
        maxseen = {}
        for fn, waits, inc in self.streams[eng_name]:
            seen = {}
            for s, v in waits:
                if maxseen.get(s, -1) >= v:
                    continue
                if s not in seen or seen[s] < v:
                    seen[s] = v
            for s, v in seen.items():
                eng.wait_ge(s.h, v)
                maxseen[s] = v
            inst = fn(eng)
            if inc is not None:
                sem, step = inc
                inst.then_inc(sem.h, step)


def build_nc():
    nc = bass.Bass("TRN2", target_bir_lowering=False)
    x_d = nc.dram_tensor("x", [BPC, S, D], F32, kind="ExternalInput").ap()
    wqt = nc.dram_tensor("wqt", [D, D], F32, kind="ExternalInput").ap()
    wkt = nc.dram_tensor("wkt", [D, 128], F32, kind="ExternalInput").ap()
    wvt = nc.dram_tensor("wvt", [D, 128], F32, kind="ExternalInput").ap()
    wot = nc.dram_tensor("wot", [D, D], F32, kind="ExternalInput").ap()
    wgt = nc.dram_tensor("wgt", [D, HID], I32, kind="ExternalInput").ap()
    wut = nc.dram_tensor("wut", [D, HID], I32, kind="ExternalInput").ap()
    wdt = nc.dram_tensor("wdt", [HID, D], I32, kind="ExternalInput").ap()
    ns_d = nc.dram_tensor("ns", [NC, 128], F32, kind="ExternalInput").ap()
    ss_d = nc.dram_tensor("ss", [NC, 128], F32, kind="ExternalInput").ap()
    sg_d = nc.dram_tensor("sg", [NHC, 128], F32, kind="ExternalInput").ap()
    su_d = nc.dram_tensor("su", [NHC, 128], F32, kind="ExternalInput").ap()
    sdb_d = nc.dram_tensor("sdb", [128, D], F32, kind="ExternalInput").ap()
    y_d = nc.dram_tensor("y", [BPC, S, D], F32, kind="ExternalOutput").ap()
    DBG = os.environ.get("KBG") == "1"
    KSER = int(os.environ.get("KSER", "0"))
    if DBG:
        xn_d = nc.dram_tensor("xn", [BPC, S, D], F32, kind="ExternalOutput").ap()
    else:
        xn_d = nc.dram_tensor("xn", [BPC, S, D], F32).ap()

    ctx = ExitStack()
    with ctx:
        sbt = lambda n, sh, dt: ctx.enter_context(nc.sbuf_tensor(n, sh, dt))
        pst = lambda n, sh, dt: ctx.enter_context(nc.psum_tensor(n, sh, dt))
        wq_b = sbt("wq_b", [128, NC * D], BF16)
        wo_b = sbt("wo_b", [128, NC * D], BF16)
        kv_b = sbt("kv_b", [128, 4096], BF16)
        bbf = sbt("bbf", [128, 14592], BF16)
        bf32 = sbt("bf32", [128, 8192], F32)
        sm = sbt("sm", [128, 160], F32)
        ident = sbt("ident", [128, 128], F32)
        ident_b = sbt("ident_b", [128, 128], BF16)
        ind2 = sbt("ind2", [128, 128], BF16)
        ones_r = sbt("ones_r", [65, 128], BF16)
        mvb = sbt("mvb", [128, 4], BF16)
        lntmp = sbt("lntmp", [128, 512], F32)
        vl1 = lntmp[:, 0:128]
        vl2 = lntmp[:, 128:256]
        A = pst("A", [128, 2048], F32)
        Bp = pst("Bp", [128, 2048], F32)

        wk_v = kv_b[:, 0:2048]
        wv_v = kv_b[:, 2048:4096]
        KT = bbf[:, 0:2048]
        Vv = bbf[:, 2048:4128]
        qT = bbf[:, 4352:6400]
        hbT = bbf[:, 6400:8448]
        attnT = bbf[:, 8448:10496]
        mrow = bbf[:, 8448:10496]
        q_tm = bbf[:, 8448:10496]
        Pb = [bbf[:, 10496:12544], bbf[:, 12544:14592]]
        ygT = bbf[:, 0:8192]
        wu_bufs = [bbf[:, 8192:10240], bbf[:, 10240:12288]]
        gate_sb = bbf[:, 12288:13312]
        up_sb = bbf[:, 13312:14336]
        wg_bufs = [kv_b[:, 0:2048], kv_b[:, 2048:4096]]
        d_acc = wq_b[:, :].bitcast(F32)
        X2T = wo_b[:, 0:16384]
        wd_slot = lambda i: wo_b[:, 16384 + 2048 * i:16384 + 2048 * (i + 1)]

        x_f = bf32[:, 0:2048]
        LNx = bf32[:, 2048:4096]
        hb_bf = bf32[:, 4096:5120].bitcast(BF16)
        attn_bf = bf32[:, 5120:6144].bitcast(BF16)
        resid_f = bf32[:, 6144:8192]
        X2b = bf32[:, 2048:3072].bitcast(BF16)
        sdr_bc = bf32[:, 4096:6144]
        oev = bf32[:, 6144:8192]

        stats = sm[:, 0:24]
        mv = sm[:, 24:26]
        lnv = sm[:, 26:27]
        rstd = sm[:, 27:28]
        t1v = sm[:, 28:29]
        t2v = sm[:, 29:30]
        lnr = sm[:, 30:31]
        rmsv = sm[:, 31:32]
        rmse = sm[:, 32:33]
        rv = sm[:, 33:34]
        rrv = sm[:, 34:35]
        sqcol = sm[:, 36:40]
        kn_t = sm[:, 38:39]
        kmax = sm[:, 39:40]
        mk_vec = sm[:, 40:41]
        recip = sm[:, 41:42]
        eps5 = sm[:, 42:43]
        eps18 = sm[:, 43:44]
        qm_t = sm[:, 44:45]
        qmax2 = sm[:, 45:46]
        qln = sm[:, 46:47]
        bias_g = [sm[:, 152:153], sm[:, 153:154]]
        a_tile = sm[:, 48:64]
        sg_tile = sm[:, 64:108]
        su_tile = sm[:, 108:152]

        prog = Prog(ctx, nc)
        E = prog.emit
        PW = prog.psum_write
        PR = prog.psum_read
        s_watt = Sem(ctx, nc, "d_watt")
        s_misc = Sem(ctx, nc, "d_misc")
        s_wg = [Sem(ctx, nc, "d_wg0"), Sem(ctx, nc, "d_wg1")]
        s_wu = [Sem(ctx, nc, "d_wu0"), Sem(ctx, nc, "d_wu1")]
        s_wd = [Sem(ctx, nc, f"d_wd{i}") for i in range(FG)]
        prog.extra_dma += [s_watt, s_misc] + s_wg + s_wu + s_wd

        TR = [(c % 2) * 512 + (c // 2) * 128 for c in range(8)]
        bTR = lambda i: 4 + ((TR[i]) // 512)
        STM = [0, 512]
        QP = [1024, 1536]
        KP = 1152
        VP = 1664
        AVS = [1280, 1792]

        st = {"x_free": None, "P_free": [None, None], "ST_free": [None, None],
              "y_prev": None, "wg_free": [None, None], "wu_free": [None, None],
              "wd_free": [None] * FG, "oev_free": None, "milo": None,
              "x2b_free": None, "resid_free": None, "lnx_free": None}

        E("gpsimd", lambda g: g.memset(ident[:, :], 0.0))
        milI = E("gpsimd", lambda g: g.affine_select(
            out=ident[:, :], in_=ident[:, :], compare_op=ALU.not_equal,
            fill=1.0, base=0, pattern=[[-1, 128]], channel_multiplier=1))
        E("gpsimd", lambda g: g.memset(ident_b[:, :], 0.0))
        milIb = E("gpsimd", lambda g: g.affine_select(
            out=ident_b[:, :], in_=ident_b[:, :], compare_op=ALU.not_equal,
            fill=1.0, base=0, pattern=[[-1, 128]], channel_multiplier=1))
        m_onesr = E("vector", lambda v: v.memset(ones_r[:, :], 1.0))
        E("vector", lambda v: v.memset(ind2[:, :], 0.0))
        E("vector", lambda v: v.memset(ind2[0:64, 0:1], 1.0))
        m_ind2 = E("vector", lambda v: v.memset(ind2[64:128, 64:65], 1.0))
        E("vector", lambda v: v.memset(eps5[:, :], 1e-5))
        m_eps = E("vector", lambda v: v.memset(eps18[:, :], 1e-18))

        E("gpsimd", lambda g: g.dma_start(
            out=wq_b[:, :].rearrange("p (c n) -> p c n", c=NC),
            in_=wqt.rearrange("(c p) n -> p c n", p=128)), dma=True, dsem=s_watt)
        E("gpsimd", lambda g: g.dma_start(
            out=wo_b[:, :].rearrange("p (c n) -> p c n", c=NC),
            in_=wot.rearrange("(c p) n -> p c n", p=128)), dma=True, dsem=s_watt)
        E("gpsimd", lambda g: g.dma_start(
            out=wk_v.rearrange("p (c n) -> p c n", c=NC),
            in_=wkt.rearrange("(c p) n -> p c n", p=128)), dma=True, dsem=s_watt)
        E("gpsimd", lambda g: g.dma_start(
            out=wv_v.rearrange("p (c n) -> p c n", c=NC),
            in_=wvt.rearrange("(c p) n -> p c n", p=128)), dma=True, dsem=s_watt)
        mil_wq = mil_wo = mil_wk = mil_wv = (s_watt, 64)
        E("sync", lambda s: s.dma_start(out=vl1[0:NC, :], in_=ns_d), dma=True,
          dsem=s_misc)
        E("sync", lambda s: s.dma_start(out=vl2[0:NC, :], in_=ss_d), dma=True,
          dsem=s_misc)
        mil_v1 = mil_v2 = (s_misc, 32)
        mil_sdr = None

        mil_a16 = E("vector", lambda v: v.tensor_tensor(
            out=vl1[0:NC, :], in0=vl1[0:NC, :], in1=vl2[0:NC, :], op=ALU.mult),
            waits=[mil_v1, mil_v2])
        mil_aT = PW("tensor", lambda t: t.transpose(
            Bp[:, 0:16], vl1[0:NC, :], ident[0:NC, 0:NC]),
            banks=[4], waits=[mil_a16, milI])
        PR("vector", lambda v: v.tensor_copy(a_tile, Bp[:, 0:16]),
           banks=[4], waits=[mil_aT])
        E("sync", lambda s: s.dma_start(out=vl1[0:NHC, :], in_=sg_d),
          waits=[mil_a16], dma=True, dsem=s_misc)
        E("sync", lambda s: s.dma_start(out=vl2[0:NHC, :], in_=su_d),
          waits=[mil_a16], dma=True, dsem=s_misc)
        mil_g1 = mil_g2 = (s_misc, 64)
        mil_gT = PW("tensor", lambda t: t.transpose(
            Bp[:, 16:60], vl1[0:NHC, :], ident[0:NHC, 0:NHC]),
            banks=[4], waits=[mil_g1])
        mil_gev = PR("vector", lambda v: v.tensor_copy(sg_tile, Bp[:, 16:60]),
                     banks=[4], waits=[mil_gT])
        mil_uT = PW("tensor", lambda t: t.transpose(
            Bp[:, 512:556], vl2[0:NHC, :], ident[0:NHC, 0:NHC]),
            banks=[5], waits=[mil_g2])
        mil_uev = PR("vector", lambda v: v.tensor_copy(su_tile, Bp[:, 512:556]),
                     banks=[5], waits=[mil_uT])

        def ln_chain(mil_x, want_lnx, want_hbase, hb_waits=()):
            mil_st = None
            for i in range(4):
                mil_st = E("vector", lambda v, i=i: v.bn_stats(
                    stats[:, 6 * i:6 * i + 6], x_f[:, 512 * i:512 * i + 512]),
                    waits=[mil_x])
            mil_mv = E("vector", lambda v: v.bn_aggr(mv, stats), waits=[mil_st])
            mil_lnv = E("scalar", lambda s: s.activation(
                lnv, mv[:, 1:2], AF.Ln, bias=eps5), waits=[mil_mv, m_eps])
            mil_rstd = E("scalar", lambda s: s.activation(
                rstd, lnv, AF.Exp, scale=-0.5), waits=[mil_lnv])
            out = {"rstd": mil_rstd, "lnx": None, "hb": None}
            if want_lnx:
                out["lnx"] = E("vector", lambda v: v.tensor_scalar(
                    out=LNx, in0=x_f, scalar1=mv[:, 0:1], scalar2=rstd,
                    op0=ALU.subtract, op1=ALU.mult), waits=[mil_rstd])
            if want_hbase:
                m1 = E("vector", lambda v: v.tensor_tensor(
                    out=t1v, in0=mv[:, 1:2], in1=rstd, op=ALU.mult),
                    waits=[mil_rstd])
                m2 = E("vector", lambda v: v.tensor_tensor(
                    out=t2v, in0=t1v, in1=rstd, op=ALU.mult), waits=[m1])
                m3 = E("scalar", lambda s: s.activation(
                    lnr, t2v, AF.Ln, bias=eps18), waits=[m2])
                m4 = E("scalar", lambda s: s.activation(
                    rmsv, lnr, AF.Exp, scale=0.5), waits=[m3])
                m5 = E("vector", lambda v: v.tensor_scalar(
                    out=rmse, in0=rmsv, scalar1=1e-8, scalar2=None,
                    op0=ALU.add), waits=[m4])
                m6 = E("vector", lambda v: v.reciprocal(rv, rmse), waits=[m5])
                m7 = E("vector", lambda v: v.tensor_tensor(
                    out=rrv, in0=rstd, in1=rv, op=ALU.mult), waits=[m6])
                out["hb"] = E("vector", lambda v: v.tensor_scalar(
                    out=hb_bf, in0=x_f, scalar1=mv[:, 0:1], scalar2=rrv,
                    op0=ALU.subtract, op1=ALU.mult),
                    waits=[m7] + list(hb_waits))
            return out

        def transpose_16(src, dst_bf, scale_fn, extra_wait, dst_off_fn=None):
            evacs = []
            for c in range(NC):
                slot = TR[c % 8]
                bank = bTR(c % 8)
                milT = PW("tensor", lambda t, c=c, slot=slot, src=src: t.transpose(
                    Bp[:, slot:slot + 128].bitcast(BF16)[:, 0:128],
                    src[:, 128 * c:128 * c + 128],
                    ident_b[:, :]), banks=[bank], waits=[extra_wait, milIb])
                off = dst_off_fn(c) if dst_off_fn else 128 * c
                sc = scale_fn(c) if scale_fn else None
                if c % 2 == 0:
                    mile = PR("scalar",
                              lambda s, slot=slot, off=off, sc=sc, dst=dst_bf:
                              s.activation(dst[:, off:off + 128],
                                           Bp[:, slot:slot + 128]
                                           .bitcast(BF16)[:, 0:128],
                                           AF.Identity,
                                           scale=(1.0 if sc is None else sc)),
                              banks=[bank], waits=[milT])
                elif sc is None:
                    mile = PR("vector",
                              lambda v, slot=slot, off=off, dst=dst_bf:
                              v.tensor_copy(dst[:, off:off + 128],
                                            Bp[:, slot:slot + 128]
                                            .bitcast(BF16)[:, 0:128]),
                              banks=[bank], waits=[milT])
                else:
                    mile = PR("vector",
                              lambda v, slot=slot, off=off, sc=sc, dst=dst_bf:
                              v.tensor_scalar(out=dst[:, off:off + 128],
                                              in0=Bp[:, slot:slot + 128]
                                              .bitcast(BF16)[:, 0:128],
                                              scalar1=sc, scalar2=None,
                                              op0=ALU.mult),
                              banks=[bank], waits=[milT])
                evacs.append(mile)
            return evacs

        def phase1(s_idx):
            mil_vm = E("vector", lambda v: v.memset(Vv, 1.0))
            E("vector", lambda v: v.memset(kmax, 0.0))
            mil_x = E("sync", lambda s: s.dma_start(
                out=x_f, in_=x_d[s_idx, 0:128, :]),
                waits=[st["x_free"]], dma=True)
            ln = ln_chain(mil_x, want_lnx=False, want_hbase=True)
            for j in range(NT):
                st["x_free"] = ln["hb"]
                ev = transpose_16(hb_bf, hbT, lambda c: a_tile[:, c:c + 1], ln["hb"])
                if j + 1 < NT:
                    mil_x = E("sync", lambda s, j=j: s.dma_start(
                        out=x_f, in_=x_d[s_idx, 128 * (j + 1):128 * (j + 2), :]),
                        waits=[st["x_free"]], dma=True)
                    ln = ln_chain(mil_x, want_lnx=False, want_hbase=True,
                                  hb_waits=[ev[-1]])
                milk = None
                for c in range(NC):
                    w = [ev[c]] + ([mil_wk] if c == 0 else [])
                    milk = PW("tensor", lambda t, c=c: t.matmul(
                        Bp[:, KP:KP + 128], wk_v[:, 128 * c:128 * c + 128],
                        hbT[:, 128 * c:128 * c + 128],
                        start=(c == 0), stop=(c == NC - 1)), banks=[6], waits=w,
                        inc=(c == NC - 1))
                milke = PR("scalar", lambda s, j=j: s.activation(
                    KT[:, 128 * j:128 * j + 128], Bp[:, KP:KP + 128],
                    AF.Identity, scale=1.0), banks=[6], waits=[milk])
                milsq = E("vector", lambda v, j=j: v.tensor_tensor(
                    out=lntmp[:, 0:128].bitcast(BF16)[:, 0:128],
                    in0=KT[:, 128 * j:128 * j + 128],
                    in1=KT[:, 128 * j:128 * j + 128], op=ALU.mult),
                    waits=[milke])
                milknm = PW("tensor", lambda t: t.matmul(
                    Bp[:, VP:VP + 128], ind2[:, :],
                    lntmp[:, 0:128].bitcast(BF16)[:, 0:128],
                    start=True, stop=True), banks=[7], waits=[milsq, m_ind2])
                milknr = PR("vector", lambda v: v.reduce_max(
                    kn_t, Bp[:, VP:VP + 128], axis=AX.X),
                    banks=[7], waits=[milknm])
                E("vector", lambda v: v.tensor_tensor(
                    out=kmax, in0=kn_t, in1=kmax, op=ALU.max), waits=[milknr])
                milv = None
                for c in range(NC):
                    w = [ev[c]] + ([mil_wv] if c == 0 else [])
                    milv = PW("tensor", lambda t, c=c: t.matmul(
                        Bp[:, VP:VP + 128], hbT[:, 128 * c:128 * c + 128],
                        wv_v[:, 128 * c:128 * c + 128],
                        start=(c == 0), stop=(c == NC - 1)), banks=[7], waits=w,
                        inc=(c == NC - 1))
                PR("scalar", lambda s, j=j: s.activation(
                    Vv[:, 130 * j:130 * j + 64], Bp[:, VP:VP + 64],
                    AF.Identity, scale=1.0), banks=[7], waits=[milv, mil_vm])
                PR("scalar", lambda s, j=j: s.activation(
                    Vv[:, 130 * j + 65:130 * j + 129], Bp[:, VP + 64:VP + 128],
                    AF.Identity, scale=1.0), banks=[7], waits=[milv, mil_vm])
            E("vector", lambda v: v.tensor_scalar(
                out=mk_vec, in0=kmax, scalar1=1.0 / 64.0,
                scalar2=None, op0=ALU.mult))

        def emit_av(hs):
            h, m, STb, bank0, mil_exp = hs
            milav = None
            for c in range(NC):
                w = []
                if c == 0:
                    w = [mil_exp[0]]
                if c == 8:
                    w = [mil_exp[1]]
                milav = PW("tensor", lambda t, c=c, h=h, m=m, P=STb:
                           t.matmul(P[:, 0:65],
                                    Pb[m][:, 128 * c:128 * c + 128],
                                    Vv[:, 130 * c + 65 * m:130 * c + 65 * m + 65],
                                    start=(c == 0), stop=(c == NC - 1)),
                           banks=[bank0], waits=w, inc=(c == NC - 1))
            st["P_free"][m] = milav
            milr = PR("vector", lambda v, P=STb: v.reciprocal(
                recip, P[:, 64:65]), banks=[bank0], waits=[milav])
            st["milo"] = PR("vector", lambda v, h=h, P=STb: v.tensor_scalar(
                out=attn_bf[:, 64 * h:64 * h + 64], in0=P[:, 0:64],
                scalar1=recip, scalar2=None, op0=ALU.mult),
                banks=[bank0], waits=[milr])

        def phase3(s_idx):
            mil_x = E("sync", lambda s: s.dma_start(
                out=x_f, in_=x_d[s_idx, 0:128, :]),
                waits=[st["x_free"]], dma=True)
            ln = ln_chain(mil_x, want_lnx=True, want_hbase=True)
            for j in range(NT):
                milt1 = E("vector", lambda v: v.tensor_tensor(
                    out=resid_f, in0=x_f, in1=LNx, op=ALU.add),
                    waits=[ln["lnx"], st["resid_free"]])
                st["x_free"] = milt1
                ev = transpose_16(hb_bf, hbT, lambda c: a_tile[:, c:c + 1], ln["hb"])
                milqq = None
                for c in range(NC):
                    for b4 in range(4):
                        w = [ev[c]] + ([mil_wq] if (c == 0 and b4 == 0) else [])
                        milqq = PW("tensor", lambda t, c=c, b4=b4: t.matmul(
                            A[:, 512 * b4:512 * b4 + 512],
                            hbT[:, 128 * c:128 * c + 128],
                            wq_b[:, 2048 * c + 512 * b4:2048 * c + 512 * b4 + 512],
                            start=(c == 0), stop=(c == NC - 1)),
                            banks=[b4], waits=w,
                            inc=(c == NC - 1 and b4 in (1, 3)))
                        if c == NC - 1 and b4 == 1:
                            milq01 = milqq
                milqe = [None, None]
                milqe[0] = PR("vector", lambda v: v.tensor_copy(
                    q_tm[:, 0:1024], A[:, 0:1024]),
                    banks=[0, 1], waits=[milq01])
                milqe[1] = PR("vector", lambda v: v.tensor_copy(
                    q_tm[:, 1024:2048], A[:, 1024:2048]),
                    banks=[2, 3], waits=[milqq])
                mil_q = [None] * NC
                for oc in range(NC):
                    slot = QP[oc % 2]
                    bank = 6 + (oc % 2)
                    milqt = PW("tensor", lambda t, oc=oc, slot=slot: t.transpose(
                        Bp[:, slot:slot + 128].bitcast(BF16)[:, 0:128],
                        q_tm[:, 128 * oc:128 * oc + 128], ident_b[:, :]),
                        banks=[bank], waits=[milqe[oc // 8], milIb])
                    if oc % 2 == 0:
                        mil_q[oc] = PR("scalar",
                                       lambda s, oc=oc, slot=slot: s.activation(
                                           qT[:, 128 * oc:128 * oc + 128],
                                           Bp[:, slot:slot + 128]
                                           .bitcast(BF16)[:, 0:128],
                                           AF.Identity, scale=1.0),
                                       banks=[bank], waits=[milqt])
                    else:
                        mil_q[oc] = PR("vector",
                                       lambda v, oc=oc, slot=slot: v.tensor_copy(
                                           qT[:, 128 * oc:128 * oc + 128],
                                           Bp[:, slot:slot + 128]
                                           .bitcast(BF16)[:, 0:128]),
                                       banks=[bank], waits=[milqt])
                milsq = E("vector", lambda v: v.tensor_tensor(
                    out=hbT, in0=qT, in1=qT, op=ALU.mult),
                    waits=[mil_q[NC - 1], mil_q[NC - 2]])
                milqm = None
                for b4 in range(4):
                    slot = STM[b4 % 2]
                    bank = 4 + (b4 % 2)
                    milm = PW("tensor", lambda t, b4=b4, slot=slot: t.matmul(
                        Bp[:, slot:slot + 512], ind2[:, :],
                        hbT[:, 512 * b4:512 * b4 + 512],
                        start=True, stop=True), banks=[bank],
                        waits=[milsq, m_ind2])
                    milrq = PR("vector", lambda v, slot=slot: v.reduce_max(
                        qm_t, Bp[:, slot:slot + 512], axis=AX.X),
                        banks=[bank], waits=[milm])
                    if b4 == 0:
                        milqm = E("vector", lambda v: v.tensor_copy(
                            qmax2, qm_t), waits=[milrq])
                    else:
                        milqm = E("vector", lambda v: v.tensor_tensor(
                            out=qmax2, in0=qm_t, in1=qmax2, op=ALU.max),
                            waits=[milrq])
                mlnq = E("scalar", lambda s: s.activation(
                    qln, qmax2, AF.Ln, bias=eps18, scale=mk_vec),
                    waits=[milqm])
                mexq = E("scalar", lambda s: s.activation(
                    mvb[:, 0:1], qln, AF.Exp, scale=0.5), waits=[mlnq])
                mil_bias = [None, None]
                for g2 in range(2):
                    slot = STM[g2]
                    bank = 4 + g2
                    mb = PW("tensor", lambda t, g2=g2, slot=slot: t.matmul(
                        Bp[:, slot:slot + 1],
                        ones_r[64 * g2:64 * g2 + 1, 0:128],
                        mvb[64 * g2:64 * g2 + 1, 0:1],
                        start=True, stop=True), banks=[bank],
                        waits=[mexq, m_onesr])
                    mil_bias[g2] = PR("scalar", lambda s, g2=g2, slot=slot:
                                      s.activation(bias_g[g2],
                                                   Bp[:, slot:slot + 1],
                                                   AF.Identity, scale=-0.125),
                                      banks=[bank], waits=[mb])
                if j + 1 < NT:
                    mil_x = E("sync", lambda s, j=j: s.dma_start(
                        out=x_f, in_=x_d[s_idx, 128 * (j + 1):128 * (j + 2), :]),
                        waits=[st["x_free"]], dma=True)
                    ln = ln_chain(mil_x, want_lnx=True, want_hbase=True,
                                  hb_waits=[ev[-1]])
                milo = None
                mil_exp_prev = None
                head_state = []
                for i in range(H):
                    m = i % 2
                    pb = i // 2
                    h = 16 * m + pb
                    qsl = qT[64 * m:64 * m + 64, 128 * pb:128 * pb + 128]
                    STb = A if m == 0 else Bp
                    bank0 = 0 if m == 0 else 4
                    mil_half = [None, None]
                    for c in list(range(4, NC)) + [0, 1, 2, 3]:
                        bank = bank0 + c // 4
                        w = [mil_q[pb]] if c % 4 == 0 else []
                        mq = PW("tensor", lambda t, c=c, qsl=qsl, m=m, P=STb:
                                t.matmul(P[:, 128 * c:128 * c + 128],
                                         KT[64 * m:64 * m + 64,
                                            128 * c:128 * c + 128], qsl,
                                         start=(c % 4 == 0), stop=(c % 4 == 3)),
                                banks=[bank], waits=w, inc=(c in (15, 3)))
                        if c == 15:
                            mil_half[1] = mq
                        elif c == 3:
                            mil_half[0] = mq
                    mil_exp = [None, None]
                    for e in (1, 0):
                        mil_exp[e] = PR("scalar", lambda s, e=e, m=m, P=STb:
                                        s.activation(
                                            Pb[m][:, 1024 * e:1024 * e + 1024],
                                            P[:, 1024 * e:1024 * e + 1024],
                                            AF.Exp, scale=0.125,
                                            bias=bias_g[m]),
                                        banks=[bank0 + 2 * e, bank0 + 2 * e + 1],
                                        waits=[mil_half[e], mil_bias[m],
                                               st["P_free"][m]])
                    head_state.append((h, m, STb, bank0, mil_exp))
                    if i >= 1:
                        emit_av(head_state[i - 1])
                emit_av(head_state[H - 1])
                milo = st["milo"]
                evA = transpose_16(attn_bf, attnT, None, milo)
                milwo = None
                for c in range(NC):
                    for b4 in range(4):
                        w = [evA[c]] + ([mil_wo] if (c == 0 and b4 == 0) else [])
                        milwo = PW("tensor", lambda t, c=c, b4=b4: t.matmul(
                            A[:, 512 * b4:512 * b4 + 512],
                            attnT[:, 128 * c:128 * c + 128],
                            wo_b[:, 2048 * c + 512 * b4:2048 * c + 512 * b4 + 512],
                            start=(c == 0), stop=(c == NC - 1)),
                            banks=[b4], waits=w,
                            inc=(c == NC - 1 and b4 == 3))
                milxn = PR("vector", lambda v: v.tensor_tensor(
                    out=resid_f, in0=A[:, :], in1=resid_f, op=ALU.add),
                    banks=[0, 1, 2, 3], waits=[milwo, milt1])
                milst = E("sync", lambda s, j=j: s.dma_start(
                    out=xn_d[s_idx, 128 * j:128 * j + 128, :], in_=resid_f),
                    waits=[milxn], dma=True)
                st["resid_free"] = milst

        def phaseF(mi):
            nonlocal mil_sdr
            s_idx = mi // 2
            tb = 1024 * (mi % 2)
            if mi == 0:
                E("sync", lambda s: s.dma_start(out=sdr_bc, in_=sdb_d),
                  dma=True, dsem=s_misc)
                mil_sdr = (s_misc, 80)
            for jj in range(NSUB):
                mil_x = E("sync", lambda s, jj=jj: s.dma_start(
                    out=x_f,
                    in_=xn_d[s_idx, tb + 128 * jj:tb + 128 * jj + 128, :]),
                    waits=[st["x_free"]], dma=True)
                ln = ln_chain(mil_x, want_lnx=False, want_hbase=False)
                milx2 = E("vector", lambda v: v.tensor_scalar(
                    out=X2b, in0=x_f, scalar1=mv[:, 0:1], scalar2=rstd,
                    op0=ALU.subtract, op1=ALU.mult), waits=[ln["rstd"]])
                st["x_free"] = milx2
                evs = transpose_16(X2b, X2T, None, milx2,
                                   dst_off_fn=lambda c, jj=jj: 1024 * c + 128 * jj)
                st["x2b_free"] = evs[-1]

            for g in range(NGRP):
                g0 = FG * g
                gn = min(FG, NHC - g0)
                last = (g == NGRP - 1)
                ymils = [None] * gn
                wdm = [None] * gn
                pf_mil = [None] * NSUB
                xnb = lambda jj: x_f if jj % 2 == 0 else LNx
                if last:
                    pf_mil[0] = E("sync", lambda s: s.dma_start(
                        out=x_f, in_=xn_d[s_idx, tb:tb + 128, :]),
                        waits=[st["x_free"]], dma=True)
                    pf_mil[1] = E("sync", lambda s: s.dma_start(
                        out=LNx, in_=xn_d[s_idx, tb + 128:tb + 256, :]),
                        waits=[st["x2b_free"], st["lnx_free"]], dma=True)
                for k in range(gn):
                    hc = g0 + k
                    par = hc % 2
                    P1 = A if par == 0 else Bp
                    pb0 = 4 * par
                    mil_wg = E("gpsimd", lambda gg, hc=hc, par=par: gg.dma_start(
                        out=wg_bufs[par][:, :].rearrange("p (c n) -> p c n", c=NC),
                        in_=wgt.rearrange("(c p) n -> p c n", p=128)
                        [:, :, 128 * hc:128 * hc + 128]),
                        waits=[st["wg_free"][par]], dma=True, dsem=s_wg[par])
                    mil_wu = E("gpsimd", lambda gg, hc=hc, par=par: gg.dma_start(
                        out=wu_bufs[par][:, :].rearrange("p (c n) -> p c n", c=NC),
                        in_=wut.rearrange("(c p) n -> p c n", p=128)
                        [:, :, 128 * hc:128 * hc + 128]),
                        waits=[st["wu_free"][par]], dma=True, dsem=s_wu[par])
                    wdm[k] = E("gpsimd", lambda gg, hc=hc, k=k: gg.dma_start(
                        out=wd_slot(k), in_=wdt[128 * hc:128 * hc + 128, :]),
                        waits=[st["wd_free"][k]], dma=True, dsem=s_wd[k])
                    sub_g = E("vector", lambda v, par=par: v.tensor_scalar(
                        out=wg_bufs[par][:, :], in0=wg_bufs[par][:, :],
                        scalar1=-128.0, scalar2=None, op0=ALU.add),
                        waits=[mil_wg])
                    sub_u = E("vector", lambda v, par=par: v.tensor_scalar(
                        out=wu_bufs[par][:, :], in0=wu_bufs[par][:, :],
                        scalar1=-128.0, scalar2=None, op0=ALU.add),
                        waits=[mil_wu])
                    sub_d = E("vector", lambda v, k=k: v.tensor_scalar(
                        out=wd_slot(k), in0=wd_slot(k),
                        scalar1=-128.0, scalar2=None, op0=ALU.add),
                        waits=[wdm[k]])
                    wdm[k] = sub_d
                    mm = None
                    for c in range(NC):
                        for b2 in range(2):
                            w = [sub_g] if (c == 0 and b2 == 0) else []
                            mm = PW("tensor", lambda t, c=c, b2=b2, par=par,
                                    P1=P1: t.matmul(
                                P1[:, 512 * b2:512 * b2 + 512],
                                wg_bufs[par][:, 128 * c:128 * c + 128],
                                X2T[:, 1024 * c + 512 * b2:
                                    1024 * c + 512 * b2 + 512],
                                start=(c == 0), stop=(c == NC - 1)),
                                banks=[pb0 + b2], waits=w,
                                inc=(c == NC - 1 and b2 == 1))
                    st["wg_free"][par] = mm
                    milg = PR("scalar", lambda s, hc=hc, P1=P1: s.activation(
                        gate_sb, P1[:, 0:1024], AF.Silu,
                        scale=sg_tile[:, hc:hc + 1]), banks=[pb0, pb0 + 1],
                        waits=[mm, mil_gev, st["y_prev"]])
                    mm = None
                    for c in range(NC):
                        for b2 in range(2):
                            w = [sub_u] if (c == 0 and b2 == 0) else []
                            mm = PW("tensor", lambda t, c=c, b2=b2, par=par,
                                    P1=P1: t.matmul(
                                P1[:, 1024 + 512 * b2:1024 + 512 * b2 + 512],
                                wu_bufs[par][:, 128 * c:128 * c + 128],
                                X2T[:, 1024 * c + 512 * b2:
                                    1024 * c + 512 * b2 + 512],
                                start=(c == 0), stop=(c == NC - 1)),
                                banks=[pb0 + 2 + b2], waits=w,
                                inc=(c == NC - 1 and b2 == 1))
                    st["wu_free"][par] = mm
                    milu = PR("vector", lambda v, hc=hc, P1=P1: v.tensor_scalar(
                        out=up_sb, in0=P1[:, 1024:2048],
                        scalar1=su_tile[:, hc:hc + 1], scalar2=None,
                        op0=ALU.mult), banks=[pb0 + 2, pb0 + 3],
                        waits=[mm, mil_uev, st["y_prev"]])
                    ymils[k] = E("vector", lambda v, k=k: v.tensor_tensor(
                        out=ygT[:, 1024 * k:1024 * k + 1024], in0=gate_sb,
                        in1=up_sb, op=ALU.mult), waits=[milg, milu])
                    st["y_prev"] = ymils[k]
                    if KSER >= 2:
                        prog.barrier()
                for jj in range(NSUB):
                    Pp = A if jj % 2 == 0 else Bp
                    pbanks = [0, 1, 2, 3] if jj % 2 == 0 else [4, 5, 6, 7]
                    mm = None
                    for k in range(gn):
                        for b4 in range(4):
                            w = [wdm[k], ymils[k]] if b4 == 0 else []
                            mm = PW("tensor", lambda t, k=k, b4=b4, jj=jj, P=Pp:
                                    t.matmul(P[:, 512 * b4:512 * b4 + 512],
                                             ygT[:, 1024 * k + 128 * jj:
                                                 1024 * k + 128 * jj + 128],
                                             wd_slot(k)[:, 512 * b4:512 * b4 + 512],
                                             start=(k == 0),
                                             stop=(k == gn - 1)),
                                    banks=[pbanks[b4]], waits=w,
                                    inc=(k == gn - 1 and b4 == 3))
                    if g == 0 and not last:
                        PR("vector", lambda v, jj=jj, P=Pp: v.tensor_copy(
                            d_acc[:, 2048 * jj:2048 * jj + 2048], P[:, :]),
                            banks=pbanks, waits=[mm])
                    elif not last:
                        PR("vector", lambda v, jj=jj, P=Pp: v.tensor_tensor(
                            out=d_acc[:, 2048 * jj:2048 * jj + 2048], in0=P[:, :],
                            in1=d_acc[:, 2048 * jj:2048 * jj + 2048], op=ALU.add),
                            banks=pbanks, waits=[mm])
                    else:
                        m1 = PR("vector", lambda v, jj=jj, P=Pp: v.tensor_tensor(
                            out=oev, in0=P[:, :],
                            in1=d_acc[:, 2048 * jj:2048 * jj + 2048],
                            op=ALU.add), banks=pbanks,
                            waits=[mm, st["oev_free"]])
                        m2 = E("vector", lambda v, jj=jj: v.tensor_tensor(
                            out=d_acc[:, 2048 * jj:2048 * jj + 2048],
                            in0=oev, in1=sdr_bc, op=ALU.mult),
                            waits=[m1, mil_sdr])
                        m3 = E("vector", lambda v, jj=jj: v.tensor_tensor(
                            out=oev, in0=d_acc[:, 2048 * jj:2048 * jj + 2048],
                            in1=xnb(jj), op=ALU.add),
                            waits=[m2, pf_mil[jj]])
                        if jj % 2 == 0:
                            st["x_free"] = m3
                        else:
                            st["lnx_free"] = m3
                        if jj + 2 < NSUB:
                            pf_mil[jj + 2] = E("sync", lambda s, jj=jj: s.dma_start(
                                out=xnb(jj), in_=xn_d[s_idx, tb + 128 * (jj + 2):
                                                      tb + 128 * (jj + 3), :]),
                                waits=[m3], dma=True)
                        st["oev_free"] = E("sync", lambda s, jj=jj: s.dma_start(
                            out=y_d[s_idx, tb + 128 * jj:tb + 128 * jj + 128, :],
                            in_=oev), waits=[m3], dma=True)
                for k in range(gn):
                    st["wd_free"][k] = mm
                if KSER >= 1:
                    prog.barrier()

        phase1(0)
        prog.barrier()
        phase3(0)
        prog.barrier()
        phase1(1)
        prog.barrier()
        phase3(1)
        prog.barrier()
        for mi in range(NMACRO):
            phaseF(mi)
        prog.barrier()

        fin = prog.all_milestones()
        E("sync", lambda s: s.nop(), waits=fin, inc=False)

        with nc.Block() as block:
            @block.sync
            def _(e):
                prog.replay("sync", e)

            @block.tensor
            def _(e):
                prog.replay("tensor", e)

            @block.vector
            def _(e):
                prog.replay("vector", e)

            @block.scalar
            def _(e):
                prog.replay("scalar", e)

            @block.gpsimd
            def _(e):
                prog.replay("gpsimd", e)

    return nc


def kernel(**inp):
    x = np.ascontiguousarray(inp["L0_attn_x"], dtype=np.float32)
    perm = []
    for oc in range(16):
        perm += list(range(64 * oc, 64 * oc + 64))
        perm += list(range(64 * (oc + 16), 64 * (oc + 16) + 64))
    wqt = np.ascontiguousarray(np.asarray(inp["L0_attn_Wq"])[perm, :].T)
    wkt = np.ascontiguousarray(np.asarray(inp["L0_attn_Wk"]).T)
    wvt = np.ascontiguousarray(np.asarray(inp["L0_attn_Wv"]).T)
    wot = np.ascontiguousarray(np.asarray(inp["L0_attn_Wo"]).T)
    wgt = np.ascontiguousarray(np.asarray(inp["L0_ffn_wq_gate"]).T)
    wut = np.ascontiguousarray(np.asarray(inp["L0_ffn_wq_up"]).T)
    wdt = np.ascontiguousarray(np.asarray(inp["L0_ffn_wq_down"]).T)
    ns = np.ascontiguousarray(np.asarray(inp["L0_attn_norm_scale"]).reshape(NC, 128))
    ss = np.ascontiguousarray(np.asarray(inp["L0_attn_split_scale"]).reshape(NC, 128))
    sg = np.ascontiguousarray(np.asarray(inp["L0_ffn_s_gate"]).reshape(NHC, 128))
    su = np.ascontiguousarray(np.asarray(inp["L0_ffn_s_up"]).reshape(NHC, 128))
    sdb = np.ascontiguousarray(np.broadcast_to(
        np.asarray(inp["L0_ffn_s_down"]).reshape(1, D), (128, D)))

    nc = build_nc()
    in_maps = []
    for c in range(NCORE):
        in_maps.append({
            "x": np.ascontiguousarray(x[BPC * c:BPC * (c + 1)]),
            "wqt": wqt, "wkt": wkt, "wvt": wvt, "wot": wot,
            "wgt": wgt, "wut": wut, "wdt": wdt,
            "ns": ns, "ss": ss, "sg": sg, "su": su, "sdb": sdb,
        })
    res = run_bass_kernel_spmd(nc, in_maps, core_ids=list(range(NCORE)))
    return np.concatenate([np.asarray(r["y"]) for r in res.results], axis=0)


# version 20 (final)
# speedup vs baseline: 1.2527x; 1.1780x over previous
import os
import numpy as np
from contextlib import ExitStack
import concourse.bass as bass
import concourse.mybir as mybir
import concourse.bass_utils as _bu
from concourse.bass_utils import run_bass_kernel_spmd

if os.environ.get("KLDW", "1") == "1" and not getattr(_bu, "_ldw_patched", False):
    _orig_run_command = _bu.run_command

    def _run_command_ldw(argv, **kwargs):
        argv = [a.replace("--enable-ldw-opt=false", "--enable-ldw-opt=true")
                if isinstance(a, str) else a for a in argv]
        return _orig_run_command(argv, **kwargs)

    _bu.run_command = _run_command_ldw
    _bu._ldw_patched = True

F32 = mybir.dt.float32
BF16 = mybir.dt.bfloat16
I32 = mybir.dt.int32
AF = mybir.ActivationFunctionType
AX = mybir.AxisListType
ALU = mybir.AluOpType

B, S, D, HID = 16, 2048, 2048, 5632
H, DH = 32, 64
NCORE = 8
BPC = B // NCORE
NT = S // 128
NC = D // 128
NHC = HID // 128
MACRO = 1024
NMACRO = BPC * S // MACRO
NSUB = MACRO // 128
FG = 8
NGRP = (NHC + FG - 1) // FG


class Sem:
    def __init__(self, ctx, nc, name):
        self.h = ctx.enter_context(nc.semaphore(name))
        self.cnt = 0


class Prog:
    ENGINES = ("sync", "tensor", "vector", "scalar", "gpsimd")

    def __init__(self, ctx, nc):
        self.streams = {e: [] for e in self.ENGINES}
        self.sems = {e: Sem(ctx, nc, f"s_{e}") for e in self.ENGINES}
        self.dma_sp = Sem(ctx, nc, "dma_sp")
        self.dma_gp = Sem(ctx, nc, "dma_gp")
        self.extra_dma = []
        self.bank_readers = {b: {} for b in range(8)}
        self.bank_write = {b: None for b in range(8)}
        self.serial = {"vector": False, "scalar": False, "gpsimd": False,
                       "tensor": False, "sync": False}
        self.pending = {e: [] for e in self.ENGINES}

    def emit(self, eng, fn, waits=(), dma=False, inc=True, dsem=None):
        w = [x for x in waits if x is not None]
        if self.pending[eng]:
            w += self.pending[eng]
            self.pending[eng] = []
        if self.serial[eng] and not dma and self.sems[eng].cnt > 0:
            w.append((self.sems[eng], self.sems[eng].cnt))
        if dma and dsem is not None:
            sem = dsem
        else:
            sem = (self.dma_sp if eng == "sync" else self.dma_gp) if dma else self.sems[eng]
        step = 16 if dma else 1
        mil = None
        if inc:
            sem.cnt += step
            mil = (sem, sem.cnt)
        self.streams[eng].append((fn, w, (sem, step) if inc else None))
        return mil

    def psum_write(self, eng, fn, banks, waits=(), inc=True):
        w = list(waits)
        for b in banks:
            for s, v in self.bank_readers[b].items():
                w.append((s, v))
        mil = self.emit(eng, fn, w, inc=inc)
        for b in banks:
            if mil is not None:
                self.bank_write[b] = mil
            self.bank_readers[b] = {}
        return mil

    def psum_read(self, eng, fn, banks, waits=(), dma=False):
        w = list(waits)
        for b in banks:
            if self.bank_write[b] is not None:
                w.append(self.bank_write[b])
        mil = self.emit(eng, fn, w, dma=dma)
        for b in banks:
            self.bank_readers[b][mil[0]] = mil[1]
        return mil

    def all_milestones(self):
        out = []
        for e in self.ENGINES:
            if self.sems[e].cnt:
                out.append((self.sems[e], self.sems[e].cnt))
        if self.dma_sp.cnt:
            out.append((self.dma_sp, self.dma_sp.cnt))
        if self.dma_gp.cnt:
            out.append((self.dma_gp, self.dma_gp.cnt))
        for sem in self.extra_dma:
            if sem.cnt:
                out.append((sem, sem.cnt))
        return out

    def barrier(self):
        mils = self.all_milestones()
        for e in self.ENGINES:
            self.pending[e] = list(mils)

    def replay(self, eng_name, eng):
        maxseen = {}
        for fn, waits, inc in self.streams[eng_name]:
            seen = {}
            for s, v in waits:
                if maxseen.get(s, -1) >= v:
                    continue
                if s not in seen or seen[s] < v:
                    seen[s] = v
            for s, v in seen.items():
                eng.wait_ge(s.h, v)
                maxseen[s] = v
            inst = fn(eng)
            if inc is not None:
                sem, step = inc
                inst.then_inc(sem.h, step)


def build_nc():
    nc = bass.Bass("TRN2", target_bir_lowering=False)
    x_d = nc.dram_tensor("x", [BPC, S, D], F32, kind="ExternalInput").ap()
    wqt = nc.dram_tensor("wqt", [D, D], F32, kind="ExternalInput").ap()
    wkt = nc.dram_tensor("wkt", [D, 128], F32, kind="ExternalInput").ap()
    wvt = nc.dram_tensor("wvt", [D, 128], F32, kind="ExternalInput").ap()
    wot = nc.dram_tensor("wot", [D, D], F32, kind="ExternalInput").ap()
    wgt = nc.dram_tensor("wgt", [D, HID], I32, kind="ExternalInput").ap()
    wut = nc.dram_tensor("wut", [D, HID], I32, kind="ExternalInput").ap()
    wdt = nc.dram_tensor("wdt", [HID, D], I32, kind="ExternalInput").ap()
    ns_d = nc.dram_tensor("ns", [NC, 128], F32, kind="ExternalInput").ap()
    ss_d = nc.dram_tensor("ss", [NC, 128], F32, kind="ExternalInput").ap()
    sg_d = nc.dram_tensor("sg", [NHC, 128], F32, kind="ExternalInput").ap()
    su_d = nc.dram_tensor("su", [NHC, 128], F32, kind="ExternalInput").ap()
    sdb_d = nc.dram_tensor("sdb", [128, D], F32, kind="ExternalInput").ap()
    y_d = nc.dram_tensor("y", [BPC, S, D], F32, kind="ExternalOutput").ap()
    DBG = os.environ.get("KBG") == "1"
    KSER = int(os.environ.get("KSER", "0"))
    if DBG:
        xn_d = nc.dram_tensor("xn", [BPC, S, D], F32, kind="ExternalOutput").ap()
    else:
        xn_d = nc.dram_tensor("xn", [BPC, S, D], F32).ap()

    ctx = ExitStack()
    with ctx:
        sbt = lambda n, sh, dt: ctx.enter_context(nc.sbuf_tensor(n, sh, dt))
        pst = lambda n, sh, dt: ctx.enter_context(nc.psum_tensor(n, sh, dt))
        wq_b = sbt("wq_b", [128, NC * D], BF16)
        wo_b = sbt("wo_b", [128, NC * D], BF16)
        kv_b = sbt("kv_b", [128, 4096], BF16)
        bbf = sbt("bbf", [128, 14592], BF16)
        bf32 = sbt("bf32", [128, 8192], F32)
        sm = sbt("sm", [128, 160], F32)
        ident = sbt("ident", [128, 128], F32)
        ident_b = sbt("ident_b", [128, 128], BF16)
        ind2 = sbt("ind2", [128, 128], BF16)
        ones_r = sbt("ones_r", [65, 128], BF16)
        mvb = sbt("mvb", [128, 4], BF16)
        lntmp = sbt("lntmp", [128, 512], F32)
        vl1 = lntmp[:, 0:128]
        vl2 = lntmp[:, 128:256]
        A = pst("A", [128, 2048], F32)
        Bp = pst("Bp", [128, 2048], F32)

        wk_v = kv_b[:, 0:2048]
        wv_v = kv_b[:, 2048:4096]
        KT = bbf[:, 0:2048]
        Vv = bbf[:, 2048:4128]
        qT = bbf[:, 4352:6400]
        hbT = bbf[:, 6400:8448]
        attnT = bbf[:, 8448:10496]
        mrow = bbf[:, 8448:10496]
        q_tm = bbf[:, 8448:10496]
        Pb = [bbf[:, 10496:12544], bbf[:, 12544:14592]]
        ygT = bbf[:, 0:8192]
        wu_bufs = [bbf[:, 8192:10240], bbf[:, 10240:12288]]
        gate_sb = bbf[:, 12288:13312]
        up_sb = bbf[:, 13312:14336]
        wg_bufs = [kv_b[:, 0:2048], kv_b[:, 2048:4096]]
        d_acc = wq_b[:, :].bitcast(F32)
        X2T = wo_b[:, 0:16384]
        wd_slot = lambda i: wo_b[:, 16384 + 2048 * i:16384 + 2048 * (i + 1)]

        x_f = bf32[:, 0:2048]
        LNx = bf32[:, 2048:4096]
        hb_bf = bf32[:, 4096:5120].bitcast(BF16)
        attn_bf = bf32[:, 5120:6144].bitcast(BF16)
        resid_f = bf32[:, 6144:8192]
        X2b = bf32[:, 2048:3072].bitcast(BF16)
        sdr_bc = bf32[:, 4096:6144]
        oev = bf32[:, 6144:8192]

        stats = sm[:, 0:24]
        mv = sm[:, 24:26]
        lnv = sm[:, 26:27]
        rstd = sm[:, 27:28]
        t1v = sm[:, 28:29]
        t2v = sm[:, 29:30]
        lnr = sm[:, 30:31]
        rmsv = sm[:, 31:32]
        rmse = sm[:, 32:33]
        rv = sm[:, 33:34]
        rrv = sm[:, 34:35]
        sqcol = sm[:, 36:40]
        kn_t = sm[:, 38:39]
        kmax = sm[:, 39:40]
        mk_vec = sm[:, 40:41]
        recip = sm[:, 41:42]
        eps5 = sm[:, 42:43]
        eps18 = sm[:, 43:44]
        qm_t = sm[:, 44:45]
        qmax2 = sm[:, 45:46]
        qln = sm[:, 46:47]
        bias_g = [sm[:, 152:153], sm[:, 153:154]]
        a_tile = sm[:, 48:64]
        sg_tile = sm[:, 64:108]
        su_tile = sm[:, 108:152]

        prog = Prog(ctx, nc)
        E = prog.emit
        PW = prog.psum_write
        PR = prog.psum_read
        s_watt = Sem(ctx, nc, "d_watt")
        s_misc = Sem(ctx, nc, "d_misc")
        s_wg = [Sem(ctx, nc, "d_wg0"), Sem(ctx, nc, "d_wg1")]
        s_wu = [Sem(ctx, nc, "d_wu0"), Sem(ctx, nc, "d_wu1")]
        s_wd = [Sem(ctx, nc, f"d_wd{i}") for i in range(FG)]
        prog.extra_dma += [s_watt, s_misc] + s_wg + s_wu + s_wd

        TR = [(c % 2) * 512 + (c // 2) * 128 for c in range(8)]
        bTR = lambda i: 4 + ((TR[i]) // 512)
        STM = [0, 512]
        QP = [1024, 1536]
        KP = 1152
        VP = 1664
        AVS = [1280, 1792]

        st = {"x_free": None, "P_free": [None, None], "ST_free": [None, None],
              "y_prev": None, "wg_free": [None, None], "wu_free": [None, None],
              "wd_free": [None] * FG, "oev_free": None, "milo": None,
              "x2b_free": None, "resid_free": None, "lnx_free": None}

        E("gpsimd", lambda g: g.memset(ident[:, :], 0.0))
        milI = E("gpsimd", lambda g: g.affine_select(
            out=ident[:, :], in_=ident[:, :], compare_op=ALU.not_equal,
            fill=1.0, base=0, pattern=[[-1, 128]], channel_multiplier=1))
        E("gpsimd", lambda g: g.memset(ident_b[:, :], 0.0))
        milIb = E("gpsimd", lambda g: g.affine_select(
            out=ident_b[:, :], in_=ident_b[:, :], compare_op=ALU.not_equal,
            fill=1.0, base=0, pattern=[[-1, 128]], channel_multiplier=1))
        m_onesr = E("vector", lambda v: v.memset(ones_r[:, :], 1.0))
        m_qTz = E("vector", lambda v: v.memset(qT[64:128, :], 0.0))
        E("vector", lambda v: v.memset(ind2[:, :], 0.0))
        E("vector", lambda v: v.memset(ind2[0:64, 0:1], 1.0))
        m_ind2 = E("vector", lambda v: v.memset(ind2[64:128, 64:65], 1.0))
        E("vector", lambda v: v.memset(eps5[:, :], 1e-5))
        m_eps = E("vector", lambda v: v.memset(eps18[:, :], 1e-18))

        E("gpsimd", lambda g: g.dma_start(
            out=wq_b[:, :].rearrange("p (c n) -> p c n", c=NC),
            in_=wqt.rearrange("(c p) n -> p c n", p=128)), dma=True, dsem=s_watt)
        E("gpsimd", lambda g: g.dma_start(
            out=wo_b[:, :].rearrange("p (c n) -> p c n", c=NC),
            in_=wot.rearrange("(c p) n -> p c n", p=128)), dma=True, dsem=s_watt)
        E("gpsimd", lambda g: g.dma_start(
            out=wk_v.rearrange("p (c n) -> p c n", c=NC),
            in_=wkt.rearrange("(c p) n -> p c n", p=128)), dma=True, dsem=s_watt)
        E("gpsimd", lambda g: g.dma_start(
            out=wv_v.rearrange("p (c n) -> p c n", c=NC),
            in_=wvt.rearrange("(c p) n -> p c n", p=128)), dma=True, dsem=s_watt)
        mil_wq = mil_wo = mil_wk = mil_wv = (s_watt, 64)
        E("sync", lambda s: s.dma_start(out=vl1[0:NC, :], in_=ns_d), dma=True,
          dsem=s_misc)
        E("sync", lambda s: s.dma_start(out=vl2[0:NC, :], in_=ss_d), dma=True,
          dsem=s_misc)
        mil_v1 = mil_v2 = (s_misc, 32)
        mil_sdr = None

        mil_a16 = E("vector", lambda v: v.tensor_tensor(
            out=vl1[0:NC, :], in0=vl1[0:NC, :], in1=vl2[0:NC, :], op=ALU.mult),
            waits=[mil_v1, mil_v2])
        mil_aT = PW("tensor", lambda t: t.transpose(
            Bp[:, 0:16], vl1[0:NC, :], ident[0:NC, 0:NC]),
            banks=[4], waits=[mil_a16, milI])
        PR("vector", lambda v: v.tensor_copy(a_tile, Bp[:, 0:16]),
           banks=[4], waits=[mil_aT])
        E("sync", lambda s: s.dma_start(out=vl1[0:NHC, :], in_=sg_d),
          waits=[mil_a16], dma=True, dsem=s_misc)
        E("sync", lambda s: s.dma_start(out=vl2[0:NHC, :], in_=su_d),
          waits=[mil_a16], dma=True, dsem=s_misc)
        mil_g1 = mil_g2 = (s_misc, 64)
        mil_gT = PW("tensor", lambda t: t.transpose(
            Bp[:, 16:60], vl1[0:NHC, :], ident[0:NHC, 0:NHC]),
            banks=[4], waits=[mil_g1])
        mil_gev = PR("vector", lambda v: v.tensor_copy(sg_tile, Bp[:, 16:60]),
                     banks=[4], waits=[mil_gT])
        mil_uT = PW("tensor", lambda t: t.transpose(
            Bp[:, 512:556], vl2[0:NHC, :], ident[0:NHC, 0:NHC]),
            banks=[5], waits=[mil_g2])
        mil_uev = PR("vector", lambda v: v.tensor_copy(su_tile, Bp[:, 512:556]),
                     banks=[5], waits=[mil_uT])

        def ln_chain(mil_x, want_lnx, want_hbase, hb_waits=()):
            mil_st = None
            for i in range(4):
                mil_st = E("vector", lambda v, i=i: v.bn_stats(
                    stats[:, 6 * i:6 * i + 6], x_f[:, 512 * i:512 * i + 512]),
                    waits=[mil_x])
            mil_mv = E("vector", lambda v: v.bn_aggr(mv, stats), waits=[mil_st])
            mil_lnv = E("scalar", lambda s: s.activation(
                lnv, mv[:, 1:2], AF.Ln, bias=eps5), waits=[mil_mv, m_eps])
            mil_rstd = E("scalar", lambda s: s.activation(
                rstd, lnv, AF.Exp, scale=-0.5), waits=[mil_lnv])
            out = {"rstd": mil_rstd, "lnx": None, "hb": None}
            if want_lnx:
                out["lnx"] = E("vector", lambda v: v.tensor_scalar(
                    out=LNx, in0=x_f, scalar1=mv[:, 0:1], scalar2=rstd,
                    op0=ALU.subtract, op1=ALU.mult), waits=[mil_rstd])
            if want_hbase:
                m1 = E("vector", lambda v: v.tensor_tensor(
                    out=t1v, in0=mv[:, 1:2], in1=rstd, op=ALU.mult),
                    waits=[mil_rstd])
                m2 = E("vector", lambda v: v.tensor_tensor(
                    out=t2v, in0=t1v, in1=rstd, op=ALU.mult), waits=[m1])
                m3 = E("scalar", lambda s: s.activation(
                    lnr, t2v, AF.Ln, bias=eps18), waits=[m2])
                m4 = E("scalar", lambda s: s.activation(
                    rmsv, lnr, AF.Exp, scale=0.5), waits=[m3])
                m5 = E("vector", lambda v: v.tensor_scalar(
                    out=rmse, in0=rmsv, scalar1=1e-8, scalar2=None,
                    op0=ALU.add), waits=[m4])
                m6 = E("vector", lambda v: v.reciprocal(rv, rmse), waits=[m5])
                m7 = E("vector", lambda v: v.tensor_tensor(
                    out=rrv, in0=rstd, in1=rv, op=ALU.mult), waits=[m6])
                out["hb"] = E("vector", lambda v: v.tensor_scalar(
                    out=hb_bf, in0=x_f, scalar1=mv[:, 0:1], scalar2=rrv,
                    op0=ALU.subtract, op1=ALU.mult),
                    waits=[m7] + list(hb_waits))
            return out

        def transpose_16(src, dst_bf, scale_fn, extra_wait, dst_off_fn=None):
            evacs = []
            for c in range(NC):
                slot = TR[c % 8]
                bank = bTR(c % 8)
                milT = PW("tensor", lambda t, c=c, slot=slot, src=src: t.transpose(
                    Bp[:, slot:slot + 128].bitcast(BF16)[:, 0:128],
                    src[:, 128 * c:128 * c + 128],
                    ident_b[:, :]), banks=[bank], waits=[extra_wait, milIb])
                off = dst_off_fn(c) if dst_off_fn else 128 * c
                sc = scale_fn(c) if scale_fn else None
                if c % 2 == 0:
                    mile = PR("scalar",
                              lambda s, slot=slot, off=off, sc=sc, dst=dst_bf:
                              s.activation(dst[:, off:off + 128],
                                           Bp[:, slot:slot + 128]
                                           .bitcast(BF16)[:, 0:128],
                                           AF.Identity,
                                           scale=(1.0 if sc is None else sc)),
                              banks=[bank], waits=[milT])
                elif sc is None:
                    mile = PR("vector",
                              lambda v, slot=slot, off=off, dst=dst_bf:
                              v.tensor_copy(dst[:, off:off + 128],
                                            Bp[:, slot:slot + 128]
                                            .bitcast(BF16)[:, 0:128]),
                              banks=[bank], waits=[milT])
                else:
                    mile = PR("vector",
                              lambda v, slot=slot, off=off, sc=sc, dst=dst_bf:
                              v.tensor_scalar(out=dst[:, off:off + 128],
                                              in0=Bp[:, slot:slot + 128]
                                              .bitcast(BF16)[:, 0:128],
                                              scalar1=sc, scalar2=None,
                                              op0=ALU.mult),
                              banks=[bank], waits=[milT])
                evacs.append(mile)
            return evacs

        def phase1(s_idx):
            mil_vm = E("vector", lambda v: v.memset(Vv, 1.0))
            E("vector", lambda v: v.memset(kmax, 0.0))
            mil_x = E("sync", lambda s: s.dma_start(
                out=x_f, in_=x_d[s_idx, 0:128, :]),
                waits=[st["x_free"]], dma=True)
            ln = ln_chain(mil_x, want_lnx=False, want_hbase=True)
            for j in range(NT):
                st["x_free"] = ln["hb"]
                ev = transpose_16(hb_bf, hbT, lambda c: a_tile[:, c:c + 1], ln["hb"])
                if j + 1 < NT:
                    mil_x = E("sync", lambda s, j=j: s.dma_start(
                        out=x_f, in_=x_d[s_idx, 128 * (j + 1):128 * (j + 2), :]),
                        waits=[st["x_free"]], dma=True)
                    ln = ln_chain(mil_x, want_lnx=False, want_hbase=True,
                                  hb_waits=[ev[-1]])
                milk = None
                for c in range(NC):
                    w = [ev[c]] + ([mil_wk] if c == 0 else [])
                    milk = PW("tensor", lambda t, c=c: t.matmul(
                        Bp[:, KP:KP + 128], wk_v[:, 128 * c:128 * c + 128],
                        hbT[:, 128 * c:128 * c + 128],
                        start=(c == 0), stop=(c == NC - 1)), banks=[6], waits=w,
                        inc=(c == NC - 1))
                milke = PR("scalar", lambda s, j=j: s.activation(
                    KT[:, 128 * j:128 * j + 128], Bp[:, KP:KP + 128],
                    AF.Identity, scale=1.0), banks=[6], waits=[milk])
                milsq = E("vector", lambda v, j=j: v.tensor_tensor(
                    out=lntmp[:, 0:128].bitcast(BF16)[:, 0:128],
                    in0=KT[:, 128 * j:128 * j + 128],
                    in1=KT[:, 128 * j:128 * j + 128], op=ALU.mult),
                    waits=[milke])
                milknm = PW("tensor", lambda t: t.matmul(
                    Bp[:, VP:VP + 128], ind2[:, :],
                    lntmp[:, 0:128].bitcast(BF16)[:, 0:128],
                    start=True, stop=True), banks=[7], waits=[milsq, m_ind2])
                milknr = PR("vector", lambda v: v.reduce_max(
                    kn_t, Bp[:, VP:VP + 128], axis=AX.X),
                    banks=[7], waits=[milknm])
                E("vector", lambda v: v.tensor_tensor(
                    out=kmax, in0=kn_t, in1=kmax, op=ALU.max), waits=[milknr])
                milv = None
                for c in range(NC):
                    w = [ev[c]] + ([mil_wv] if c == 0 else [])
                    milv = PW("tensor", lambda t, c=c: t.matmul(
                        Bp[:, VP:VP + 128], hbT[:, 128 * c:128 * c + 128],
                        wv_v[:, 128 * c:128 * c + 128],
                        start=(c == 0), stop=(c == NC - 1)), banks=[7], waits=w,
                        inc=(c == NC - 1))
                PR("scalar", lambda s, j=j: s.activation(
                    Vv[:, 130 * j:130 * j + 64], Bp[:, VP:VP + 64],
                    AF.Identity, scale=1.0), banks=[7], waits=[milv, mil_vm])
                PR("scalar", lambda s, j=j: s.activation(
                    Vv[:, 130 * j + 65:130 * j + 129], Bp[:, VP + 64:VP + 128],
                    AF.Identity, scale=1.0), banks=[7], waits=[milv, mil_vm])
            E("vector", lambda v: v.tensor_scalar(
                out=mk_vec, in0=kmax, scalar1=1.0 / 64.0,
                scalar2=None, op0=ALU.mult))

        def emit_av(hs):
            h, m, STb, bank0, mil_exp = hs
            milav = None
            for c in range(NC):
                w = []
                if c == 0:
                    w = [mil_exp[0]]
                if c == 8:
                    w = [mil_exp[1]]
                milav = PW("tensor", lambda t, c=c, h=h, m=m, P=STb:
                           t.matmul(P[:, 0:65],
                                    Pb[m][:, 128 * c:128 * c + 128],
                                    Vv[:, 130 * c + 65 * m:130 * c + 65 * m + 65],
                                    start=(c == 0), stop=(c == NC - 1)),
                           banks=[bank0], waits=w, inc=(c == NC - 1))
            st["P_free"][m] = milav
            milr = PR("vector", lambda v, P=STb: v.reciprocal(
                recip, P[:, 64:65]), banks=[bank0], waits=[milav])
            st["milo"] = PR("vector", lambda v, h=h, P=STb: v.tensor_scalar(
                out=attn_bf[:, 64 * h:64 * h + 64], in0=P[:, 0:64],
                scalar1=recip, scalar2=None, op0=ALU.mult),
                banks=[bank0], waits=[milr])

        def phase3(s_idx):
            mil_x = E("sync", lambda s: s.dma_start(
                out=x_f, in_=x_d[s_idx, 0:128, :]),
                waits=[st["x_free"]], dma=True)
            ln = ln_chain(mil_x, want_lnx=True, want_hbase=True)
            for j in range(NT):
                milt1 = E("vector", lambda v: v.tensor_tensor(
                    out=resid_f, in0=x_f, in1=LNx, op=ALU.add),
                    waits=[ln["lnx"], st["resid_free"]])
                st["x_free"] = milt1
                ev = transpose_16(hb_bf, hbT, lambda c: a_tile[:, c:c + 1], ln["hb"])
                milqq = None
                for c in range(NC):
                    for b4 in range(4):
                        w = [ev[c]] + ([mil_wq] if (c == 0 and b4 == 0) else [])
                        milqq = PW("tensor", lambda t, c=c, b4=b4: t.matmul(
                            A[:, 512 * b4:512 * b4 + 512],
                            hbT[:, 128 * c:128 * c + 128],
                            wq_b[:, 2048 * c + 512 * b4:2048 * c + 512 * b4 + 512],
                            start=(c == 0), stop=(c == NC - 1)),
                            banks=[b4], waits=w,
                            inc=(c == NC - 1 and b4 in (1, 3)))
                        if c == NC - 1 and b4 == 1:
                            milq01 = milqq
                milqe = [None, None]
                milqe[0] = PR("vector", lambda v: v.tensor_copy(
                    q_tm[:, 0:1024], A[:, 0:1024]),
                    banks=[0, 1], waits=[milq01])
                milqe[1] = PR("vector", lambda v: v.tensor_copy(
                    q_tm[:, 1024:2048], A[:, 1024:2048]),
                    banks=[2, 3], waits=[milqq])
                mil_hbz = E("vector", lambda v: v.memset(hbT[0:64, :], 0.0),
                            waits=[milqq])
                mil_qa = [None] * NC
                mil_qb = [None] * NC
                for oc in range(NC):
                    slot = QP[oc % 2]
                    bank = 6 + (oc % 2)
                    milqt = PW("tensor", lambda t, oc=oc, slot=slot: t.transpose(
                        Bp[:, slot:slot + 128].bitcast(BF16)[:, 0:128],
                        q_tm[:, 128 * oc:128 * oc + 128], ident_b[:, :]),
                        banks=[bank], waits=[milqe[oc // 8], milIb])
                    mil_qa[oc] = PR("scalar",
                                    lambda s, oc=oc, slot=slot: s.activation(
                                        qT[0:64, 128 * oc:128 * oc + 128],
                                        Bp[0:64, slot:slot + 128]
                                        .bitcast(BF16)[:, 0:128],
                                        AF.Identity, scale=1.0),
                                    banks=[bank], waits=[milqt])
                    mil_qb[oc] = PR("vector",
                                    lambda v, oc=oc, slot=slot: v.tensor_copy(
                                        hbT[64:128, 128 * oc:128 * oc + 128],
                                        Bp[64:128, slot:slot + 128]
                                        .bitcast(BF16)[:, 0:128]),
                                    banks=[bank], waits=[milqt, mil_hbz])
                mil_bias = [None, None]
                qz_half = [qT, hbT]
                mlnq = None
                for m in range(2):
                    milsq = E("vector", lambda v, m=m: v.tensor_tensor(
                        out=Pb[m], in0=qz_half[m], in1=qz_half[m],
                        op=ALU.mult),
                        waits=[mil_qa[NC - 1], mil_qb[NC - 1],
                               st["P_free"][m]])
                    milqm = None
                    for b4 in range(4):
                        slot = STM[b4 % 2]
                        bank = 4 + (b4 % 2)
                        milm = PW("tensor", lambda t, m=m, b4=b4, slot=slot:
                                  t.matmul(
                                      Bp[:, slot:slot + 512], ind2[:, :],
                                      Pb[m][:, 512 * b4:512 * b4 + 512],
                                      start=True, stop=True), banks=[bank],
                                  waits=[milsq, m_ind2])
                        milrq = PR("vector", lambda v, slot=slot: v.reduce_max(
                            qm_t, Bp[:, slot:slot + 512], axis=AX.X),
                            banks=[bank], waits=[milm])
                        if b4 == 0:
                            milqm = E("vector", lambda v: v.tensor_copy(
                                qmax2, qm_t), waits=[milrq, mlnq])
                        else:
                            milqm = E("vector", lambda v: v.tensor_tensor(
                                out=qmax2, in0=qm_t, in1=qmax2, op=ALU.max),
                                waits=[milrq])
                    mlnq = E("scalar", lambda s: s.activation(
                        qln, qmax2, AF.Ln, bias=eps18, scale=mk_vec),
                        waits=[milqm])
                    mexq = E("scalar", lambda s, m=m: s.activation(
                        mvb[:, m:m + 1], qln, AF.Exp, scale=0.5),
                        waits=[mlnq])
                    slot = STM[m]
                    bank = 4 + m
                    mb = PW("tensor", lambda t, m=m, slot=slot: t.matmul(
                        Bp[:, slot:slot + 1],
                        ones_r[64 * m:64 * m + 1, 0:128],
                        mvb[64 * m:64 * m + 1, m:m + 1],
                        start=True, stop=True), banks=[bank],
                        waits=[mexq, m_onesr])
                    mil_bias[m] = PR("scalar", lambda s, m=m, slot=slot:
                                     s.activation(bias_g[m],
                                                  Bp[:, slot:slot + 1],
                                                  AF.Identity, scale=-0.125),
                                     banks=[bank], waits=[mb])
                if j + 1 < NT:
                    mil_x = E("sync", lambda s, j=j: s.dma_start(
                        out=x_f, in_=x_d[s_idx, 128 * (j + 1):128 * (j + 2), :]),
                        waits=[st["x_free"]], dma=True)
                    ln = ln_chain(mil_x, want_lnx=True, want_hbase=True,
                                  hb_waits=[ev[-1]])
                milo = None
                mil_exp_prev = None
                head_state = []
                for i in range(H):
                    m = i % 2
                    pb = i // 2
                    h = 16 * m + pb
                    qsl = qz_half[m][:, 128 * pb:128 * pb + 128]
                    STb = A if m == 0 else Bp
                    bank0 = 0 if m == 0 else 4
                    mil_half = [None, None]
                    for c in list(range(4, NC)) + [0, 1, 2, 3]:
                        bank = bank0 + c // 4
                        w = [mil_qa[pb] if m == 0 else mil_qb[pb]] \
                            if c % 4 == 0 else []
                        mq = PW("tensor", lambda t, c=c, qsl=qsl, m=m, P=STb:
                                t.matmul(P[:, 128 * c:128 * c + 128],
                                         KT[:, 128 * c:128 * c + 128], qsl,
                                         start=(c % 4 == 0), stop=(c % 4 == 3)),
                                banks=[bank], waits=w, inc=(c in (15, 3)))
                        if c == 15:
                            mil_half[1] = mq
                        elif c == 3:
                            mil_half[0] = mq
                    mil_exp = [None, None]
                    for e in (1, 0):
                        mil_exp[e] = PR("scalar", lambda s, e=e, m=m, P=STb:
                                        s.activation(
                                            Pb[m][:, 1024 * e:1024 * e + 1024],
                                            P[:, 1024 * e:1024 * e + 1024],
                                            AF.Exp, scale=0.125,
                                            bias=bias_g[m]),
                                        banks=[bank0 + 2 * e, bank0 + 2 * e + 1],
                                        waits=[mil_half[e], mil_bias[m],
                                               st["P_free"][m]])
                    head_state.append((h, m, STb, bank0, mil_exp))
                    if i >= 1:
                        emit_av(head_state[i - 1])
                emit_av(head_state[H - 1])
                milo = st["milo"]
                evA = transpose_16(attn_bf, attnT, None, milo)
                milwo = None
                for c in range(NC):
                    for b4 in range(4):
                        w = [evA[c]] + ([mil_wo] if (c == 0 and b4 == 0) else [])
                        milwo = PW("tensor", lambda t, c=c, b4=b4: t.matmul(
                            A[:, 512 * b4:512 * b4 + 512],
                            attnT[:, 128 * c:128 * c + 128],
                            wo_b[:, 2048 * c + 512 * b4:2048 * c + 512 * b4 + 512],
                            start=(c == 0), stop=(c == NC - 1)),
                            banks=[b4], waits=w,
                            inc=(c == NC - 1 and b4 == 3))
                milxn = PR("vector", lambda v: v.tensor_tensor(
                    out=resid_f, in0=A[:, :], in1=resid_f, op=ALU.add),
                    banks=[0, 1, 2, 3], waits=[milwo, milt1])
                milst = E("sync", lambda s, j=j: s.dma_start(
                    out=xn_d[s_idx, 128 * j:128 * j + 128, :], in_=resid_f),
                    waits=[milxn], dma=True)
                st["resid_free"] = milst

        def phaseF(mi):
            nonlocal mil_sdr
            s_idx = mi // 2
            tb = 1024 * (mi % 2)
            if mi == 0:
                E("sync", lambda s: s.dma_start(out=sdr_bc, in_=sdb_d),
                  dma=True, dsem=s_misc)
                mil_sdr = (s_misc, 80)
            for jj in range(NSUB):
                mil_x = E("sync", lambda s, jj=jj: s.dma_start(
                    out=x_f,
                    in_=xn_d[s_idx, tb + 128 * jj:tb + 128 * jj + 128, :]),
                    waits=[st["x_free"]], dma=True)
                ln = ln_chain(mil_x, want_lnx=False, want_hbase=False)
                milx2 = E("vector", lambda v: v.tensor_scalar(
                    out=X2b, in0=x_f, scalar1=mv[:, 0:1], scalar2=rstd,
                    op0=ALU.subtract, op1=ALU.mult), waits=[ln["rstd"]])
                st["x_free"] = milx2
                evs = transpose_16(X2b, X2T, None, milx2,
                                   dst_off_fn=lambda c, jj=jj: 1024 * c + 128 * jj)
                st["x2b_free"] = evs[-1]

            for g in range(NGRP):
                g0 = FG * g
                gn = min(FG, NHC - g0)
                last = (g == NGRP - 1)
                ymils = [None] * gn
                wdm = [None] * gn
                pf_mil = [None] * NSUB
                xnb = lambda jj: x_f if jj % 2 == 0 else LNx
                if last:
                    pf_mil[0] = E("sync", lambda s: s.dma_start(
                        out=x_f, in_=xn_d[s_idx, tb:tb + 128, :]),
                        waits=[st["x_free"]], dma=True)
                    pf_mil[1] = E("sync", lambda s: s.dma_start(
                        out=LNx, in_=xn_d[s_idx, tb + 128:tb + 256, :]),
                        waits=[st["x2b_free"], st["lnx_free"]], dma=True)
                for k in range(gn):
                    hc = g0 + k
                    par = hc % 2
                    P1 = A if par == 0 else Bp
                    pb0 = 4 * par
                    mil_wg = E("gpsimd", lambda gg, hc=hc, par=par: gg.dma_start(
                        out=wg_bufs[par][:, :].rearrange("p (c n) -> p c n", c=NC),
                        in_=wgt.rearrange("(c p) n -> p c n", p=128)
                        [:, :, 128 * hc:128 * hc + 128]),
                        waits=[st["wg_free"][par]], dma=True, dsem=s_wg[par])
                    mil_wu = E("gpsimd", lambda gg, hc=hc, par=par: gg.dma_start(
                        out=wu_bufs[par][:, :].rearrange("p (c n) -> p c n", c=NC),
                        in_=wut.rearrange("(c p) n -> p c n", p=128)
                        [:, :, 128 * hc:128 * hc + 128]),
                        waits=[st["wu_free"][par]], dma=True, dsem=s_wu[par])
                    wdm[k] = E("gpsimd", lambda gg, hc=hc, k=k: gg.dma_start(
                        out=wd_slot(k), in_=wdt[128 * hc:128 * hc + 128, :]),
                        waits=[st["wd_free"][k]], dma=True, dsem=s_wd[k])
                    sub_g = E("vector", lambda v, par=par: v.tensor_scalar(
                        out=wg_bufs[par][:, :], in0=wg_bufs[par][:, :],
                        scalar1=-128.0, scalar2=None, op0=ALU.add),
                        waits=[mil_wg])
                    sub_u = E("vector", lambda v, par=par: v.tensor_scalar(
                        out=wu_bufs[par][:, :], in0=wu_bufs[par][:, :],
                        scalar1=-128.0, scalar2=None, op0=ALU.add),
                        waits=[mil_wu])
                    sub_d = E("vector", lambda v, k=k: v.tensor_scalar(
                        out=wd_slot(k), in0=wd_slot(k),
                        scalar1=-128.0, scalar2=None, op0=ALU.add),
                        waits=[wdm[k]])
                    wdm[k] = sub_d
                    mm = None
                    for c in range(NC):
                        for b2 in range(2):
                            w = [sub_g] if (c == 0 and b2 == 0) else []
                            mm = PW("tensor", lambda t, c=c, b2=b2, par=par,
                                    P1=P1: t.matmul(
                                P1[:, 512 * b2:512 * b2 + 512],
                                wg_bufs[par][:, 128 * c:128 * c + 128],
                                X2T[:, 1024 * c + 512 * b2:
                                    1024 * c + 512 * b2 + 512],
                                start=(c == 0), stop=(c == NC - 1)),
                                banks=[pb0 + b2], waits=w,
                                inc=(c == NC - 1 and b2 == 1))
                    st["wg_free"][par] = mm
                    milg = PR("scalar", lambda s, hc=hc, P1=P1: s.activation(
                        gate_sb, P1[:, 0:1024], AF.Silu,
                        scale=sg_tile[:, hc:hc + 1]), banks=[pb0, pb0 + 1],
                        waits=[mm, mil_gev, st["y_prev"]])
                    mm = None
                    for c in range(NC):
                        for b2 in range(2):
                            w = [sub_u] if (c == 0 and b2 == 0) else []
                            mm = PW("tensor", lambda t, c=c, b2=b2, par=par,
                                    P1=P1: t.matmul(
                                P1[:, 1024 + 512 * b2:1024 + 512 * b2 + 512],
                                wu_bufs[par][:, 128 * c:128 * c + 128],
                                X2T[:, 1024 * c + 512 * b2:
                                    1024 * c + 512 * b2 + 512],
                                start=(c == 0), stop=(c == NC - 1)),
                                banks=[pb0 + 2 + b2], waits=w,
                                inc=(c == NC - 1 and b2 == 1))
                    st["wu_free"][par] = mm
                    milu = PR("vector", lambda v, hc=hc, P1=P1: v.tensor_scalar(
                        out=up_sb, in0=P1[:, 1024:2048],
                        scalar1=su_tile[:, hc:hc + 1], scalar2=None,
                        op0=ALU.mult), banks=[pb0 + 2, pb0 + 3],
                        waits=[mm, mil_uev, st["y_prev"]])
                    ymils[k] = E("vector", lambda v, k=k: v.tensor_tensor(
                        out=ygT[:, 1024 * k:1024 * k + 1024], in0=gate_sb,
                        in1=up_sb, op=ALU.mult), waits=[milg, milu])
                    st["y_prev"] = ymils[k]
                    if KSER >= 2:
                        prog.barrier()
                for jj in range(NSUB):
                    Pp = A if jj % 2 == 0 else Bp
                    pbanks = [0, 1, 2, 3] if jj % 2 == 0 else [4, 5, 6, 7]
                    mm = None
                    for k in range(gn):
                        for b4 in range(4):
                            w = [wdm[k], ymils[k]] if b4 == 0 else []
                            mm = PW("tensor", lambda t, k=k, b4=b4, jj=jj, P=Pp:
                                    t.matmul(P[:, 512 * b4:512 * b4 + 512],
                                             ygT[:, 1024 * k + 128 * jj:
                                                 1024 * k + 128 * jj + 128],
                                             wd_slot(k)[:, 512 * b4:512 * b4 + 512],
                                             start=(k == 0),
                                             stop=(k == gn - 1)),
                                    banks=[pbanks[b4]], waits=w,
                                    inc=(k == gn - 1 and b4 == 3))
                    if g == 0 and not last:
                        PR("vector", lambda v, jj=jj, P=Pp: v.tensor_copy(
                            d_acc[:, 2048 * jj:2048 * jj + 2048], P[:, :]),
                            banks=pbanks, waits=[mm])
                    elif not last:
                        PR("vector", lambda v, jj=jj, P=Pp: v.tensor_tensor(
                            out=d_acc[:, 2048 * jj:2048 * jj + 2048], in0=P[:, :],
                            in1=d_acc[:, 2048 * jj:2048 * jj + 2048], op=ALU.add),
                            banks=pbanks, waits=[mm])
                    else:
                        m1 = PR("vector", lambda v, jj=jj, P=Pp: v.tensor_tensor(
                            out=oev, in0=P[:, :],
                            in1=d_acc[:, 2048 * jj:2048 * jj + 2048],
                            op=ALU.add), banks=pbanks,
                            waits=[mm, st["oev_free"]])
                        m2 = E("vector", lambda v, jj=jj: v.tensor_tensor(
                            out=d_acc[:, 2048 * jj:2048 * jj + 2048],
                            in0=oev, in1=sdr_bc, op=ALU.mult),
                            waits=[m1, mil_sdr])
                        m3 = E("vector", lambda v, jj=jj: v.tensor_tensor(
                            out=oev, in0=d_acc[:, 2048 * jj:2048 * jj + 2048],
                            in1=xnb(jj), op=ALU.add),
                            waits=[m2, pf_mil[jj]])
                        if jj % 2 == 0:
                            st["x_free"] = m3
                        else:
                            st["lnx_free"] = m3
                        if jj + 2 < NSUB:
                            pf_mil[jj + 2] = E("sync", lambda s, jj=jj: s.dma_start(
                                out=xnb(jj), in_=xn_d[s_idx, tb + 128 * (jj + 2):
                                                      tb + 128 * (jj + 3), :]),
                                waits=[m3], dma=True)
                        st["oev_free"] = E("sync", lambda s, jj=jj: s.dma_start(
                            out=y_d[s_idx, tb + 128 * jj:tb + 128 * jj + 128, :],
                            in_=oev), waits=[m3], dma=True)
                for k in range(gn):
                    st["wd_free"][k] = mm
                if KSER >= 1:
                    prog.barrier()

        phase1(0)
        prog.barrier()
        phase3(0)
        prog.barrier()
        phase1(1)
        prog.barrier()
        phase3(1)
        prog.barrier()
        for mi in range(NMACRO):
            phaseF(mi)
        prog.barrier()

        fin = prog.all_milestones()
        E("sync", lambda s: s.nop(), waits=fin, inc=False)

        with nc.Block() as block:
            @block.sync
            def _(e):
                prog.replay("sync", e)

            @block.tensor
            def _(e):
                prog.replay("tensor", e)

            @block.vector
            def _(e):
                prog.replay("vector", e)

            @block.scalar
            def _(e):
                prog.replay("scalar", e)

            @block.gpsimd
            def _(e):
                prog.replay("gpsimd", e)

    return nc


def kernel(**inp):
    x = np.ascontiguousarray(inp["L0_attn_x"], dtype=np.float32)
    perm = []
    for oc in range(16):
        perm += list(range(64 * oc, 64 * oc + 64))
        perm += list(range(64 * (oc + 16), 64 * (oc + 16) + 64))
    wqt = np.ascontiguousarray(np.asarray(inp["L0_attn_Wq"])[perm, :].T)
    wkt = np.ascontiguousarray(np.asarray(inp["L0_attn_Wk"]).T)
    wvt = np.ascontiguousarray(np.asarray(inp["L0_attn_Wv"]).T)
    wot = np.ascontiguousarray(np.asarray(inp["L0_attn_Wo"]).T)
    wgt = np.ascontiguousarray(np.asarray(inp["L0_ffn_wq_gate"]).T)
    wut = np.ascontiguousarray(np.asarray(inp["L0_ffn_wq_up"]).T)
    wdt = np.ascontiguousarray(np.asarray(inp["L0_ffn_wq_down"]).T)
    ns = np.ascontiguousarray(np.asarray(inp["L0_attn_norm_scale"]).reshape(NC, 128))
    ss = np.ascontiguousarray(np.asarray(inp["L0_attn_split_scale"]).reshape(NC, 128))
    sg = np.ascontiguousarray(np.asarray(inp["L0_ffn_s_gate"]).reshape(NHC, 128))
    su = np.ascontiguousarray(np.asarray(inp["L0_ffn_s_up"]).reshape(NHC, 128))
    sdb = np.ascontiguousarray(np.broadcast_to(
        np.asarray(inp["L0_ffn_s_down"]).reshape(1, D), (128, D)))

    nc = build_nc()
    in_maps = []
    for c in range(NCORE):
        in_maps.append({
            "x": np.ascontiguousarray(x[BPC * c:BPC * (c + 1)]),
            "wqt": wqt, "wkt": wkt, "wvt": wvt, "wot": wot,
            "wgt": wgt, "wut": wut, "wdt": wdt,
            "ns": ns, "ss": ss, "sg": sg, "su": su, "sdb": sdb,
        })
    res = run_bass_kernel_spmd(nc, in_maps, core_ids=list(range(NCORE)))
    return np.concatenate([np.asarray(r["y"]) for r in res.results], axis=0)


# version 21
# speedup vs baseline: 1.2781x; 1.0203x over previous
import os
import numpy as np
from contextlib import ExitStack
import concourse.bass as bass
import concourse.mybir as mybir
import concourse.bass_utils as _bu
from concourse.bass_utils import run_bass_kernel_spmd

if os.environ.get("KLDW", "1") == "1" and not getattr(_bu, "_ldw_patched", False):
    _orig_run_command = _bu.run_command

    def _run_command_ldw(argv, **kwargs):
        argv = [a.replace("--enable-ldw-opt=false", "--enable-ldw-opt=true")
                if isinstance(a, str) else a for a in argv]
        return _orig_run_command(argv, **kwargs)

    _bu.run_command = _run_command_ldw
    _bu._ldw_patched = True

F32 = mybir.dt.float32
BF16 = mybir.dt.bfloat16
I32 = mybir.dt.int32
AF = mybir.ActivationFunctionType
AX = mybir.AxisListType
ALU = mybir.AluOpType

B, S, D, HID = 16, 2048, 2048, 5632
H, DH = 32, 64
NCORE = 8
BPC = B // NCORE
NT = S // 128
NC = D // 128
NHC = HID // 128
MACRO = 1024
NMACRO = BPC * S // MACRO
NSUB = MACRO // 128
FG = 8
NGRP = (NHC + FG - 1) // FG


class Sem:
    def __init__(self, ctx, nc, name):
        self.h = ctx.enter_context(nc.semaphore(name))
        self.cnt = 0


class Prog:
    ENGINES = ("sync", "tensor", "vector", "scalar", "gpsimd")

    def __init__(self, ctx, nc):
        self.streams = {e: [] for e in self.ENGINES}
        self.sems = {e: Sem(ctx, nc, f"s_{e}") for e in self.ENGINES}
        self.dma_sp = Sem(ctx, nc, "dma_sp")
        self.dma_gp = Sem(ctx, nc, "dma_gp")
        self.extra_dma = []
        self.bank_readers = {b: {} for b in range(8)}
        self.bank_write = {b: None for b in range(8)}
        self.serial = {"vector": False, "scalar": False, "gpsimd": False,
                       "tensor": False, "sync": False}
        self.pending = {e: [] for e in self.ENGINES}

    def emit(self, eng, fn, waits=(), dma=False, inc=True, dsem=None):
        w = [x for x in waits if x is not None]
        if self.pending[eng]:
            w += self.pending[eng]
            self.pending[eng] = []
        if self.serial[eng] and not dma and self.sems[eng].cnt > 0:
            w.append((self.sems[eng], self.sems[eng].cnt))
        if dma and dsem is not None:
            sem = dsem
        else:
            sem = (self.dma_sp if eng == "sync" else self.dma_gp) if dma else self.sems[eng]
        step = 16 if dma else 1
        mil = None
        if inc:
            sem.cnt += step
            mil = (sem, sem.cnt)
        self.streams[eng].append((fn, w, (sem, step) if inc else None))
        return mil

    def psum_write(self, eng, fn, banks, waits=(), inc=True):
        w = list(waits)
        for b in banks:
            for s, v in self.bank_readers[b].items():
                w.append((s, v))
        mil = self.emit(eng, fn, w, inc=inc)
        for b in banks:
            if mil is not None:
                self.bank_write[b] = mil
            self.bank_readers[b] = {}
        return mil

    def psum_read(self, eng, fn, banks, waits=(), dma=False):
        w = list(waits)
        for b in banks:
            if self.bank_write[b] is not None:
                w.append(self.bank_write[b])
        mil = self.emit(eng, fn, w, dma=dma)
        for b in banks:
            self.bank_readers[b][mil[0]] = mil[1]
        return mil

    def all_milestones(self):
        out = []
        for e in self.ENGINES:
            if self.sems[e].cnt:
                out.append((self.sems[e], self.sems[e].cnt))
        if self.dma_sp.cnt:
            out.append((self.dma_sp, self.dma_sp.cnt))
        if self.dma_gp.cnt:
            out.append((self.dma_gp, self.dma_gp.cnt))
        for sem in self.extra_dma:
            if sem.cnt:
                out.append((sem, sem.cnt))
        return out

    def barrier(self):
        mils = self.all_milestones()
        for e in self.ENGINES:
            self.pending[e] = list(mils)

    def replay(self, eng_name, eng):
        maxseen = {}
        for fn, waits, inc in self.streams[eng_name]:
            seen = {}
            for s, v in waits:
                if maxseen.get(s, -1) >= v:
                    continue
                if s not in seen or seen[s] < v:
                    seen[s] = v
            for s, v in seen.items():
                eng.wait_ge(s.h, v)
                maxseen[s] = v
            inst = fn(eng)
            if inc is not None:
                sem, step = inc
                inst.then_inc(sem.h, step)


def build_nc():
    nc = bass.Bass("TRN2", target_bir_lowering=False)
    x_d = nc.dram_tensor("x", [BPC, S, D], F32, kind="ExternalInput").ap()
    wqt = nc.dram_tensor("wqt", [D, D], F32, kind="ExternalInput").ap()
    wkt = nc.dram_tensor("wkt", [D, 128], F32, kind="ExternalInput").ap()
    wvt = nc.dram_tensor("wvt", [D, 128], F32, kind="ExternalInput").ap()
    wot = nc.dram_tensor("wot", [D, D], F32, kind="ExternalInput").ap()
    wgt = nc.dram_tensor("wgt", [D, HID], I32, kind="ExternalInput").ap()
    wut = nc.dram_tensor("wut", [D, HID], I32, kind="ExternalInput").ap()
    wdt = nc.dram_tensor("wdt", [HID, D], I32, kind="ExternalInput").ap()
    ns_d = nc.dram_tensor("ns", [NC, 128], F32, kind="ExternalInput").ap()
    ss_d = nc.dram_tensor("ss", [NC, 128], F32, kind="ExternalInput").ap()
    sg_d = nc.dram_tensor("sg", [NHC, 128], F32, kind="ExternalInput").ap()
    su_d = nc.dram_tensor("su", [NHC, 128], F32, kind="ExternalInput").ap()
    sdb_d = nc.dram_tensor("sdb", [128, D], F32, kind="ExternalInput").ap()
    y_d = nc.dram_tensor("y", [BPC, S, D], F32, kind="ExternalOutput").ap()
    DBG = os.environ.get("KBG") == "1"
    KSER = int(os.environ.get("KSER", "0"))
    if DBG:
        xn_d = nc.dram_tensor("xn", [BPC, S, D], F32, kind="ExternalOutput").ap()
    else:
        xn_d = nc.dram_tensor("xn", [BPC, S, D], F32).ap()

    ctx = ExitStack()
    with ctx:
        sbt = lambda n, sh, dt: ctx.enter_context(nc.sbuf_tensor(n, sh, dt))
        pst = lambda n, sh, dt: ctx.enter_context(nc.psum_tensor(n, sh, dt))
        wq_b = sbt("wq_b", [128, NC * D], BF16)
        wo_b = sbt("wo_b", [128, NC * D], BF16)
        kv_b = sbt("kv_b", [128, 4096], BF16)
        bbf = sbt("bbf", [128, 14592], BF16)
        bf32 = sbt("bf32", [128, 8192], F32)
        sm = sbt("sm", [128, 160], F32)
        ident = sbt("ident", [128, 128], F32)
        ident_b = sbt("ident_b", [128, 128], BF16)
        ind2 = sbt("ind2", [128, 128], BF16)
        ones_r = sbt("ones_r", [65, 128], BF16)
        mvb = sbt("mvb", [128, 4], BF16)
        lntmp = sbt("lntmp", [128, 512], F32)
        vl1 = lntmp[:, 0:128]
        vl2 = lntmp[:, 128:256]
        A = pst("A", [128, 2048], F32)
        Bp = pst("Bp", [128, 2048], F32)

        wk_v = kv_b[:, 0:2048]
        wv_v = kv_b[:, 2048:4096]
        KT = bbf[:, 0:2048]
        Vv = bbf[:, 2048:4128]
        qT = bbf[:, 4352:6400]
        hbT = bbf[:, 6400:8448]
        attnT = bbf[:, 8448:10496]
        mrow = bbf[:, 8448:10496]
        q_tm = bbf[:, 8448:10496]
        Pb = [bbf[:, 10496:12544], bbf[:, 12544:14592]]
        ygT = bbf[:, 0:8192]
        wu_bufs = [bbf[:, 8192:10240], bbf[:, 10240:12288]]
        gate_sb = bbf[:, 12288:13312]
        up_sb = bbf[:, 13312:14336]
        wu_all = bbf[:, 8192:12288].bitcast(F32)
        gu_sb2 = bbf[:, 12288:14336]
        wg_bufs = [kv_b[:, 0:2048], kv_b[:, 2048:4096]]
        d_acc = wq_b[:, :].bitcast(F32)
        X2T = wo_b[:, 0:16384]
        wd_slot = lambda i: wo_b[:, 16384 + 2048 * i:16384 + 2048 * (i + 1)]

        x_f = bf32[:, 0:2048]
        LNx = bf32[:, 2048:4096]
        hb_bf = bf32[:, 4096:5120].bitcast(BF16)
        attn_bf = bf32[:, 5120:6144].bitcast(BF16)
        resid_f = bf32[:, 6144:8192]
        X2b = bf32[:, 2048:3072].bitcast(BF16)
        sdr_bc = bf32[:, 4096:6144]
        oev = bf32[:, 6144:8192]

        stats = sm[:, 0:24]
        mv = sm[:, 24:26]
        lnv = sm[:, 26:27]
        rstd = sm[:, 27:28]
        t1v = sm[:, 28:29]
        t2v = sm[:, 29:30]
        lnr = sm[:, 30:31]
        rmsv = sm[:, 31:32]
        rmse = sm[:, 32:33]
        rv = sm[:, 33:34]
        rrv = sm[:, 34:35]
        sqcol = sm[:, 36:40]
        kn_t = sm[:, 38:39]
        kmax = sm[:, 39:40]
        mk_vec = sm[:, 40:41]
        recip = sm[:, 41:42]
        eps5 = sm[:, 42:43]
        eps18 = sm[:, 43:44]
        qm_t = sm[:, 44:45]
        qmax2 = sm[:, 45:46]
        qln = sm[:, 46:47]
        bias_g = [sm[:, 152:153], sm[:, 153:154]]
        a_tile = sm[:, 48:64]
        sg_tile = sm[:, 64:108]
        su_tile = sm[:, 108:152]

        prog = Prog(ctx, nc)
        E = prog.emit
        PW = prog.psum_write
        PR = prog.psum_read
        s_watt = Sem(ctx, nc, "d_watt")
        s_misc = Sem(ctx, nc, "d_misc")
        s_wg = [Sem(ctx, nc, "d_wg0"), Sem(ctx, nc, "d_wg1")]
        s_wu = [Sem(ctx, nc, "d_wu0"), Sem(ctx, nc, "d_wu1")]
        s_wd = [Sem(ctx, nc, f"d_wd{i}") for i in range(FG)]
        prog.extra_dma += [s_watt, s_misc] + s_wg + s_wu + s_wd

        TR = [(c % 2) * 512 + (c // 2) * 128 for c in range(8)]
        bTR = lambda i: 4 + ((TR[i]) // 512)
        STM = [0, 512]
        QP = [1024, 1536]
        KP = 1152
        VP = 1664
        AVS = [1280, 1792]

        st = {"x_free": None, "P_free": [None, None], "ST_free": [None, None],
              "y_prev": None, "wg_free": [None, None], "wu_free": [None, None],
              "wd_free": [None] * FG, "oev_free": None, "milo": None,
              "x2b_free": None, "resid_free": None, "lnx_free": None}

        E("gpsimd", lambda g: g.memset(ident[:, :], 0.0))
        milI = E("gpsimd", lambda g: g.affine_select(
            out=ident[:, :], in_=ident[:, :], compare_op=ALU.not_equal,
            fill=1.0, base=0, pattern=[[-1, 128]], channel_multiplier=1))
        E("gpsimd", lambda g: g.memset(ident_b[:, :], 0.0))
        milIb = E("gpsimd", lambda g: g.affine_select(
            out=ident_b[:, :], in_=ident_b[:, :], compare_op=ALU.not_equal,
            fill=1.0, base=0, pattern=[[-1, 128]], channel_multiplier=1))
        m_onesr = E("vector", lambda v: v.memset(ones_r[:, :], 1.0))
        m_qTz = E("vector", lambda v: v.memset(qT[64:128, :], 0.0))
        E("vector", lambda v: v.memset(ind2[:, :], 0.0))
        E("vector", lambda v: v.memset(ind2[0:64, 0:1], 1.0))
        m_ind2 = E("vector", lambda v: v.memset(ind2[64:128, 64:65], 1.0))
        E("vector", lambda v: v.memset(eps5[:, :], 1e-5))
        m_eps = E("vector", lambda v: v.memset(eps18[:, :], 1e-18))

        E("gpsimd", lambda g: g.dma_start(
            out=wq_b[:, :].rearrange("p (c n) -> p c n", c=NC),
            in_=wqt.rearrange("(c p) n -> p c n", p=128)), dma=True, dsem=s_watt)
        E("gpsimd", lambda g: g.dma_start(
            out=wo_b[:, :].rearrange("p (c n) -> p c n", c=NC),
            in_=wot.rearrange("(c p) n -> p c n", p=128)), dma=True, dsem=s_watt)
        E("gpsimd", lambda g: g.dma_start(
            out=wk_v.rearrange("p (c n) -> p c n", c=NC),
            in_=wkt.rearrange("(c p) n -> p c n", p=128)), dma=True, dsem=s_watt)
        E("gpsimd", lambda g: g.dma_start(
            out=wv_v.rearrange("p (c n) -> p c n", c=NC),
            in_=wvt.rearrange("(c p) n -> p c n", p=128)), dma=True, dsem=s_watt)
        mil_wq = mil_wo = mil_wk = mil_wv = (s_watt, 64)
        E("sync", lambda s: s.dma_start(out=vl1[0:NC, :], in_=ns_d), dma=True,
          dsem=s_misc)
        E("sync", lambda s: s.dma_start(out=vl2[0:NC, :], in_=ss_d), dma=True,
          dsem=s_misc)
        mil_v1 = mil_v2 = (s_misc, 32)
        mil_sdr = None

        mil_a16 = E("vector", lambda v: v.tensor_tensor(
            out=vl1[0:NC, :], in0=vl1[0:NC, :], in1=vl2[0:NC, :], op=ALU.mult),
            waits=[mil_v1, mil_v2])
        mil_aT = PW("tensor", lambda t: t.transpose(
            Bp[:, 0:16], vl1[0:NC, :], ident[0:NC, 0:NC]),
            banks=[4], waits=[mil_a16, milI])
        PR("vector", lambda v: v.tensor_copy(a_tile, Bp[:, 0:16]),
           banks=[4], waits=[mil_aT])
        E("sync", lambda s: s.dma_start(out=vl1[0:NHC, :], in_=sg_d),
          waits=[mil_a16], dma=True, dsem=s_misc)
        E("sync", lambda s: s.dma_start(out=vl2[0:NHC, :], in_=su_d),
          waits=[mil_a16], dma=True, dsem=s_misc)
        mil_g1 = mil_g2 = (s_misc, 64)
        mil_gT = PW("tensor", lambda t: t.transpose(
            Bp[:, 16:60], vl1[0:NHC, :], ident[0:NHC, 0:NHC]),
            banks=[4], waits=[mil_g1])
        mil_gev = PR("vector", lambda v: v.tensor_copy(sg_tile, Bp[:, 16:60]),
                     banks=[4], waits=[mil_gT])
        mil_uT = PW("tensor", lambda t: t.transpose(
            Bp[:, 512:556], vl2[0:NHC, :], ident[0:NHC, 0:NHC]),
            banks=[5], waits=[mil_g2])
        mil_uev = PR("vector", lambda v: v.tensor_copy(su_tile, Bp[:, 512:556]),
                     banks=[5], waits=[mil_uT])

        def ln_chain(mil_x, want_lnx, want_hbase, hb_waits=(), src=None,
                     hb_dst=None, aggr_wait=None):
            xs = x_f if src is None else src
            hd = hb_bf if hb_dst is None else hb_dst
            mil_st = None
            for i in range(4):
                mil_st = E("vector", lambda v, i=i, xs=xs: v.bn_stats(
                    stats[:, 6 * i:6 * i + 6], xs[:, 512 * i:512 * i + 512]),
                    waits=[mil_x])
            mil_mv = E("vector", lambda v: v.bn_aggr(mv, stats),
                       waits=[mil_st, aggr_wait])
            mil_lnv = E("scalar", lambda s: s.activation(
                lnv, mv[:, 1:2], AF.Ln, bias=eps5), waits=[mil_mv, m_eps])
            mil_rstd = E("scalar", lambda s: s.activation(
                rstd, lnv, AF.Exp, scale=-0.5), waits=[mil_lnv])
            out = {"rstd": mil_rstd, "lnx": None, "hb": None,
                   "lnv": mil_lnv}
            if want_lnx:
                out["lnx"] = E("vector", lambda v, xs=xs: v.tensor_scalar(
                    out=LNx, in0=xs, scalar1=mv[:, 0:1], scalar2=rstd,
                    op0=ALU.subtract, op1=ALU.mult), waits=[mil_rstd])
            if want_hbase:
                m1 = E("vector", lambda v: v.tensor_tensor(
                    out=t1v, in0=mv[:, 1:2], in1=rstd, op=ALU.mult),
                    waits=[mil_rstd])
                m2 = E("vector", lambda v: v.tensor_tensor(
                    out=t2v, in0=t1v, in1=rstd, op=ALU.mult), waits=[m1])
                m3 = E("scalar", lambda s: s.activation(
                    lnr, t2v, AF.Ln, bias=eps18), waits=[m2])
                m4 = E("scalar", lambda s: s.activation(
                    rmsv, lnr, AF.Exp, scale=0.5), waits=[m3])
                m5 = E("vector", lambda v: v.tensor_scalar(
                    out=rmse, in0=rmsv, scalar1=1e-8, scalar2=None,
                    op0=ALU.add), waits=[m4])
                m6 = E("vector", lambda v: v.reciprocal(rv, rmse), waits=[m5])
                m7 = E("vector", lambda v: v.tensor_tensor(
                    out=rrv, in0=rstd, in1=rv, op=ALU.mult), waits=[m6])
                out["hb"] = E("vector", lambda v, xs=xs, hd=hd: v.tensor_scalar(
                    out=hd, in0=xs, scalar1=mv[:, 0:1], scalar2=rrv,
                    op0=ALU.subtract, op1=ALU.mult),
                    waits=[m7] + list(hb_waits))
            return out

        def transpose_16(src, dst_bf, scale_fn, extra_wait, dst_off_fn=None):
            evacs = []
            for c in range(NC):
                slot = TR[c % 8]
                bank = bTR(c % 8)
                milT = PW("tensor", lambda t, c=c, slot=slot, src=src: t.transpose(
                    Bp[:, slot:slot + 128].bitcast(BF16)[:, 0:128],
                    src[:, 128 * c:128 * c + 128],
                    ident_b[:, :]), banks=[bank], waits=[extra_wait, milIb])
                off = dst_off_fn(c) if dst_off_fn else 128 * c
                sc = scale_fn(c) if scale_fn else None
                if c % 2 == 0:
                    mile = PR("scalar",
                              lambda s, slot=slot, off=off, sc=sc, dst=dst_bf:
                              s.activation(dst[:, off:off + 128],
                                           Bp[:, slot:slot + 128]
                                           .bitcast(BF16)[:, 0:128],
                                           AF.Identity,
                                           scale=(1.0 if sc is None else sc)),
                              banks=[bank], waits=[milT])
                elif sc is None:
                    mile = PR("vector",
                              lambda v, slot=slot, off=off, dst=dst_bf:
                              v.tensor_copy(dst[:, off:off + 128],
                                            Bp[:, slot:slot + 128]
                                            .bitcast(BF16)[:, 0:128]),
                              banks=[bank], waits=[milT])
                else:
                    mile = PR("vector",
                              lambda v, slot=slot, off=off, sc=sc, dst=dst_bf:
                              v.tensor_scalar(out=dst[:, off:off + 128],
                                              in0=Bp[:, slot:slot + 128]
                                              .bitcast(BF16)[:, 0:128],
                                              scalar1=sc, scalar2=None,
                                              op0=ALU.mult),
                              banks=[bank], waits=[milT])
                evacs.append(mile)
            return evacs

        def phase1(s_idx):
            mil_vm = E("vector", lambda v: v.memset(Vv, 1.0))
            E("vector", lambda v: v.memset(kmax, 0.0))
            xstg = [x_f, resid_f]
            hstg = [hb_bf, attn_bf]
            ln_arr = [None] * NT
            xfree = [st["x_free"], st["resid_free"]]
            hfree = [None, None]
            lnv_prev = [None]

            def stage(j):
                p = j % 2
                mx = E("sync", lambda s, j=j, p=p: s.dma_start(
                    out=xstg[p], in_=x_d[s_idx, 128 * j:128 * j + 128, :]),
                    waits=[xfree[p]], dma=True)
                ln = ln_chain(mx, want_lnx=False, want_hbase=True,
                              hb_waits=[hfree[p]], src=xstg[p],
                              hb_dst=hstg[p], aggr_wait=lnv_prev[0])
                lnv_prev[0] = ln["lnv"]
                xfree[p] = ln["hb"]
                ln_arr[j] = ln
            stage(0)
            stage(1)
            for j in range(NT):
                p = j % 2
                ln = ln_arr[j]
                ev = transpose_16(hstg[p], hbT, lambda c: a_tile[:, c:c + 1],
                                  ln["hb"])
                hfree[p] = ev[-1]
                if j + 2 < NT:
                    stage(j + 2)
                milk = None
                for c in range(NC):
                    w = [ev[c]] + ([mil_wk] if c == 0 else [])
                    milk = PW("tensor", lambda t, c=c: t.matmul(
                        Bp[:, KP:KP + 128], wk_v[:, 128 * c:128 * c + 128],
                        hbT[:, 128 * c:128 * c + 128],
                        start=(c == 0), stop=(c == NC - 1)), banks=[6], waits=w,
                        inc=(c == NC - 1))
                milke = PR("scalar", lambda s, j=j: s.activation(
                    KT[:, 128 * j:128 * j + 128], Bp[:, KP:KP + 128],
                    AF.Identity, scale=1.0), banks=[6], waits=[milk])
                milsq = E("vector", lambda v, j=j: v.tensor_tensor(
                    out=lntmp[:, 0:128].bitcast(BF16)[:, 0:128],
                    in0=KT[:, 128 * j:128 * j + 128],
                    in1=KT[:, 128 * j:128 * j + 128], op=ALU.mult),
                    waits=[milke])
                milknm = PW("tensor", lambda t: t.matmul(
                    Bp[:, VP:VP + 128], ind2[:, :],
                    lntmp[:, 0:128].bitcast(BF16)[:, 0:128],
                    start=True, stop=True), banks=[7], waits=[milsq, m_ind2])
                milknr = PR("vector", lambda v: v.reduce_max(
                    kn_t, Bp[:, VP:VP + 128], axis=AX.X),
                    banks=[7], waits=[milknm])
                E("vector", lambda v: v.tensor_tensor(
                    out=kmax, in0=kn_t, in1=kmax, op=ALU.max), waits=[milknr])
                milv = None
                for c in range(NC):
                    w = [ev[c]] + ([mil_wv] if c == 0 else [])
                    milv = PW("tensor", lambda t, c=c: t.matmul(
                        Bp[:, VP:VP + 128], hbT[:, 128 * c:128 * c + 128],
                        wv_v[:, 128 * c:128 * c + 128],
                        start=(c == 0), stop=(c == NC - 1)), banks=[7], waits=w,
                        inc=(c == NC - 1))
                PR("scalar", lambda s, j=j: s.activation(
                    Vv[:, 130 * j:130 * j + 64], Bp[:, VP:VP + 64],
                    AF.Identity, scale=1.0), banks=[7], waits=[milv, mil_vm])
                PR("scalar", lambda s, j=j: s.activation(
                    Vv[:, 130 * j + 65:130 * j + 129], Bp[:, VP + 64:VP + 128],
                    AF.Identity, scale=1.0), banks=[7], waits=[milv, mil_vm])
            E("vector", lambda v: v.tensor_scalar(
                out=mk_vec, in0=kmax, scalar1=1.0 / 64.0,
                scalar2=None, op0=ALU.mult))
            st["x_free"] = ln_arr[NT - 2]["hb"]
            st["resid_free"] = ln_arr[NT - 1]["hb"]

        def emit_av(hs):
            h, m, STb, bank0, mil_exp = hs
            milav = None
            for c in range(NC):
                w = []
                if c == 0:
                    w = [mil_exp[0]]
                if c == 8:
                    w = [mil_exp[1]]
                milav = PW("tensor", lambda t, c=c, h=h, m=m, P=STb:
                           t.matmul(P[:, 0:65],
                                    Pb[m][:, 128 * c:128 * c + 128],
                                    Vv[:, 130 * c + 65 * m:130 * c + 65 * m + 65],
                                    start=(c == 0), stop=(c == NC - 1)),
                           banks=[bank0], waits=w, inc=(c == NC - 1))
            st["P_free"][m] = milav
            milr = PR("vector", lambda v, P=STb: v.reciprocal(
                recip, P[:, 64:65]), banks=[bank0], waits=[milav])
            st["milo"] = PR("vector", lambda v, h=h, P=STb: v.tensor_scalar(
                out=attn_bf[:, 64 * h:64 * h + 64], in0=P[:, 0:64],
                scalar1=recip, scalar2=None, op0=ALU.mult),
                banks=[bank0], waits=[milr])

        def phase3(s_idx):
            mil_x = E("sync", lambda s: s.dma_start(
                out=x_f, in_=x_d[s_idx, 0:128, :]),
                waits=[st["x_free"]], dma=True)
            ln = ln_chain(mil_x, want_lnx=True, want_hbase=True)
            for j in range(NT):
                milt1 = E("vector", lambda v: v.tensor_tensor(
                    out=resid_f, in0=x_f, in1=LNx, op=ALU.add),
                    waits=[ln["lnx"], st["resid_free"]])
                st["x_free"] = milt1
                ev = transpose_16(hb_bf, hbT, lambda c: a_tile[:, c:c + 1], ln["hb"])
                milqq = None
                for c in range(NC):
                    for b4 in range(4):
                        w = [ev[c]] + ([mil_wq] if (c == 0 and b4 == 0) else [])
                        milqq = PW("tensor", lambda t, c=c, b4=b4: t.matmul(
                            A[:, 512 * b4:512 * b4 + 512],
                            hbT[:, 128 * c:128 * c + 128],
                            wq_b[:, 2048 * c + 512 * b4:2048 * c + 512 * b4 + 512],
                            start=(c == 0), stop=(c == NC - 1)),
                            banks=[b4], waits=w,
                            inc=(c == NC - 1 and b4 in (1, 3)))
                        if c == NC - 1 and b4 == 1:
                            milq01 = milqq
                milqe = [None, None]
                milqe[0] = PR("vector", lambda v: v.tensor_copy(
                    q_tm[:, 0:1024], A[:, 0:1024]),
                    banks=[0, 1], waits=[milq01])
                milqe[1] = PR("vector", lambda v: v.tensor_copy(
                    q_tm[:, 1024:2048], A[:, 1024:2048]),
                    banks=[2, 3], waits=[milqq])
                mil_hbz = E("vector", lambda v: v.memset(hbT[0:64, :], 0.0),
                            waits=[milqq])
                mil_qa = [None] * NC
                mil_qb = [None] * NC
                for oc in range(NC):
                    slot = QP[oc % 2]
                    bank = 6 + (oc % 2)
                    milqt = PW("tensor", lambda t, oc=oc, slot=slot: t.transpose(
                        Bp[:, slot:slot + 128].bitcast(BF16)[:, 0:128],
                        q_tm[:, 128 * oc:128 * oc + 128], ident_b[:, :]),
                        banks=[bank], waits=[milqe[oc // 8], milIb])
                    mil_qa[oc] = PR("scalar",
                                    lambda s, oc=oc, slot=slot: s.activation(
                                        qT[0:64, 128 * oc:128 * oc + 128],
                                        Bp[0:64, slot:slot + 128]
                                        .bitcast(BF16)[:, 0:128],
                                        AF.Identity, scale=1.0),
                                    banks=[bank], waits=[milqt])
                    mil_qb[oc] = PR("vector",
                                    lambda v, oc=oc, slot=slot: v.tensor_copy(
                                        hbT[64:128, 128 * oc:128 * oc + 128],
                                        Bp[64:128, slot:slot + 128]
                                        .bitcast(BF16)[:, 0:128]),
                                    banks=[bank], waits=[milqt, mil_hbz])
                mil_bias = [None, None]
                qz_half = [qT, hbT]
                mlnq = None
                for m in range(2):
                    milsq = E("vector", lambda v, m=m: v.tensor_tensor(
                        out=Pb[m], in0=qz_half[m], in1=qz_half[m],
                        op=ALU.mult),
                        waits=[mil_qa[NC - 1], mil_qb[NC - 1],
                               st["P_free"][m]])
                    milqm = None
                    for b4 in range(4):
                        slot = STM[b4 % 2]
                        bank = 4 + (b4 % 2)
                        milm = PW("tensor", lambda t, m=m, b4=b4, slot=slot:
                                  t.matmul(
                                      Bp[:, slot:slot + 512], ind2[:, :],
                                      Pb[m][:, 512 * b4:512 * b4 + 512],
                                      start=True, stop=True), banks=[bank],
                                  waits=[milsq, m_ind2])
                        milrq = PR("vector", lambda v, slot=slot: v.reduce_max(
                            qm_t, Bp[:, slot:slot + 512], axis=AX.X),
                            banks=[bank], waits=[milm])
                        if b4 == 0:
                            milqm = E("vector", lambda v: v.tensor_copy(
                                qmax2, qm_t), waits=[milrq, mlnq])
                        else:
                            milqm = E("vector", lambda v: v.tensor_tensor(
                                out=qmax2, in0=qm_t, in1=qmax2, op=ALU.max),
                                waits=[milrq])
                    mlnq = E("scalar", lambda s: s.activation(
                        qln, qmax2, AF.Ln, bias=eps18, scale=mk_vec),
                        waits=[milqm])
                    mexq = E("scalar", lambda s, m=m: s.activation(
                        mvb[:, m:m + 1], qln, AF.Exp, scale=0.5),
                        waits=[mlnq])
                    slot = STM[m]
                    bank = 4 + m
                    mb = PW("tensor", lambda t, m=m, slot=slot: t.matmul(
                        Bp[:, slot:slot + 1],
                        ones_r[64 * m:64 * m + 1, 0:128],
                        mvb[64 * m:64 * m + 1, m:m + 1],
                        start=True, stop=True), banks=[bank],
                        waits=[mexq, m_onesr])
                    mil_bias[m] = PR("scalar", lambda s, m=m, slot=slot:
                                     s.activation(bias_g[m],
                                                  Bp[:, slot:slot + 1],
                                                  AF.Identity, scale=-0.125),
                                     banks=[bank], waits=[mb])
                if j + 1 < NT:
                    mil_x = E("sync", lambda s, j=j: s.dma_start(
                        out=x_f, in_=x_d[s_idx, 128 * (j + 1):128 * (j + 2), :]),
                        waits=[st["x_free"]], dma=True)
                    ln = ln_chain(mil_x, want_lnx=True, want_hbase=True,
                                  hb_waits=[ev[-1]])
                milo = None
                mil_exp_prev = None
                head_state = []
                for i in range(H):
                    m = i % 2
                    pb = i // 2
                    h = 16 * m + pb
                    qsl = qz_half[m][:, 128 * pb:128 * pb + 128]
                    STb = A if m == 0 else Bp
                    bank0 = 0 if m == 0 else 4
                    mil_half = [None, None]
                    for c in list(range(4, NC)) + [0, 1, 2, 3]:
                        bank = bank0 + c // 4
                        w = [mil_qa[pb] if m == 0 else mil_qb[pb]] \
                            if c % 4 == 0 else []
                        mq = PW("tensor", lambda t, c=c, qsl=qsl, m=m, P=STb:
                                t.matmul(P[:, 128 * c:128 * c + 128],
                                         KT[:, 128 * c:128 * c + 128], qsl,
                                         start=(c % 4 == 0), stop=(c % 4 == 3)),
                                banks=[bank], waits=w, inc=(c in (15, 3)))
                        if c == 15:
                            mil_half[1] = mq
                        elif c == 3:
                            mil_half[0] = mq
                    mil_exp = [None, None]
                    for e in (1, 0):
                        mil_exp[e] = PR("scalar", lambda s, e=e, m=m, P=STb:
                                        s.activation(
                                            Pb[m][:, 1024 * e:1024 * e + 1024],
                                            P[:, 1024 * e:1024 * e + 1024],
                                            AF.Exp, scale=0.125,
                                            bias=bias_g[m]),
                                        banks=[bank0 + 2 * e, bank0 + 2 * e + 1],
                                        waits=[mil_half[e], mil_bias[m],
                                               st["P_free"][m]])
                    head_state.append((h, m, STb, bank0, mil_exp))
                    if i >= 1:
                        emit_av(head_state[i - 1])
                emit_av(head_state[H - 1])
                milo = st["milo"]
                evA = transpose_16(attn_bf, attnT, None, milo)
                milwo = None
                for c in range(NC):
                    for b4 in range(4):
                        w = [evA[c]] + ([mil_wo] if (c == 0 and b4 == 0) else [])
                        milwo = PW("tensor", lambda t, c=c, b4=b4: t.matmul(
                            A[:, 512 * b4:512 * b4 + 512],
                            attnT[:, 128 * c:128 * c + 128],
                            wo_b[:, 2048 * c + 512 * b4:2048 * c + 512 * b4 + 512],
                            start=(c == 0), stop=(c == NC - 1)),
                            banks=[b4], waits=w,
                            inc=(c == NC - 1 and b4 == 3))
                milxn = PR("vector", lambda v: v.tensor_tensor(
                    out=resid_f, in0=A[:, :], in1=resid_f, op=ALU.add),
                    banks=[0, 1, 2, 3], waits=[milwo, milt1])
                milst = E("sync", lambda s, j=j: s.dma_start(
                    out=xn_d[s_idx, 128 * j:128 * j + 128, :], in_=resid_f),
                    waits=[milxn], dma=True)
                st["resid_free"] = milst

        def phaseF(mi):
            nonlocal mil_sdr
            s_idx = mi // 2
            tb = 1024 * (mi % 2)
            if mi == 0:
                E("sync", lambda s: s.dma_start(out=sdr_bc, in_=sdb_d),
                  dma=True, dsem=s_misc)
                mil_sdr = (s_misc, 80)
                for jj in range(NSUB):
                    mil_x = E("sync", lambda s, jj=jj: s.dma_start(
                        out=x_f,
                        in_=xn_d[s_idx, tb + 128 * jj:tb + 128 * jj + 128, :]),
                        waits=[st["x_free"]], dma=True)
                    ln = ln_chain(mil_x, want_lnx=False, want_hbase=False)
                    milx2 = E("vector", lambda v: v.tensor_scalar(
                        out=X2b, in0=x_f, scalar1=mv[:, 0:1], scalar2=rstd,
                        op0=ALU.subtract, op1=ALU.mult), waits=[ln["rstd"]])
                    st["x_free"] = milx2
                    evs = transpose_16(X2b, X2T, None, milx2,
                                       dst_off_fn=lambda c, jj=jj: 1024 * c + 128 * jj)
                    st["x2b_free"] = evs[-1]
            pre = {"lnv": None, "gu_free": None, "x2_read": None}

            for g in range(NGRP):
                g0 = FG * g
                gn = min(FG, NHC - g0)
                last = (g == NGRP - 1)
                ymils = [None] * gn
                wdm = [None] * gn
                pf_mil = [None] * NSUB
                xnb = lambda jj: x_f if jj % 2 == 0 else LNx
                if last:
                    pf_mil[0] = E("sync", lambda s: s.dma_start(
                        out=x_f, in_=xn_d[s_idx, tb:tb + 128, :]),
                        waits=[st["x_free"]], dma=True)
                    pf_mil[1] = E("sync", lambda s: s.dma_start(
                        out=LNx, in_=xn_d[s_idx, tb + 128:tb + 256, :]),
                        waits=[st["x2b_free"], st["lnx_free"]], dma=True)
                for k in range(gn):
                    hc = g0 + k
                    par = hc % 2
                    P1 = A if par == 0 else Bp
                    pb0 = 4 * par
                    mil_wg = E("gpsimd", lambda gg, hc=hc, par=par: gg.dma_start(
                        out=wg_bufs[par][:, :].rearrange("p (c n) -> p c n", c=NC),
                        in_=wgt.rearrange("(c p) n -> p c n", p=128)
                        [:, :, 128 * hc:128 * hc + 128]),
                        waits=[st["wg_free"][par]], dma=True, dsem=s_wg[par])
                    mil_wu = E("gpsimd", lambda gg, hc=hc, par=par: gg.dma_start(
                        out=wu_bufs[par][:, :].rearrange("p (c n) -> p c n", c=NC),
                        in_=wut.rearrange("(c p) n -> p c n", p=128)
                        [:, :, 128 * hc:128 * hc + 128]),
                        waits=[st["wu_free"][par]], dma=True, dsem=s_wu[par])
                    wdm[k] = E("gpsimd", lambda gg, hc=hc, k=k: gg.dma_start(
                        out=wd_slot(k), in_=wdt[128 * hc:128 * hc + 128, :]),
                        waits=[st["wd_free"][k]], dma=True, dsem=s_wd[k])
                    sub_g = E("vector", lambda v, par=par: v.tensor_scalar(
                        out=wg_bufs[par][:, :], in0=wg_bufs[par][:, :],
                        scalar1=-128.0, scalar2=None, op0=ALU.add),
                        waits=[mil_wg])
                    sub_u = E("vector", lambda v, par=par: v.tensor_scalar(
                        out=wu_bufs[par][:, :], in0=wu_bufs[par][:, :],
                        scalar1=-128.0, scalar2=None, op0=ALU.add),
                        waits=[mil_wu])
                    sub_d = E("vector", lambda v, k=k: v.tensor_scalar(
                        out=wd_slot(k), in0=wd_slot(k),
                        scalar1=-128.0, scalar2=None, op0=ALU.add),
                        waits=[wdm[k]])
                    wdm[k] = sub_d
                    mm = None
                    for c in range(NC):
                        for b2 in range(2):
                            w = [sub_g] if (c == 0 and b2 == 0) else []
                            mm = PW("tensor", lambda t, c=c, b2=b2, par=par,
                                    P1=P1: t.matmul(
                                P1[:, 512 * b2:512 * b2 + 512],
                                wg_bufs[par][:, 128 * c:128 * c + 128],
                                X2T[:, 1024 * c + 512 * b2:
                                    1024 * c + 512 * b2 + 512],
                                start=(c == 0), stop=(c == NC - 1)),
                                banks=[pb0 + b2], waits=w,
                                inc=(c == NC - 1 and b2 == 1))
                    st["wg_free"][par] = mm
                    milg = PR("scalar", lambda s, hc=hc, P1=P1: s.activation(
                        gate_sb, P1[:, 0:1024], AF.Silu,
                        scale=sg_tile[:, hc:hc + 1]), banks=[pb0, pb0 + 1],
                        waits=[mm, mil_gev, st["y_prev"]])
                    mm = None
                    for c in range(NC):
                        for b2 in range(2):
                            w = [sub_u] if (c == 0 and b2 == 0) else []
                            mm = PW("tensor", lambda t, c=c, b2=b2, par=par,
                                    P1=P1: t.matmul(
                                P1[:, 1024 + 512 * b2:1024 + 512 * b2 + 512],
                                wu_bufs[par][:, 128 * c:128 * c + 128],
                                X2T[:, 1024 * c + 512 * b2:
                                    1024 * c + 512 * b2 + 512],
                                start=(c == 0), stop=(c == NC - 1)),
                                banks=[pb0 + 2 + b2], waits=w,
                                inc=(c == NC - 1 and b2 == 1))
                    st["wu_free"][par] = mm
                    milu = PR("vector", lambda v, hc=hc, P1=P1: v.tensor_scalar(
                        out=up_sb, in0=P1[:, 1024:2048],
                        scalar1=su_tile[:, hc:hc + 1], scalar2=None,
                        op0=ALU.mult), banks=[pb0 + 2, pb0 + 3],
                        waits=[mm, mil_uev, st["y_prev"]])
                    ymils[k] = E("vector", lambda v, k=k: v.tensor_tensor(
                        out=ygT[:, 1024 * k:1024 * k + 1024], in0=gate_sb,
                        in1=up_sb, op=ALU.mult), waits=[milg, milu])
                    st["y_prev"] = ymils[k]
                    if KSER >= 2:
                        prog.barrier()
                for jj in range(NSUB):
                    Pp = A if jj % 2 == 0 else Bp
                    pbanks = [0, 1, 2, 3] if jj % 2 == 0 else [4, 5, 6, 7]
                    mm = None
                    for k in range(gn):
                        for b4 in range(4):
                            w = [wdm[k], ymils[k]] if b4 == 0 else []
                            mm = PW("tensor", lambda t, k=k, b4=b4, jj=jj, P=Pp:
                                    t.matmul(P[:, 512 * b4:512 * b4 + 512],
                                             ygT[:, 1024 * k + 128 * jj:
                                                 1024 * k + 128 * jj + 128],
                                             wd_slot(k)[:, 512 * b4:512 * b4 + 512],
                                             start=(k == 0),
                                             stop=(k == gn - 1)),
                                    banks=[pbanks[b4]], waits=w,
                                    inc=(k == gn - 1 and b4 == 3))
                    if g == 0 and not last:
                        PR("vector", lambda v, jj=jj, P=Pp: v.tensor_copy(
                            d_acc[:, 2048 * jj:2048 * jj + 2048], P[:, :]),
                            banks=pbanks, waits=[mm])
                    elif not last:
                        PR("vector", lambda v, jj=jj, P=Pp: v.tensor_tensor(
                            out=d_acc[:, 2048 * jj:2048 * jj + 2048], in0=P[:, :],
                            in1=d_acc[:, 2048 * jj:2048 * jj + 2048], op=ALU.add),
                            banks=pbanks, waits=[mm])
                    else:
                        m1 = PR("vector", lambda v, jj=jj, P=Pp: v.tensor_tensor(
                            out=oev, in0=P[:, :],
                            in1=d_acc[:, 2048 * jj:2048 * jj + 2048],
                            op=ALU.add), banks=pbanks,
                            waits=[mm, st["oev_free"]])
                        m2 = E("vector", lambda v, jj=jj: v.tensor_tensor(
                            out=d_acc[:, 2048 * jj:2048 * jj + 2048],
                            in0=oev, in1=sdr_bc, op=ALU.mult),
                            waits=[m1, mil_sdr])
                        m3 = E("vector", lambda v, jj=jj: v.tensor_tensor(
                            out=oev, in0=d_acc[:, 2048 * jj:2048 * jj + 2048],
                            in1=xnb(jj), op=ALU.add),
                            waits=[m2, pf_mil[jj]])
                        if jj % 2 == 0:
                            st["x_free"] = m3
                        else:
                            st["lnx_free"] = m3
                        if jj + 2 < NSUB:
                            pf_mil[jj + 2] = E("sync", lambda s, jj=jj: s.dma_start(
                                out=xnb(jj), in_=xn_d[s_idx, tb + 128 * (jj + 2):
                                                      tb + 128 * (jj + 3), :]),
                                waits=[m3], dma=True)
                        st["oev_free"] = E("sync", lambda s, jj=jj: s.dma_start(
                            out=y_d[s_idx, tb + 128 * jj:tb + 128 * jj + 128, :],
                            in_=oev), waits=[m3], dma=True)
                    if last and mi + 1 < NMACRO:
                        ns2 = (mi + 1) // 2
                        ntb = 1024 * ((mi + 1) % 2)
                        w = ([st["wu_free"][0], st["wu_free"][1]]
                             if jj == 0 else [pre["x2_read"]])
                        mx = E("sync", lambda s, jj=jj, ns2=ns2, ntb=ntb:
                               s.dma_start(
                                   out=wu_all,
                                   in_=xn_d[ns2, ntb + 128 * jj:
                                            ntb + 128 * jj + 128, :]),
                               waits=w, dma=True)
                        lnp = ln_chain(mx, want_lnx=False, want_hbase=False,
                                       src=wu_all, aggr_wait=pre["lnv"])
                        pre["lnv"] = lnp["lnv"]
                        mx2 = E("vector", lambda v: v.tensor_scalar(
                            out=gu_sb2, in0=wu_all, scalar1=mv[:, 0:1],
                            scalar2=rstd, op0=ALU.subtract, op1=ALU.mult),
                            waits=[lnp["rstd"], pre["gu_free"]])
                        pre["x2_read"] = mx2
                        evs2 = transpose_16(
                            gu_sb2, X2T, None, mx2,
                            dst_off_fn=lambda c, jj=jj: 1024 * c + 128 * jj)
                        pre["gu_free"] = evs2[-1]
                for k in range(gn):
                    st["wd_free"][k] = mm
                if KSER >= 1:
                    prog.barrier()
            if pre["x2_read"] is not None:
                st["wu_free"][0] = pre["x2_read"]
                st["wu_free"][1] = pre["x2_read"]

        phase1(0)
        prog.barrier()
        phase3(0)
        prog.barrier()
        phase1(1)
        prog.barrier()
        phase3(1)
        prog.barrier()
        for mi in range(NMACRO):
            phaseF(mi)
        prog.barrier()

        fin = prog.all_milestones()
        E("sync", lambda s: s.nop(), waits=fin, inc=False)

        with nc.Block() as block:
            @block.sync
            def _(e):
                prog.replay("sync", e)

            @block.tensor
            def _(e):
                prog.replay("tensor", e)

            @block.vector
            def _(e):
                prog.replay("vector", e)

            @block.scalar
            def _(e):
                prog.replay("scalar", e)

            @block.gpsimd
            def _(e):
                prog.replay("gpsimd", e)

    return nc


def kernel(**inp):
    x = np.ascontiguousarray(inp["L0_attn_x"], dtype=np.float32)
    perm = []
    for oc in range(16):
        perm += list(range(64 * oc, 64 * oc + 64))
        perm += list(range(64 * (oc + 16), 64 * (oc + 16) + 64))
    wqt = np.ascontiguousarray(np.asarray(inp["L0_attn_Wq"])[perm, :].T)
    wkt = np.ascontiguousarray(np.asarray(inp["L0_attn_Wk"]).T)
    wvt = np.ascontiguousarray(np.asarray(inp["L0_attn_Wv"]).T)
    wot = np.ascontiguousarray(np.asarray(inp["L0_attn_Wo"]).T)
    wgt = np.ascontiguousarray(np.asarray(inp["L0_ffn_wq_gate"]).T)
    wut = np.ascontiguousarray(np.asarray(inp["L0_ffn_wq_up"]).T)
    wdt = np.ascontiguousarray(np.asarray(inp["L0_ffn_wq_down"]).T)
    ns = np.ascontiguousarray(np.asarray(inp["L0_attn_norm_scale"]).reshape(NC, 128))
    ss = np.ascontiguousarray(np.asarray(inp["L0_attn_split_scale"]).reshape(NC, 128))
    sg = np.ascontiguousarray(np.asarray(inp["L0_ffn_s_gate"]).reshape(NHC, 128))
    su = np.ascontiguousarray(np.asarray(inp["L0_ffn_s_up"]).reshape(NHC, 128))
    sdb = np.ascontiguousarray(np.broadcast_to(
        np.asarray(inp["L0_ffn_s_down"]).reshape(1, D), (128, D)))

    nc = build_nc()
    in_maps = []
    for c in range(NCORE):
        in_maps.append({
            "x": np.ascontiguousarray(x[BPC * c:BPC * (c + 1)]),
            "wqt": wqt, "wkt": wkt, "wvt": wvt, "wot": wot,
            "wgt": wgt, "wut": wut, "wdt": wdt,
            "ns": ns, "ss": ss, "sg": sg, "su": su, "sdb": sdb,
        })
    res = run_bass_kernel_spmd(nc, in_maps, core_ids=list(range(NCORE)))
    return np.concatenate([np.asarray(r["y"]) for r in res.results], axis=0)


# version 23
# speedup vs baseline: 1.2968x; 1.0146x over previous
import os
import numpy as np
from contextlib import ExitStack
import concourse.bass as bass
import concourse.mybir as mybir
import concourse.bass_utils as _bu
from concourse.bass_utils import run_bass_kernel_spmd

if os.environ.get("KLDW", "1") == "1" and not getattr(_bu, "_ldw_patched", False):
    _orig_run_command = _bu.run_command

    def _run_command_ldw(argv, **kwargs):
        argv = [a.replace("--enable-ldw-opt=false", "--enable-ldw-opt=true")
                if isinstance(a, str) else a for a in argv]
        return _orig_run_command(argv, **kwargs)

    _bu.run_command = _run_command_ldw
    _bu._ldw_patched = True

F32 = mybir.dt.float32
BF16 = mybir.dt.bfloat16
I32 = mybir.dt.int32
AF = mybir.ActivationFunctionType
AX = mybir.AxisListType
ALU = mybir.AluOpType

B, S, D, HID = 16, 2048, 2048, 5632
H, DH = 32, 64
NCORE = 8
BPC = B // NCORE
NT = S // 128
NC = D // 128
NHC = HID // 128
MACRO = 1024
NMACRO = BPC * S // MACRO
NSUB = MACRO // 128
FG = 8
NGRP = (NHC + FG - 1) // FG


class Sem:
    def __init__(self, ctx, nc, name):
        self.h = ctx.enter_context(nc.semaphore(name))
        self.cnt = 0


class Prog:
    ENGINES = ("sync", "tensor", "vector", "scalar", "gpsimd")

    def __init__(self, ctx, nc):
        self.streams = {e: [] for e in self.ENGINES}
        self.sems = {e: Sem(ctx, nc, f"s_{e}") for e in self.ENGINES}
        self.dma_sp = Sem(ctx, nc, "dma_sp")
        self.dma_gp = Sem(ctx, nc, "dma_gp")
        self.extra_dma = []
        self.bank_readers = {b: {} for b in range(8)}
        self.bank_write = {b: None for b in range(8)}
        self.serial = {"vector": False, "scalar": False, "gpsimd": False,
                       "tensor": False, "sync": False}
        self.pending = {e: [] for e in self.ENGINES}

    def emit(self, eng, fn, waits=(), dma=False, inc=True, dsem=None):
        w = [x for x in waits if x is not None]
        if self.pending[eng]:
            w += self.pending[eng]
            self.pending[eng] = []
        if self.serial[eng] and not dma and self.sems[eng].cnt > 0:
            w.append((self.sems[eng], self.sems[eng].cnt))
        if dma and dsem is not None:
            sem = dsem
        else:
            sem = (self.dma_sp if eng == "sync" else self.dma_gp) if dma else self.sems[eng]
        step = 16 if dma else 1
        mil = None
        if inc:
            sem.cnt += step
            mil = (sem, sem.cnt)
        self.streams[eng].append((fn, w, (sem, step) if inc else None))
        return mil

    def psum_write(self, eng, fn, banks, waits=(), inc=True):
        w = list(waits)
        for b in banks:
            for s, v in self.bank_readers[b].items():
                w.append((s, v))
        mil = self.emit(eng, fn, w, inc=inc)
        for b in banks:
            if mil is not None:
                self.bank_write[b] = mil
            self.bank_readers[b] = {}
        return mil

    def psum_read(self, eng, fn, banks, waits=(), dma=False):
        w = list(waits)
        for b in banks:
            if self.bank_write[b] is not None:
                w.append(self.bank_write[b])
        mil = self.emit(eng, fn, w, dma=dma)
        for b in banks:
            self.bank_readers[b][mil[0]] = mil[1]
        return mil

    def all_milestones(self):
        out = []
        for e in self.ENGINES:
            if self.sems[e].cnt:
                out.append((self.sems[e], self.sems[e].cnt))
        if self.dma_sp.cnt:
            out.append((self.dma_sp, self.dma_sp.cnt))
        if self.dma_gp.cnt:
            out.append((self.dma_gp, self.dma_gp.cnt))
        for sem in self.extra_dma:
            if sem.cnt:
                out.append((sem, sem.cnt))
        return out

    def barrier(self):
        mils = self.all_milestones()
        for e in self.ENGINES:
            self.pending[e] = list(mils)

    def replay(self, eng_name, eng):
        maxseen = {}
        for fn, waits, inc in self.streams[eng_name]:
            seen = {}
            for s, v in waits:
                if maxseen.get(s, -1) >= v:
                    continue
                if s not in seen or seen[s] < v:
                    seen[s] = v
            for s, v in seen.items():
                eng.wait_ge(s.h, v)
                maxseen[s] = v
            inst = fn(eng)
            if inc is not None:
                sem, step = inc
                inst.then_inc(sem.h, step)


def build_nc():
    nc = bass.Bass("TRN2", target_bir_lowering=False)
    x_d = nc.dram_tensor("x", [BPC, S, D], F32, kind="ExternalInput").ap()
    wqt = nc.dram_tensor("wqt", [D, D], F32, kind="ExternalInput").ap()
    wkt = nc.dram_tensor("wkt", [D, 128], F32, kind="ExternalInput").ap()
    wvt = nc.dram_tensor("wvt", [D, 128], F32, kind="ExternalInput").ap()
    wot = nc.dram_tensor("wot", [D, D], F32, kind="ExternalInput").ap()
    wgt = nc.dram_tensor("wgt", [D, HID], I32, kind="ExternalInput").ap()
    wut = nc.dram_tensor("wut", [D, HID], I32, kind="ExternalInput").ap()
    wdt = nc.dram_tensor("wdt", [HID, D], I32, kind="ExternalInput").ap()
    ns_d = nc.dram_tensor("ns", [NC, 128], F32, kind="ExternalInput").ap()
    ss_d = nc.dram_tensor("ss", [NC, 128], F32, kind="ExternalInput").ap()
    sg_d = nc.dram_tensor("sg", [NHC, 128], F32, kind="ExternalInput").ap()
    su_d = nc.dram_tensor("su", [NHC, 128], F32, kind="ExternalInput").ap()
    sdb_d = nc.dram_tensor("sdb", [128, D], F32, kind="ExternalInput").ap()
    y_d = nc.dram_tensor("y", [BPC, S, D], F32, kind="ExternalOutput").ap()
    DBG = os.environ.get("KBG") == "1"
    KSER = int(os.environ.get("KSER", "0"))
    if DBG:
        xn_d = nc.dram_tensor("xn", [BPC, S, D], F32, kind="ExternalOutput").ap()
    else:
        xn_d = nc.dram_tensor("xn", [BPC, S, D], F32).ap()

    ctx = ExitStack()
    with ctx:
        sbt = lambda n, sh, dt: ctx.enter_context(nc.sbuf_tensor(n, sh, dt))
        pst = lambda n, sh, dt: ctx.enter_context(nc.psum_tensor(n, sh, dt))
        wq_b = sbt("wq_b", [128, NC * D], BF16)
        wo_b = sbt("wo_b", [128, NC * D], BF16)
        kv_b = sbt("kv_b", [128, 4096], BF16)
        bbf = sbt("bbf", [128, 14592], BF16)
        bf32 = sbt("bf32", [128, 8192], F32)
        sm = sbt("sm", [128, 160], F32)
        ident = sbt("ident", [128, 128], F32)
        ident_b = sbt("ident_b", [128, 128], BF16)
        ind2 = sbt("ind2", [128, 128], BF16)
        ones_r = sbt("ones_r", [65, 128], BF16)
        mvb = sbt("mvb", [128, 4], BF16)
        lntmp = sbt("lntmp", [128, 512], F32)
        vl1 = lntmp[:, 0:128]
        vl2 = lntmp[:, 128:256]
        A = pst("A", [128, 2048], F32)
        Bp = pst("Bp", [128, 2048], F32)

        wk_v = kv_b[:, 0:2048]
        wv_v = kv_b[:, 2048:4096]
        KT = bbf[:, 0:2048]
        Vv = bbf[:, 2048:4128]
        qT = bbf[:, 4352:6400]
        hbT = bbf[:, 6400:8448]
        attnT = bbf[:, 8448:10496]
        mrow = bbf[:, 8448:10496]
        q_tm = bbf[:, 8448:10496]
        Pb = [bbf[:, 10496:12544], bbf[:, 12544:14592]]
        ygT = bbf[:, 0:8192]
        wu_bufs = [bbf[:, 8192:10240], bbf[:, 10240:12288]]
        gate_sb = bbf[:, 12288:13312]
        up_sb = bbf[:, 13312:14336]
        wg_bufs = [kv_b[:, 0:2048], kv_b[:, 2048:4096]]
        d_acc = wq_b[:, :].bitcast(F32)
        X2T = wo_b[:, 0:16384]
        wd_slot = lambda i: wo_b[:, 16384 + 2048 * i:16384 + 2048 * (i + 1)]

        x_f = bf32[:, 0:2048]
        LNx = bf32[:, 2048:4096]
        hb_bf = bf32[:, 4096:5120].bitcast(BF16)
        attn_bf = bf32[:, 5120:6144].bitcast(BF16)
        resid_f = bf32[:, 6144:8192]
        X2b = bf32[:, 2048:3072].bitcast(BF16)
        sdr_bc = bf32[:, 4096:6144]
        oev = bf32[:, 6144:8192]

        stats = sm[:, 0:24]
        mv = sm[:, 24:26]
        lnv = sm[:, 26:27]
        rstd = sm[:, 27:28]
        t1v = sm[:, 28:29]
        t2v = sm[:, 29:30]
        lnr = sm[:, 30:31]
        rmsv = sm[:, 31:32]
        rmse = sm[:, 32:33]
        rv = sm[:, 33:34]
        rrv = sm[:, 34:35]
        sqcol = sm[:, 36:40]
        kn_t = sm[:, 38:39]
        kmax = sm[:, 39:40]
        mk_vec = sm[:, 40:41]
        recip = sm[:, 41:42]
        eps5 = sm[:, 42:43]
        eps18 = sm[:, 43:44]
        qm_t = sm[:, 44:45]
        qmax2 = sm[:, 45:46]
        qln = sm[:, 46:47]
        bias_g = [sm[:, 152:153], sm[:, 153:154]]
        a_tile = sm[:, 48:64]
        sg_tile = sm[:, 64:108]
        su_tile = sm[:, 108:152]

        prog = Prog(ctx, nc)
        E = prog.emit
        PW = prog.psum_write
        PR = prog.psum_read
        s_watt = Sem(ctx, nc, "d_watt")
        s_misc = Sem(ctx, nc, "d_misc")
        s_wg = [Sem(ctx, nc, "d_wg0"), Sem(ctx, nc, "d_wg1")]
        s_wu = [Sem(ctx, nc, "d_wu0"), Sem(ctx, nc, "d_wu1")]
        s_wd = [Sem(ctx, nc, f"d_wd{i}") for i in range(FG)]
        prog.extra_dma += [s_watt, s_misc] + s_wg + s_wu + s_wd

        TR = [(c % 2) * 512 + (c // 2) * 128 for c in range(8)]
        bTR = lambda i: 4 + ((TR[i]) // 512)
        STM = [0, 512]
        QP = [1024, 1536]
        KP = 1152
        VP = 1664
        AVS = [1280, 1792]

        st = {"x_free": None, "P_free": [None, None], "ST_free": [None, None],
              "y_prev": None, "wg_free": [None, None], "wu_free": [None, None],
              "wd_free": [None] * FG, "oev_free": None, "milo": None,
              "x2b_free": None, "resid_free": None, "lnx_free": None}

        E("gpsimd", lambda g: g.memset(ident[:, :], 0.0))
        milI = E("gpsimd", lambda g: g.affine_select(
            out=ident[:, :], in_=ident[:, :], compare_op=ALU.not_equal,
            fill=1.0, base=0, pattern=[[-1, 128]], channel_multiplier=1))
        E("gpsimd", lambda g: g.memset(ident_b[:, :], 0.0))
        milIb = E("gpsimd", lambda g: g.affine_select(
            out=ident_b[:, :], in_=ident_b[:, :], compare_op=ALU.not_equal,
            fill=1.0, base=0, pattern=[[-1, 128]], channel_multiplier=1))
        m_onesr = E("vector", lambda v: v.memset(ones_r[:, :], 1.0))
        m_qTz = E("vector", lambda v: v.memset(qT[64:128, :], 0.0))
        E("vector", lambda v: v.memset(ind2[:, :], 0.0))
        E("vector", lambda v: v.memset(ind2[0:64, 0:1], 1.0))
        m_ind2 = E("vector", lambda v: v.memset(ind2[64:128, 64:65], 1.0))
        E("vector", lambda v: v.memset(eps5[:, :], 1e-5))
        m_eps = E("vector", lambda v: v.memset(eps18[:, :], 1e-18))

        E("gpsimd", lambda g: g.dma_start(
            out=wq_b[:, :].rearrange("p (c n) -> p c n", c=NC),
            in_=wqt.rearrange("(c p) n -> p c n", p=128)), dma=True, dsem=s_watt)
        E("gpsimd", lambda g: g.dma_start(
            out=wo_b[:, :].rearrange("p (c n) -> p c n", c=NC),
            in_=wot.rearrange("(c p) n -> p c n", p=128)), dma=True, dsem=s_watt)
        E("gpsimd", lambda g: g.dma_start(
            out=wk_v.rearrange("p (c n) -> p c n", c=NC),
            in_=wkt.rearrange("(c p) n -> p c n", p=128)), dma=True, dsem=s_watt)
        E("gpsimd", lambda g: g.dma_start(
            out=wv_v.rearrange("p (c n) -> p c n", c=NC),
            in_=wvt.rearrange("(c p) n -> p c n", p=128)), dma=True, dsem=s_watt)
        mil_wq = mil_wo = mil_wk = mil_wv = (s_watt, 64)
        E("sync", lambda s: s.dma_start(out=vl1[0:NC, :], in_=ns_d), dma=True,
          dsem=s_misc)
        E("sync", lambda s: s.dma_start(out=vl2[0:NC, :], in_=ss_d), dma=True,
          dsem=s_misc)
        mil_v1 = mil_v2 = (s_misc, 32)
        mil_sdr = None

        mil_a16 = E("vector", lambda v: v.tensor_tensor(
            out=vl1[0:NC, :], in0=vl1[0:NC, :], in1=vl2[0:NC, :], op=ALU.mult),
            waits=[mil_v1, mil_v2])
        mil_aT = PW("tensor", lambda t: t.transpose(
            Bp[:, 0:16], vl1[0:NC, :], ident[0:NC, 0:NC]),
            banks=[4], waits=[mil_a16, milI])
        PR("vector", lambda v: v.tensor_copy(a_tile, Bp[:, 0:16]),
           banks=[4], waits=[mil_aT])
        E("sync", lambda s: s.dma_start(out=vl1[0:NHC, :], in_=sg_d),
          waits=[mil_a16], dma=True, dsem=s_misc)
        E("sync", lambda s: s.dma_start(out=vl2[0:NHC, :], in_=su_d),
          waits=[mil_a16], dma=True, dsem=s_misc)
        mil_g1 = mil_g2 = (s_misc, 64)
        mil_gT = PW("tensor", lambda t: t.transpose(
            Bp[:, 16:60], vl1[0:NHC, :], ident[0:NHC, 0:NHC]),
            banks=[4], waits=[mil_g1])
        mil_gev = PR("vector", lambda v: v.tensor_copy(sg_tile, Bp[:, 16:60]),
                     banks=[4], waits=[mil_gT])
        mil_uT = PW("tensor", lambda t: t.transpose(
            Bp[:, 512:556], vl2[0:NHC, :], ident[0:NHC, 0:NHC]),
            banks=[5], waits=[mil_g2])
        mil_uev = PR("vector", lambda v: v.tensor_copy(su_tile, Bp[:, 512:556]),
                     banks=[5], waits=[mil_uT])

        def ln_chain(mil_x, want_lnx, want_hbase, hb_waits=(), src=None,
                     hb_dst=None, aggr_wait=None):
            xs = x_f if src is None else src
            hd = hb_bf if hb_dst is None else hb_dst
            mil_st = None
            for i in range(4):
                mil_st = E("vector", lambda v, i=i, xs=xs: v.bn_stats(
                    stats[:, 6 * i:6 * i + 6], xs[:, 512 * i:512 * i + 512]),
                    waits=[mil_x])
            mil_mv = E("vector", lambda v: v.bn_aggr(mv, stats),
                       waits=[mil_st, aggr_wait])
            mil_lnv = E("scalar", lambda s: s.activation(
                lnv, mv[:, 1:2], AF.Ln, bias=eps5), waits=[mil_mv, m_eps])
            mil_rstd = E("scalar", lambda s: s.activation(
                rstd, lnv, AF.Exp, scale=-0.5), waits=[mil_lnv])
            out = {"rstd": mil_rstd, "lnx": None, "hb": None,
                   "lnv": mil_lnv}
            if want_lnx:
                out["lnx"] = E("vector", lambda v, xs=xs: v.tensor_scalar(
                    out=LNx, in0=xs, scalar1=mv[:, 0:1], scalar2=rstd,
                    op0=ALU.subtract, op1=ALU.mult), waits=[mil_rstd])
            if want_hbase:
                m1 = E("vector", lambda v: v.tensor_tensor(
                    out=t1v, in0=mv[:, 1:2], in1=rstd, op=ALU.mult),
                    waits=[mil_rstd])
                m2 = E("vector", lambda v: v.tensor_tensor(
                    out=t2v, in0=t1v, in1=rstd, op=ALU.mult), waits=[m1])
                m3 = E("scalar", lambda s: s.activation(
                    lnr, t2v, AF.Ln, bias=eps18), waits=[m2])
                m4 = E("scalar", lambda s: s.activation(
                    rmsv, lnr, AF.Exp, scale=0.5), waits=[m3])
                m5 = E("vector", lambda v: v.tensor_scalar(
                    out=rmse, in0=rmsv, scalar1=1e-8, scalar2=None,
                    op0=ALU.add), waits=[m4])
                m6 = E("vector", lambda v: v.reciprocal(rv, rmse), waits=[m5])
                m7 = E("vector", lambda v: v.tensor_tensor(
                    out=rrv, in0=rstd, in1=rv, op=ALU.mult), waits=[m6])
                out["hb"] = E("vector", lambda v, xs=xs, hd=hd: v.tensor_scalar(
                    out=hd, in0=xs, scalar1=mv[:, 0:1], scalar2=rrv,
                    op0=ALU.subtract, op1=ALU.mult),
                    waits=[m7] + list(hb_waits))
            return out

        def transpose_16(src, dst_bf, scale_fn, extra_wait, dst_off_fn=None,
                         veng=False):
            evacs = []
            for c in range(NC):
                slot = TR[c % 8]
                bank = bTR(c % 8)
                milT = PW("tensor", lambda t, c=c, slot=slot, src=src: t.transpose(
                    Bp[:, slot:slot + 128].bitcast(BF16)[:, 0:128],
                    src[:, 128 * c:128 * c + 128],
                    ident_b[:, :]), banks=[bank], waits=[extra_wait, milIb])
                off = dst_off_fn(c) if dst_off_fn else 128 * c
                sc = scale_fn(c) if scale_fn else None
                if c % 2 == 0 and not veng:
                    mile = PR("scalar",
                              lambda s, slot=slot, off=off, sc=sc, dst=dst_bf:
                              s.activation(dst[:, off:off + 128],
                                           Bp[:, slot:slot + 128]
                                           .bitcast(BF16)[:, 0:128],
                                           AF.Identity,
                                           scale=(1.0 if sc is None else sc)),
                              banks=[bank], waits=[milT])
                elif sc is None:
                    mile = PR("vector",
                              lambda v, slot=slot, off=off, dst=dst_bf:
                              v.tensor_copy(dst[:, off:off + 128],
                                            Bp[:, slot:slot + 128]
                                            .bitcast(BF16)[:, 0:128]),
                              banks=[bank], waits=[milT])
                else:
                    mile = PR("vector",
                              lambda v, slot=slot, off=off, sc=sc, dst=dst_bf:
                              v.tensor_scalar(out=dst[:, off:off + 128],
                                              in0=Bp[:, slot:slot + 128]
                                              .bitcast(BF16)[:, 0:128],
                                              scalar1=sc, scalar2=None,
                                              op0=ALU.mult),
                              banks=[bank], waits=[milT])
                evacs.append(mile)
            return evacs

        def phase1(s_idx):
            mil_vm = E("vector", lambda v: v.memset(Vv, 1.0))
            E("vector", lambda v: v.memset(kmax, 0.0))
            xstg = [x_f, resid_f]
            hstg = [hb_bf, attn_bf]
            ln_arr = [None] * NT
            xfree = [st["x_free"], st["resid_free"]]
            hfree = [None, None]
            lnv_prev = [None]

            def stage(j):
                p = j % 2
                mx = E("sync", lambda s, j=j, p=p: s.dma_start(
                    out=xstg[p], in_=x_d[s_idx, 128 * j:128 * j + 128, :]),
                    waits=[xfree[p]], dma=True)
                ln = ln_chain(mx, want_lnx=False, want_hbase=True,
                              hb_waits=[hfree[p]], src=xstg[p],
                              hb_dst=hstg[p], aggr_wait=lnv_prev[0])
                lnv_prev[0] = ln["lnv"]
                xfree[p] = ln["hb"]
                ln_arr[j] = ln
            stage(0)
            stage(1)
            for j in range(NT):
                p = j % 2
                ln = ln_arr[j]
                ev = transpose_16(hstg[p], hbT, lambda c: a_tile[:, c:c + 1],
                                  ln["hb"])
                hfree[p] = ev[-1]
                if j + 2 < NT:
                    stage(j + 2)
                milk = None
                for c in range(NC):
                    w = [ev[c]] + ([mil_wk] if c == 0 else [])
                    milk = PW("tensor", lambda t, c=c: t.matmul(
                        Bp[:, KP:KP + 128], wk_v[:, 128 * c:128 * c + 128],
                        hbT[:, 128 * c:128 * c + 128],
                        start=(c == 0), stop=(c == NC - 1)), banks=[6], waits=w,
                        inc=(c == NC - 1))
                milke = PR("scalar", lambda s, j=j: s.activation(
                    KT[:, 128 * j:128 * j + 128], Bp[:, KP:KP + 128],
                    AF.Identity, scale=1.0), banks=[6], waits=[milk])
                milsq = E("vector", lambda v, j=j: v.tensor_tensor(
                    out=lntmp[:, 0:128].bitcast(BF16)[:, 0:128],
                    in0=KT[:, 128 * j:128 * j + 128],
                    in1=KT[:, 128 * j:128 * j + 128], op=ALU.mult),
                    waits=[milke])
                milknm = PW("tensor", lambda t: t.matmul(
                    Bp[:, VP:VP + 128], ind2[:, :],
                    lntmp[:, 0:128].bitcast(BF16)[:, 0:128],
                    start=True, stop=True), banks=[7], waits=[milsq, m_ind2])
                milknr = PR("vector", lambda v: v.reduce_max(
                    kn_t, Bp[:, VP:VP + 128], axis=AX.X),
                    banks=[7], waits=[milknm])
                E("vector", lambda v: v.tensor_tensor(
                    out=kmax, in0=kn_t, in1=kmax, op=ALU.max), waits=[milknr])
                milv = None
                for c in range(NC):
                    w = [ev[c]] + ([mil_wv] if c == 0 else [])
                    milv = PW("tensor", lambda t, c=c: t.matmul(
                        Bp[:, VP:VP + 128], hbT[:, 128 * c:128 * c + 128],
                        wv_v[:, 128 * c:128 * c + 128],
                        start=(c == 0), stop=(c == NC - 1)), banks=[7], waits=w,
                        inc=(c == NC - 1))
                PR("scalar", lambda s, j=j: s.activation(
                    Vv[:, 130 * j:130 * j + 64], Bp[:, VP:VP + 64],
                    AF.Identity, scale=1.0), banks=[7], waits=[milv, mil_vm])
                PR("scalar", lambda s, j=j: s.activation(
                    Vv[:, 130 * j + 65:130 * j + 129], Bp[:, VP + 64:VP + 128],
                    AF.Identity, scale=1.0), banks=[7], waits=[milv, mil_vm])
            E("vector", lambda v: v.tensor_scalar(
                out=mk_vec, in0=kmax, scalar1=1.0 / 64.0,
                scalar2=None, op0=ALU.mult))
            st["x_free"] = ln_arr[NT - 2]["hb"]
            st["resid_free"] = ln_arr[NT - 1]["hb"]

        def emit_av(hs):
            h, m, STb, bank0, mil_exp = hs
            milav = None
            for c in range(NC):
                w = []
                if c == 0:
                    w = [mil_exp[0]]
                if c == 8:
                    w = [mil_exp[1]]
                milav = PW("tensor", lambda t, c=c, h=h, m=m, P=STb:
                           t.matmul(P[:, 0:65],
                                    Pb[m][:, 128 * c:128 * c + 128],
                                    Vv[:, 130 * c + 65 * m:130 * c + 65 * m + 65],
                                    start=(c == 0), stop=(c == NC - 1)),
                           banks=[bank0], waits=w, inc=(c == NC - 1))
            st["P_free"][m] = milav
            milr = PR("vector", lambda v, P=STb: v.reciprocal(
                recip, P[:, 64:65]), banks=[bank0], waits=[milav])
            st["milo"] = PR("vector", lambda v, h=h, P=STb: v.tensor_scalar(
                out=attn_bf[:, 64 * h:64 * h + 64], in0=P[:, 0:64],
                scalar1=recip, scalar2=None, op0=ALU.mult),
                banks=[bank0], waits=[milr])

        def phase3(s_idx):
            mil_x = E("sync", lambda s: s.dma_start(
                out=x_f, in_=x_d[s_idx, 0:128, :]),
                waits=[st["x_free"]], dma=True)
            ln = ln_chain(mil_x, want_lnx=True, want_hbase=True)
            for j in range(NT):
                milt1 = E("vector", lambda v: v.tensor_tensor(
                    out=resid_f, in0=x_f, in1=LNx, op=ALU.add),
                    waits=[ln["lnx"], st["resid_free"]])
                st["x_free"] = milt1
                ev = transpose_16(hb_bf, hbT, lambda c: a_tile[:, c:c + 1],
                                  ln["hb"], veng=True)
                milqq = None
                for c in range(NC):
                    for b4 in range(4):
                        w = [ev[c]] + ([mil_wq] if (c == 0 and b4 == 0) else [])
                        milqq = PW("tensor", lambda t, c=c, b4=b4: t.matmul(
                            A[:, 512 * b4:512 * b4 + 512],
                            hbT[:, 128 * c:128 * c + 128],
                            wq_b[:, 2048 * c + 512 * b4:2048 * c + 512 * b4 + 512],
                            start=(c == 0), stop=(c == NC - 1)),
                            banks=[b4], waits=w,
                            inc=(c == NC - 1 and b4 in (1, 3)))
                        if c == NC - 1 and b4 == 1:
                            milq01 = milqq
                milqe = [None, None]
                milqe[0] = PR("vector", lambda v: v.tensor_copy(
                    q_tm[:, 0:1024], A[:, 0:1024]),
                    banks=[0, 1], waits=[milq01])
                milqe[1] = PR("vector", lambda v: v.tensor_copy(
                    q_tm[:, 1024:2048], A[:, 1024:2048]),
                    banks=[2, 3], waits=[milqq])
                mil_hbz = E("vector", lambda v: v.memset(hbT[0:64, :], 0.0),
                            waits=[milqq])
                mil_qa = [None] * NC
                mil_qb = [None] * NC
                for oc in range(NC):
                    slot = QP[oc % 2]
                    bank = 6 + (oc % 2)
                    milqt = PW("tensor", lambda t, oc=oc, slot=slot: t.transpose(
                        Bp[:, slot:slot + 128].bitcast(BF16)[:, 0:128],
                        q_tm[:, 128 * oc:128 * oc + 128], ident_b[:, :]),
                        banks=[bank], waits=[milqe[oc // 8], milIb])
                    mil_qa[oc] = PR("vector",
                                    lambda v, oc=oc, slot=slot: v.tensor_copy(
                                        qT[0:64, 128 * oc:128 * oc + 128],
                                        Bp[0:64, slot:slot + 128]
                                        .bitcast(BF16)[:, 0:128]),
                                    banks=[bank], waits=[milqt])
                    mil_qb[oc] = PR("vector",
                                    lambda v, oc=oc, slot=slot: v.tensor_copy(
                                        hbT[64:128, 128 * oc:128 * oc + 128],
                                        Bp[64:128, slot:slot + 128]
                                        .bitcast(BF16)[:, 0:128]),
                                    banks=[bank], waits=[milqt, mil_hbz])
                mil_bias = [None, None]
                qz_half = [qT, hbT]
                mlnq = None
                for m in range(2):
                    milsq = E("vector", lambda v, m=m: v.tensor_tensor(
                        out=Pb[m], in0=qz_half[m], in1=qz_half[m],
                        op=ALU.mult),
                        waits=[mil_qa[NC - 1], mil_qb[NC - 1],
                               st["P_free"][m]])
                    milqm = None
                    for b4 in range(4):
                        slot = STM[b4 % 2]
                        bank = 4 + (b4 % 2)
                        milm = PW("tensor", lambda t, m=m, b4=b4, slot=slot:
                                  t.matmul(
                                      Bp[:, slot:slot + 512], ind2[:, :],
                                      Pb[m][:, 512 * b4:512 * b4 + 512],
                                      start=True, stop=True), banks=[bank],
                                  waits=[milsq, m_ind2])
                        milrq = PR("vector", lambda v, slot=slot: v.reduce_max(
                            qm_t, Bp[:, slot:slot + 512], axis=AX.X),
                            banks=[bank], waits=[milm])
                        if b4 == 0:
                            milqm = E("vector", lambda v: v.tensor_copy(
                                qmax2, qm_t), waits=[milrq, mlnq])
                        else:
                            milqm = E("vector", lambda v: v.tensor_tensor(
                                out=qmax2, in0=qm_t, in1=qmax2, op=ALU.max),
                                waits=[milrq])
                    mlnq = E("scalar", lambda s: s.activation(
                        qln, qmax2, AF.Ln, bias=eps18, scale=mk_vec),
                        waits=[milqm])
                    mexq = E("scalar", lambda s, m=m: s.activation(
                        mvb[:, m:m + 1], qln, AF.Exp, scale=0.5),
                        waits=[mlnq])
                    slot = STM[m]
                    bank = 4 + m
                    mb = PW("tensor", lambda t, m=m, slot=slot: t.matmul(
                        Bp[:, slot:slot + 1],
                        ones_r[64 * m:64 * m + 1, 0:128],
                        mvb[64 * m:64 * m + 1, m:m + 1],
                        start=True, stop=True), banks=[bank],
                        waits=[mexq, m_onesr])
                    mil_bias[m] = PR("scalar", lambda s, m=m, slot=slot:
                                     s.activation(bias_g[m],
                                                  Bp[:, slot:slot + 1],
                                                  AF.Identity, scale=-0.125),
                                     banks=[bank], waits=[mb])
                if j + 1 < NT:
                    mil_x = E("sync", lambda s, j=j: s.dma_start(
                        out=x_f, in_=x_d[s_idx, 128 * (j + 1):128 * (j + 2), :]),
                        waits=[st["x_free"]], dma=True)
                    ln = ln_chain(mil_x, want_lnx=True, want_hbase=True,
                                  hb_waits=[ev[-1]])
                milo = None
                mil_exp_prev = None
                head_state = []
                for i in range(H):
                    m = i % 2
                    pb = i // 2
                    h = 16 * m + pb
                    qsl = qz_half[m][:, 128 * pb:128 * pb + 128]
                    STb = A if m == 0 else Bp
                    bank0 = 0 if m == 0 else 4
                    mil_half = [None, None]
                    for c in list(range(4, NC)) + [0, 1, 2, 3]:
                        bank = bank0 + c // 4
                        w = [mil_qa[pb] if m == 0 else mil_qb[pb]] \
                            if c % 4 == 0 else []
                        mq = PW("tensor", lambda t, c=c, qsl=qsl, m=m, P=STb:
                                t.matmul(P[:, 128 * c:128 * c + 128],
                                         KT[:, 128 * c:128 * c + 128], qsl,
                                         start=(c % 4 == 0), stop=(c % 4 == 3)),
                                banks=[bank], waits=w, inc=(c in (15, 3)))
                        if c == 15:
                            mil_half[1] = mq
                        elif c == 3:
                            mil_half[0] = mq
                    mil_exp = [None, None]
                    for e in (1, 0):
                        mil_exp[e] = PR("scalar", lambda s, e=e, m=m, P=STb:
                                        s.activation(
                                            Pb[m][:, 1024 * e:1024 * e + 1024],
                                            P[:, 1024 * e:1024 * e + 1024],
                                            AF.Exp, scale=0.125,
                                            bias=bias_g[m]),
                                        banks=[bank0 + 2 * e, bank0 + 2 * e + 1],
                                        waits=[mil_half[e], mil_bias[m],
                                               st["P_free"][m]])
                    head_state.append((h, m, STb, bank0, mil_exp))
                    if i >= 1:
                        emit_av(head_state[i - 1])
                emit_av(head_state[H - 1])
                milo = st["milo"]
                evA = transpose_16(attn_bf, attnT, None, milo, veng=True)
                milwo = None
                for c in range(NC):
                    for b4 in range(4):
                        w = [evA[c]] + ([mil_wo] if (c == 0 and b4 == 0) else [])
                        milwo = PW("tensor", lambda t, c=c, b4=b4: t.matmul(
                            A[:, 512 * b4:512 * b4 + 512],
                            attnT[:, 128 * c:128 * c + 128],
                            wo_b[:, 2048 * c + 512 * b4:2048 * c + 512 * b4 + 512],
                            start=(c == 0), stop=(c == NC - 1)),
                            banks=[b4], waits=w,
                            inc=(c == NC - 1 and b4 == 3))
                milxn = PR("vector", lambda v: v.tensor_tensor(
                    out=resid_f, in0=A[:, :], in1=resid_f, op=ALU.add),
                    banks=[0, 1, 2, 3], waits=[milwo, milt1])
                milst = E("sync", lambda s, j=j: s.dma_start(
                    out=xn_d[s_idx, 128 * j:128 * j + 128, :], in_=resid_f),
                    waits=[milxn], dma=True)
                st["resid_free"] = milst

        def phaseF(mi):
            nonlocal mil_sdr
            s_idx = mi // 2
            tb = 1024 * (mi % 2)
            if mi == 0:
                E("sync", lambda s: s.dma_start(out=sdr_bc, in_=sdb_d),
                  dma=True, dsem=s_misc)
                mil_sdr = (s_misc, 80)
            for jj in range(NSUB):
                mil_x = E("sync", lambda s, jj=jj: s.dma_start(
                    out=x_f,
                    in_=xn_d[s_idx, tb + 128 * jj:tb + 128 * jj + 128, :]),
                    waits=[st["x_free"]], dma=True)
                ln = ln_chain(mil_x, want_lnx=False, want_hbase=False)
                milx2 = E("vector", lambda v: v.tensor_scalar(
                    out=X2b, in0=x_f, scalar1=mv[:, 0:1], scalar2=rstd,
                    op0=ALU.subtract, op1=ALU.mult), waits=[ln["rstd"]])
                st["x_free"] = milx2
                evs = transpose_16(X2b, X2T, None, milx2,
                                   dst_off_fn=lambda c, jj=jj: 1024 * c + 128 * jj)
                st["x2b_free"] = evs[-1]

            for g in range(NGRP):
                g0 = FG * g
                gn = min(FG, NHC - g0)
                last = (g == NGRP - 1)
                ymils = [None] * gn
                wdm = [None] * gn
                pf_mil = [None] * NSUB
                xnb = lambda jj: x_f if jj % 2 == 0 else LNx
                if last:
                    pf_mil[0] = E("sync", lambda s: s.dma_start(
                        out=x_f, in_=xn_d[s_idx, tb:tb + 128, :]),
                        waits=[st["x_free"]], dma=True)
                    pf_mil[1] = E("sync", lambda s: s.dma_start(
                        out=LNx, in_=xn_d[s_idx, tb + 128:tb + 256, :]),
                        waits=[st["x2b_free"], st["lnx_free"]], dma=True)
                for k in range(gn):
                    hc = g0 + k
                    par = hc % 2
                    P1 = A if par == 0 else Bp
                    pb0 = 4 * par
                    mil_wg = E("gpsimd", lambda gg, hc=hc, par=par: gg.dma_start(
                        out=wg_bufs[par][:, :].rearrange("p (c n) -> p c n", c=NC),
                        in_=wgt.rearrange("(c p) n -> p c n", p=128)
                        [:, :, 128 * hc:128 * hc + 128]),
                        waits=[st["wg_free"][par]], dma=True, dsem=s_wg[par])
                    mil_wu = E("gpsimd", lambda gg, hc=hc, par=par: gg.dma_start(
                        out=wu_bufs[par][:, :].rearrange("p (c n) -> p c n", c=NC),
                        in_=wut.rearrange("(c p) n -> p c n", p=128)
                        [:, :, 128 * hc:128 * hc + 128]),
                        waits=[st["wu_free"][par]], dma=True, dsem=s_wu[par])
                    wdm[k] = E("gpsimd", lambda gg, hc=hc, k=k: gg.dma_start(
                        out=wd_slot(k), in_=wdt[128 * hc:128 * hc + 128, :]),
                        waits=[st["wd_free"][k]], dma=True, dsem=s_wd[k])
                    sub_g = E("vector", lambda v, par=par: v.tensor_scalar(
                        out=wg_bufs[par][:, :], in0=wg_bufs[par][:, :],
                        scalar1=-128.0, scalar2=None, op0=ALU.add),
                        waits=[mil_wg])
                    sub_u = E("vector", lambda v, par=par: v.tensor_scalar(
                        out=wu_bufs[par][:, :], in0=wu_bufs[par][:, :],
                        scalar1=-128.0, scalar2=None, op0=ALU.add),
                        waits=[mil_wu])
                    sub_d = E("vector", lambda v, k=k: v.tensor_scalar(
                        out=wd_slot(k), in0=wd_slot(k),
                        scalar1=-128.0, scalar2=None, op0=ALU.add),
                        waits=[wdm[k]])
                    wdm[k] = sub_d
                    mm = None
                    for c in range(NC):
                        for b2 in range(2):
                            w = [sub_g] if (c == 0 and b2 == 0) else []
                            mm = PW("tensor", lambda t, c=c, b2=b2, par=par,
                                    P1=P1: t.matmul(
                                P1[:, 512 * b2:512 * b2 + 512],
                                wg_bufs[par][:, 128 * c:128 * c + 128],
                                X2T[:, 1024 * c + 512 * b2:
                                    1024 * c + 512 * b2 + 512],
                                start=(c == 0), stop=(c == NC - 1)),
                                banks=[pb0 + b2], waits=w,
                                inc=(c == NC - 1 and b2 == 1))
                    st["wg_free"][par] = mm
                    milg = PR("scalar", lambda s, hc=hc, P1=P1: s.activation(
                        gate_sb, P1[:, 0:1024], AF.Silu,
                        scale=sg_tile[:, hc:hc + 1]), banks=[pb0, pb0 + 1],
                        waits=[mm, mil_gev, st["y_prev"]])
                    mm = None
                    for c in range(NC):
                        for b2 in range(2):
                            w = [sub_u] if (c == 0 and b2 == 0) else []
                            mm = PW("tensor", lambda t, c=c, b2=b2, par=par,
                                    P1=P1: t.matmul(
                                P1[:, 1024 + 512 * b2:1024 + 512 * b2 + 512],
                                wu_bufs[par][:, 128 * c:128 * c + 128],
                                X2T[:, 1024 * c + 512 * b2:
                                    1024 * c + 512 * b2 + 512],
                                start=(c == 0), stop=(c == NC - 1)),
                                banks=[pb0 + 2 + b2], waits=w,
                                inc=(c == NC - 1 and b2 == 1))
                    st["wu_free"][par] = mm
                    milu = PR("vector", lambda v, hc=hc, P1=P1: v.tensor_scalar(
                        out=up_sb, in0=P1[:, 1024:2048],
                        scalar1=su_tile[:, hc:hc + 1], scalar2=None,
                        op0=ALU.mult), banks=[pb0 + 2, pb0 + 3],
                        waits=[mm, mil_uev, st["y_prev"]])
                    ymils[k] = E("vector", lambda v, k=k: v.tensor_tensor(
                        out=ygT[:, 1024 * k:1024 * k + 1024], in0=gate_sb,
                        in1=up_sb, op=ALU.mult), waits=[milg, milu])
                    st["y_prev"] = ymils[k]
                    if KSER >= 2:
                        prog.barrier()
                for jj in range(NSUB):
                    Pp = A if jj % 2 == 0 else Bp
                    pbanks = [0, 1, 2, 3] if jj % 2 == 0 else [4, 5, 6, 7]
                    mm = None
                    for k in range(gn):
                        for b4 in range(4):
                            w = [wdm[k], ymils[k]] if b4 == 0 else []
                            mm = PW("tensor", lambda t, k=k, b4=b4, jj=jj, P=Pp:
                                    t.matmul(P[:, 512 * b4:512 * b4 + 512],
                                             ygT[:, 1024 * k + 128 * jj:
                                                 1024 * k + 128 * jj + 128],
                                             wd_slot(k)[:, 512 * b4:512 * b4 + 512],
                                             start=(k == 0),
                                             stop=(k == gn - 1)),
                                    banks=[pbanks[b4]], waits=w,
                                    inc=(k == gn - 1 and b4 == 3))
                    if g == 0 and not last:
                        PR("vector", lambda v, jj=jj, P=Pp: v.tensor_copy(
                            d_acc[:, 2048 * jj:2048 * jj + 2048], P[:, :]),
                            banks=pbanks, waits=[mm])
                    elif not last:
                        PR("vector", lambda v, jj=jj, P=Pp: v.tensor_tensor(
                            out=d_acc[:, 2048 * jj:2048 * jj + 2048], in0=P[:, :],
                            in1=d_acc[:, 2048 * jj:2048 * jj + 2048], op=ALU.add),
                            banks=pbanks, waits=[mm])
                    else:
                        m1 = PR("vector", lambda v, jj=jj, P=Pp: v.tensor_tensor(
                            out=oev, in0=P[:, :],
                            in1=d_acc[:, 2048 * jj:2048 * jj + 2048],
                            op=ALU.add), banks=pbanks,
                            waits=[mm, st["oev_free"]])
                        m2 = E("vector", lambda v, jj=jj: v.tensor_tensor(
                            out=d_acc[:, 2048 * jj:2048 * jj + 2048],
                            in0=oev, in1=sdr_bc, op=ALU.mult),
                            waits=[m1, mil_sdr])
                        m3 = E("vector", lambda v, jj=jj: v.tensor_tensor(
                            out=oev, in0=d_acc[:, 2048 * jj:2048 * jj + 2048],
                            in1=xnb(jj), op=ALU.add),
                            waits=[m2, pf_mil[jj]])
                        if jj % 2 == 0:
                            st["x_free"] = m3
                        else:
                            st["lnx_free"] = m3
                        if jj + 2 < NSUB:
                            pf_mil[jj + 2] = E("sync", lambda s, jj=jj: s.dma_start(
                                out=xnb(jj), in_=xn_d[s_idx, tb + 128 * (jj + 2):
                                                      tb + 128 * (jj + 3), :]),
                                waits=[m3], dma=True)
                        st["oev_free"] = E("sync", lambda s, jj=jj: s.dma_start(
                            out=y_d[s_idx, tb + 128 * jj:tb + 128 * jj + 128, :],
                            in_=oev), waits=[m3], dma=True)
                for k in range(gn):
                    st["wd_free"][k] = mm
                if KSER >= 1:
                    prog.barrier()

        phase1(0)
        prog.barrier()
        phase3(0)
        prog.barrier()
        phase1(1)
        prog.barrier()
        phase3(1)
        prog.barrier()
        for mi in range(NMACRO):
            phaseF(mi)
        prog.barrier()

        fin = prog.all_milestones()
        E("sync", lambda s: s.nop(), waits=fin, inc=False)

        with nc.Block() as block:
            @block.sync
            def _(e):
                prog.replay("sync", e)

            @block.tensor
            def _(e):
                prog.replay("tensor", e)

            @block.vector
            def _(e):
                prog.replay("vector", e)

            @block.scalar
            def _(e):
                prog.replay("scalar", e)

            @block.gpsimd
            def _(e):
                prog.replay("gpsimd", e)

    return nc


def kernel(**inp):
    x = np.ascontiguousarray(inp["L0_attn_x"], dtype=np.float32)
    perm = []
    for oc in range(16):
        perm += list(range(64 * oc, 64 * oc + 64))
        perm += list(range(64 * (oc + 16), 64 * (oc + 16) + 64))
    wqt = np.ascontiguousarray(np.asarray(inp["L0_attn_Wq"])[perm, :].T)
    wkt = np.ascontiguousarray(np.asarray(inp["L0_attn_Wk"]).T)
    wvt = np.ascontiguousarray(np.asarray(inp["L0_attn_Wv"]).T)
    wot = np.ascontiguousarray(np.asarray(inp["L0_attn_Wo"]).T)
    wgt = np.ascontiguousarray(np.asarray(inp["L0_ffn_wq_gate"]).T)
    wut = np.ascontiguousarray(np.asarray(inp["L0_ffn_wq_up"]).T)
    wdt = np.ascontiguousarray(np.asarray(inp["L0_ffn_wq_down"]).T)
    ns = np.ascontiguousarray(np.asarray(inp["L0_attn_norm_scale"]).reshape(NC, 128))
    ss = np.ascontiguousarray(np.asarray(inp["L0_attn_split_scale"]).reshape(NC, 128))
    sg = np.ascontiguousarray(np.asarray(inp["L0_ffn_s_gate"]).reshape(NHC, 128))
    su = np.ascontiguousarray(np.asarray(inp["L0_ffn_s_up"]).reshape(NHC, 128))
    sdb = np.ascontiguousarray(np.broadcast_to(
        np.asarray(inp["L0_ffn_s_down"]).reshape(1, D), (128, D)))

    nc = build_nc()
    in_maps = []
    for c in range(NCORE):
        in_maps.append({
            "x": np.ascontiguousarray(x[BPC * c:BPC * (c + 1)]),
            "wqt": wqt, "wkt": wkt, "wvt": wvt, "wot": wot,
            "wgt": wgt, "wut": wut, "wdt": wdt,
            "ns": ns, "ss": ss, "sg": sg, "su": su, "sdb": sdb,
        })
    res = run_bass_kernel_spmd(nc, in_maps, core_ids=list(range(NCORE)))
    return np.concatenate([np.asarray(r["y"]) for r in res.results], axis=0)
